# version 05
# baseline (speedup 1.0000x reference)
import numpy as np
import concourse.bass as bass
import concourse.mybir as mybir
from concourse.ap import AP
from concourse.bass_utils import run_bass_kernel_spmd
from concourse.replica_groups import maybe_share_collective_output_space

F32 = mybir.dt.float32
BF16 = mybir.dt.bfloat16
AF = mybir.ActivationFunctionType
ALU = mybir.AluOpType
AXL = mybir.AxisListType

B, S, D = 2, 4096, 2048
NCORES = 8
TOK = 1024
HALO = 512
KC = D // 128
RG = [[0, 1, 2, 3], [4, 5, 6, 7]]
LN_EPS = 1e-5

N_SUB = 8
DEBUG = False



def wprep(W, obs=128):
    Din, M = W.shape
    kcw = Din // 128
    nob = M // obs
    return np.ascontiguousarray(
        W.reshape(kcw, 128, nob, obs).transpose(1, 2, 0, 3).reshape(128, nob * kcw * obs))


def fmprep(Xtok):
    T = Xtok.shape[0]
    return np.ascontiguousarray(
        Xtok.T.reshape(KC, 128, T).transpose(1, 0, 2).reshape(128, KC * T))


def colprep(v):
    return np.ascontiguousarray(v.reshape(-1, 128).T)



class Sem:
    __slots__ = ("h", "n", "off", "g")

    def __init__(self, h, off, g):
        self.h = h
        self.n = off
        self.off = off
        self.g = g


class G:
    NPOOL = 40

    def __init__(self, nc):
        self.nc = nc
        self._uid = 0
        self._pool = []
        self._counts = {}
        self._rr = 0

    def sem(self, prefix="s"):
        if len(self._pool) < self.NPOOL:
            h = self.nc.alloc_semaphore(name=f"pool{len(self._pool)}")
            self._pool.append(h)
            self._counts[id(h)] = 0
            hh = h
        else:
            hh = self._pool[self._rr % self.NPOOL]
            self._rr += 1
        return Sem(hh, self._counts[id(hh)], self)

    def uid(self):
        self._uid += 1
        return self._uid

    def inc(self, ins, sem, k=1):
        ins.then_inc(sem.h, k)
        sem.n += k
        self._counts[id(sem.h)] = sem.n

    def bar(self):
        self.nc.all_engine_barrier()


def build_program(n_sub):
    nc = bass.Bass("TRN2", target_bir_lowering=False)
    g = G(nc)

    def din(name, shape):
        return nc.dram_tensor(name, shape, F32, kind="ExternalInput").ap()

    def dint(name, shape):
        kind = "ExternalOutput" if DEBUG else "Internal"
        return nc.dram_tensor(name, shape, F32, kind=kind).ap()

    xin = din("xin", [128, KC * TOK])
    xhalo = din("xhalo", [128, KC * HALO])
    hk = din("hk", [1, 1])
    i = {}
    i["l0a_wq"] = din("l0a_wq", [128, 16 * KC * 128])
    i["l0a_wk"] = din("l0a_wk", [128, 16 * KC * 128])
    i["l0a_wv"] = din("l0a_wv", [128, 16 * KC * 128])
    i["l0a_wo"] = din("l0a_wo", [128, 16 * KC * 128])
    i["l0a_dog"] = din("l0a_dog", [32, 8])
    i["l0a_sinks"] = din("l0a_sinks", [1, 32])
    if n_sub >= 2:
        i["l0f_wT"] = din("l0f_wT", [128, 16 * KC * 128])
        i["l0f_cols"] = din("l0f_cols", [128, 4 * KC])
    if n_sub >= 3:
        i["l0b_wqd"] = din("l0b_wqd", [128, 2 * KC * 128])
        i["l0b_wkvd"] = din("l0b_wkvd", [128, 1 * KC * 256])
        i["l0b_wqu"] = din("l0b_wqu", [128, 8 * 2 * 128])
        i["l0b_wku"] = din("l0b_wku", [128, 8 * 2 * 128])
        i["l0b_wvu"] = din("l0b_wvu", [128, 16 * 2 * 128])
        i["l0b_wo"] = din("l0b_wo", [128, 16 * KC * 128])
        i["tpos"] = din("tpos", [1, 1])
    if n_sub >= 5:
        i["a1_wq"] = din("a1_wq", [128, 4 * KC * 128])
        i["a1_wk"] = din("a1_wk", [128, 4 * KC * 128])
        i["a1_wv"] = din("a1_wv", [128, 4 * KC * 128])
        i["a1_wilr"] = din("a1_wilr", [128, KC * 8])
        i["a1_bilr8"] = din("a1_bilr8", [8, 1])
        i["a1_lnw"] = din("a1_lnw", [8, 64])
        i["a1_lnb"] = din("a1_lnb", [8, 64])
        i["a1_w1init"] = din("a1_w1init", [128, 4 * 64])
        i["a1_b1init"] = din("a1_b1init", [8, 64])
        i["a1_wo"] = din("a1_wo", [128, 16 * KC * 128])
        i["idxcol"] = din("idxcol", [128, 1])
        i["pcol"] = din("pcol", [128, 1])
        i["phcol"] = din("phcol", [128, 1])
        i["pm16col"] = din("pm16col", [128, 1])
        i["qselcol"] = din("qselcol", [128, 4])
    if n_sub >= 6:
        i["f1_w1t"] = din("f1_w1t", [128, 64 * KC * 128])
        i["f1_w2t"] = din("f1_w2t", [128, 16 * 64 * 128])
        i["f1_b1c"] = din("f1_b1c", [128, 64])
        i["f1_b2c"] = din("f1_b2c", [128, 16])
    if n_sub >= 7:
        i["b1_wq"] = din("b1_wq", [128, 4 * KC * 128])
        i["b1_wk"] = din("b1_wk", [128, 4 * KC * 128])
        i["b1_wv"] = din("b1_wv", [128, 4 * KC * 128])
        i["b1_wo"] = din("b1_wo", [128, 16 * KC * 128])

    y = nc.dram_tensor("y", [128, KC * TOK], F32, kind="ExternalOutput").ap()

    qstash = dint("qstash", [16, 128, TOK])
    kstash = dint("kstash", [16, 128, HALO + TOK])
    vstash = dint("vstash", [16, 128, HALO + TOK])
    etab = dint("etab", [32, 2048])
    etab3 = dint("etab3", [32, 128, 1408])
    outdbg = dint("outdbg", [128, KC * (HALO + TOK)])
    scdbg = dint("scdbg", [1, 64])
    dbg2 = dint("dbg2", [128, 8])
    dbg3 = dint("dbg3", [1, 512])
    sml = dint("sml", [1, 128])
    cc_lat_in = nc.dram_tensor("cc_lat_in", [TOK, 256], F32, kind="Internal").ap()
    _space = maybe_share_collective_output_space("AllGather", RG)
    cc_lat_out = nc.dram_tensor("cc_lat_out", [S, 256], F32, kind="Internal",
                                addr_space=_space).ap()
    bstash = dint("bstash", [16, 128, TOK])
    xbfm = dint("xbfm", [4, 128, S])
    xcfm = dint("xcfm", [4, 128, S])
    xafm = dint("xafm", [4, 128, S])
    ilrst = dint("ilrst", [8, S])
    cc_xn_in = nc.dram_tensor("cc_xn_in", [D, TOK], F32, kind="Internal").ap()
    cc_xn_out = nc.dram_tensor("cc_xn_out", [4 * D, TOK], F32, kind="Internal",
                               addr_space=_space).ap()
    cc_s_in = nc.dram_tensor("cc_s_in", [512, S], F32, kind="Internal").ap()
    scandump = dint("scandump", [512, S])
    sbdump = dint("sbdump", [128, 16 * 64 + 24])
    cndump = dint("cndump", [128, 512])
    cc_s_out = nc.dram_tensor("cc_s_out", [D, S], F32, kind="Internal",
                              addr_space=_space).ap()
    w1q = nc.dram_tensor("w1q", [128, 64 * KC * 128], BF16, kind="Internal").ap()
    w2q = nc.dram_tensor("w2q", [128, 16 * 64 * 128], BF16, kind="Internal").ap()

    X = nc.alloc_sbuf_tensor("X", [128, KC * TOK], F32)
    CONST = nc.alloc_sbuf_tensor("CONST", [128, 4], F32)
    IDENT = nc.alloc_sbuf_tensor("IDENT", [128, 128], F32)
    SC = nc.alloc_sbuf_tensor("SC", [1, 64], F32)
    CREP = nc.alloc_sbuf_tensor("CREP", [128, 2], F32)
    SQ = nc.alloc_sbuf_tensor("SQ", [128, 2 * 512], F32)
    EV = nc.alloc_sbuf_tensor("EV", [128, 2 * 512], F32)
    RECIP = nc.alloc_sbuf_tensor("RECIP", [1, 512], F32)

    pb = [nc.alloc_psum_tensor(f"pb{j}", [128, 512], F32) for j in range(6)]
    pst = [nc.alloc_psum_tensor(f"pst{j}", [1, 512], F32) for j in range(2)]

    ONESR = nc.alloc_sbuf_tensor("ONESR", [1, 128], F32)
    ones = CONST[:, 0:1]
    eps1 = CONST[0:1, 1:2]

    nc.vector.memset(CONST[:, 0:1], 1.0)
    nc.vector.memset(CONST[:, 1:2], LN_EPS)
    nc.vector.memset(ONESR[:, :], 1.0)
    nc.vector.drain()
    with nc.sbuf_tensor("i_rid", [128, 128], F32) as rid, \
         nc.sbuf_tensor("i_cid", [128, 128], F32) as cid:
        nc.gpsimd.iota(rid[:], pattern=[[0, 128]], channel_multiplier=1,
                       allow_small_or_imprecise_dtypes=True)
        nc.gpsimd.iota(cid[:], pattern=[[1, 128]], channel_multiplier=0,
                       allow_small_or_imprecise_dtypes=True)
        gp0 = g.sem("gp0")
        g.inc(nc.gpsimd.drain(), gp0)
        nc.vector.wait_ge(gp0.h, gp0.n)
        nc.vector.tensor_tensor(IDENT[:], rid[:], cid[:], op=ALU.is_equal)
        nc.vector.drain()
    g.bar()


    def ln_stats(src_fn, tcols, stat):
        ntb = tcols // 512
        mm = g.sem("lnmm")
        sq = g.sem("lnsq")
        rd = g.sem("lnrd")
        for tb in range(ntb):
            if tb >= 1:
                nc.tensor.wait_ge(rd.h, rd.n)
            for c in range(KC):
                g.inc(nc.tensor.matmul(pst[0][0:1, :], ones, src_fn(c, tb),
                                       start=(c == 0), stop=(c == KC - 1)), mm)
            for c in range(KC):
                buf = SQ[:, (c % 2) * 512:(c % 2) * 512 + 512]
                if c >= 2:
                    nc.scalar.wait_ge(mm.h, mm.n - 1)
                g.inc(nc.scalar.activation(buf, src_fn(c, tb), AF.Square), sq)
                nc.tensor.wait_ge(sq.h, sq.n)
                g.inc(nc.tensor.matmul(pst[1][0:1, :], ones, buf,
                                       start=(c == 0), stop=(c == KC - 1)), mm)
            nc.vector.wait_ge(mm.h, mm.n)
            mcol = stat[0:1, tb * 512:tb * 512 + 512]
            rcol = stat[0:1, tcols + tb * 512:tcols + tb * 512 + 512]
            nc.vector.tensor_scalar_mul(mcol, pst[0][0:1, :], 1.0 / D)
            nc.vector.drain()
            nc.vector.tensor_scalar_mul(rcol, pst[1][0:1, :], 1.0 / D)
            g.inc(nc.vector.drain(), rd)
            nc.vector.tensor_tensor(SQ[0:1, 0:512], mcol, mcol, op=ALU.mult)
            nc.vector.drain()
            nc.vector.tensor_tensor(rcol, rcol, SQ[0:1, 0:512], op=ALU.subtract)
            g.inc(nc.vector.drain(), sq)
            nc.scalar.wait_ge(sq.h, sq.n)
            g.inc(nc.scalar.activation(rcol, rcol, AF.Sqrt, bias=eps1), sq)
            nc.vector.wait_ge(sq.h, sq.n)
            nc.vector.reciprocal(rcol, rcol)
            nc.vector.drain()
        g.bar()

    def ln_apply(src_fn, dst_fn, tcols, stat, aff=None):
        ntb = tcols // 512
        mm = g.sem("lamm")
        for tb in range(ntb):
            mcol = stat[0:1, tb * 512:tb * 512 + 512]
            rcol = stat[0:1, tcols + tb * 512:tcols + tb * 512 + 512]
            nc.tensor.matmul(pb[4][:, :], ONESR[0:1, :], mcol,
                             start=True, stop=True)
            g.inc(nc.tensor.matmul(pb[5][:, :], ONESR[0:1, :], rcol,
                                   start=True, stop=True), mm)
            nc.vector.wait_ge(mm.h, mm.n)
            for c in range(KC):
                s_ = src_fn(c, tb)
                d_ = dst_fn(c, tb)
                nc.vector.tensor_tensor(d_, s_, pb[4][:, :], op=ALU.subtract)
                nc.vector.drain()
                nc.vector.tensor_tensor(d_, d_, pb[5][:, :], op=ALU.mult)
                nc.vector.drain()
                if aff is not None:
                    lnw_col, lnb_col = aff
                    nc.vector.tensor_scalar(d_, d_, lnw_col[:, c:c + 1], lnb_col[:, c:c + 1],
                                            op0=ALU.mult, op1=ALU.add)
                    nc.vector.drain()
            g.inc(nc.vector.drain(), mm)
            nc.tensor.wait_ge(mm.h, mm.n)
        g.bar()

    def proj(w_hbm, nob, src_fn, ntb, evac, kc=KC, obs=128):
        mm = g.sem("pjmm")
        ev = g.sem("pjev")
        wl = g.sem("pjwl")
        with nc.sbuf_tensor(f"WB{g.uid()}", [128, 2 * kc * obs], F32) as WB:
            j = 0
            for ob in range(nob):
                buf = WB[:, (ob % 2) * kc * obs:(ob % 2) * kc * obs + kc * obs]
                if ob >= 2:
                    nc.sync.wait_ge(mm.h, mm.off + (ob - 1) * ntb)
                g.inc(nc.sync.dma_start(
                    out=buf, in_=w_hbm[:, ob * kc * obs:(ob + 1) * kc * obs]), wl, 16)
                nc.tensor.wait_ge(wl.h, wl.off + 16 * (ob + 1))
                for tb in range(ntb):
                    psb = pb[j % 2]
                    if j >= 2:
                        nc.tensor.wait_ge(ev.h, ev.off + j - 1)
                    out_ap = psb[:, :] if obs == 128 else psb[0:obs, :]
                    for c in range(kc):
                        nc.tensor.matmul(out_ap,
                                         buf[:, c * obs:(c + 1) * obs],
                                         src_fn(c, tb),
                                         start=(c == 0), stop=(c == kc - 1))
                    g.inc(nc.tensor.drain(), mm)
                    evac(ob, tb, out_ap, mm, ev, j)
                    j += 1
            g.bar()

    def scalar_bcast(src_11, dst_col, nrows=128):
        mm = g.sem("bc")
        g.inc(nc.vector.drain(), mm)
        g.inc(nc.scalar.drain(), mm)
        nc.tensor.wait_ge(mm.h, mm.n)
        g.inc(nc.tensor.matmul(pb[4][0:nrows, 0:1], ONESR[0:1, 0:nrows],
                               src_11, start=True, stop=True), mm)
        nc.vector.wait_ge(mm.h, mm.n)
        nc.vector.tensor_copy(dst_col, pb[4][0:nrows, 0:1])
        g.inc(nc.vector.drain(), mm)
        nc.tensor.wait_ge(mm.h, mm.n)

    def cross_maxP(src_col, dst_11, nrows):
        st = g.sem("cmp")
        g.inc(nc.sync.dma_start(out=AP(sml.tensor, 0, [[1, nrows], [1, 1]]), in_=src_col), st, 16)
        nc.sync.wait_ge(st.h, st.n)
        with nc.sbuf_tensor(f"cmp{g.uid()}", [1, nrows], F32) as t:
            g.inc(nc.sync.dma_start(out=t[:], in_=sml[0:1, 0:nrows]), st, 16)
            nc.vector.wait_ge(st.h, st.n)
            nc.vector.tensor_reduce(dst_11, t[:], axis=AXL.X, op=ALU.max)
            nc.vector.drain()
        g.bar()

    def cross_max32(src_col32, dst_11):
        st = g.sem("cm")
        g.inc(nc.sync.dma_start(out=AP(sml.tensor, 0, [[1, 32], [1, 1]]), in_=src_col32), st, 16)
        nc.sync.wait_ge(st.h, st.n)
        with nc.sbuf_tensor(f"cm{g.uid()}", [1, 32], F32) as t:
            g.inc(nc.sync.dma_start(out=t[:], in_=sml[0:1, 0:32]), st, 16)
            nc.vector.wait_ge(st.h, st.n)
            nc.vector.tensor_reduce(dst_11, t[:], axis=AXL.X, op=ALU.max)
            nc.vector.drain()
        g.bar()

    def s1_l0_attn_a():
        TW = HALO + TOK
        ld = g.sem("ld1")
        with nc.sbuf_tensor("XN_a", [128, KC * TW], F32) as XN:
            with nc.sbuf_tensor("LST_a", [1, 2 * TOK], F32) as LST:
                g.inc(nc.sync.dma_start(out=X[:], in_=xin[:, :]), ld, 16)
                g.inc(nc.sync.dma_start(
                    out=AP(XN.ap().tensor, 0, [list(XN.ap().ap[0]), [TW, KC], [1, HALO]]),
                    in_=xhalo[:, :].rearrange("p (c t) -> p c t", c=KC)), ld, 16)
                nc.sync.wait_ge(ld.h, ld.n)
                g.bar()
                ln_stats(lambda c, tb: XN[:, c * TW + tb * 512: c * TW + tb * 512 + 512],
                         HALO, LST[0:1, 0:1024])
                ln_apply(lambda c, tb: XN[:, c * TW + tb * 512: c * TW + tb * 512 + 512],
                         lambda c, tb: XN[:, c * TW + tb * 512: c * TW + tb * 512 + 512],
                         HALO, LST[0:1, 0:1024])
                ln_stats(lambda c, tb: X[:, c * TOK + tb * 512: c * TOK + tb * 512 + 512],
                         TOK, LST)
                ln_apply(lambda c, tb: X[:, c * TOK + tb * 512: c * TOK + tb * 512 + 512],
                         lambda c, tb: XN[:, c * TW + HALO + tb * 512: c * TW + HALO + tb * 512 + 512],
                         TOK, LST)

            with nc.sbuf_tensor("DOG", [32, 8], F32) as DOG, \
                 nc.sbuf_tensor("PP", [32, 8], F32) as PP, \
                 nc.sbuf_tensor("ET", [32, 1536], F32) as ET, \
                 nc.sbuf_tensor("SCR_e", [32, 2 * 1536], F32) as SCR:
                sq = g.sem("esq")
                ev = g.sem("eev")
                st = g.sem("est")
                g.inc(nc.sync.dma_start(out=DOG[:, 0:7], in_=i["l0a_dog"][:, 0:7]), ld, 16)
                g.inc(nc.sync.dma_start(out=SC[0:1, 32:64], in_=i["l0a_sinks"][:, :]), ld, 16)
                g.inc(nc.sync.dma_start(out=SC[0:1, 30:31], in_=hk[:, :]), ld, 16)
                nc.gpsimd.iota(SCR[:, 0:1536], pattern=[[1, 1536]], base=-512,
                               channel_multiplier=0, allow_small_or_imprecise_dtypes=True)
                g.inc(nc.gpsimd.drain(), ev)
                nc.scalar.wait_ge(ld.h, ld.n)
                nc.vector.wait_ge(ld.h, ld.n)
                nc.vector.wait_ge(ev.h, ev.n)
                nc.scalar.wait_ge(ev.h, ev.n)
                for j, scol in ((0, 1), (1, 4)):
                    nc.scalar.activation(PP[:, j:j + 1], DOG[:, scol:scol + 1], AF.Exp)
                    nc.scalar.drain()
                    nc.scalar.activation(PP[:, j:j + 1], PP[:, j:j + 1], AF.Ln,
                                         bias=CONST[0:32, 0:1])
                    nc.scalar.drain()
                    nc.scalar.activation(PP[:, j:j + 1], PP[:, j:j + 1], AF.Square)
                    nc.scalar.drain()
                    g.inc(nc.scalar.mul(PP[:, j:j + 1], PP[:, j:j + 1], 2.0), sq)
                nc.vector.wait_ge(sq.h, sq.n)
                for j in (0, 1):
                    nc.vector.reciprocal(PP[:, j:j + 1], PP[:, j:j + 1])
                    nc.vector.drain()
                    nc.vector.tensor_scalar_mul(PP[:, j:j + 1], PP[:, j:j + 1], -1.0)
                    nc.vector.drain()
                for j, acol, mcol in ((0, 0, 2), (1, 3, 5)):
                    wk = SCR[:, 1536:3072]
                    nc.vector.tensor_scalar(wk, SCR[:, 0:1536], DOG[:, mcol:mcol + 1], None,
                                            op0=ALU.subtract)
                    nc.vector.drain()
                    nc.vector.tensor_tensor(wk, wk, wk, op=ALU.mult)
                    nc.vector.drain()
                    nc.vector.tensor_scalar(wk, wk, PP[:, j:j + 1], None, op0=ALU.mult)
                    g.inc(nc.vector.drain(), ev)
                    nc.scalar.wait_ge(ev.h, ev.n)
                    g.inc(nc.scalar.activation(wk, wk, AF.Exp), sq)
                    nc.vector.wait_ge(sq.h, sq.n)
                    if j == 0:
                        nc.vector.tensor_scalar(ET[:, :], wk, DOG[:, acol:acol + 1], None,
                                                op0=ALU.mult)
                    else:
                        nc.vector.tensor_scalar(wk, wk, DOG[:, acol:acol + 1], None,
                                                op0=ALU.mult)
                        nc.vector.drain()
                        nc.vector.tensor_tensor(ET[:, :], ET[:, :], wk, op=ALU.subtract)
                    nc.vector.drain()
                nc.vector.tensor_scalar(ET[:, :], ET[:, :], DOG[:, 6:7], None, op0=ALU.add)
                nc.vector.drain()
                band = SCR[:, 1536:3072]
                nc.vector.tensor_scalar(band, SCR[:, 0:1536], -0.5, None, op0=ALU.is_ge)
                nc.vector.drain()
                nc.vector.tensor_scalar(SCR[:, 0:1536], SCR[:, 0:1536], 511.5, None,
                                        op0=ALU.is_le)
                nc.vector.drain()
                nc.vector.tensor_tensor(band, band, SCR[:, 0:1536], op=ALU.mult)
                nc.vector.drain()
                fm_ = SCR[:, 0:1536]
                nc.vector.scalar_tensor_tensor(fm_, ET[:, :], 1e30, band,
                                               op0=ALU.add, op1=ALU.mult)
                nc.vector.drain()
                nc.vector.tensor_scalar(fm_, fm_, -1e30, None, op0=ALU.add)
                nc.vector.drain()
                nc.vector.tensor_reduce(PP[:, 4:5], fm_, axis=AXL.X, op=ALU.max)
                g.inc(nc.vector.drain(), ev)
                nc.scalar.wait_ge(ev.h, ev.n)
                g.inc(nc.scalar.activation(ET[:, :], ET[:, :], AF.Exp), sq)
                nc.vector.wait_ge(sq.h, sq.n)
                nc.vector.tensor_tensor(ET[:, :], ET[:, :], band, op=ALU.mult)
                g.inc(nc.vector.drain(), ev)
                nc.sync.wait_ge(ev.h, ev.n)
                g.inc(nc.sync.dma_start(out=etab[:, 0:1536], in_=ET[:, :]), st, 16)
                nc.sync.wait_ge(st.h, st.n)
                g.inc(nc.sync.dma_start(
                    out=AP(etab3.tensor, 0, [[128 * 1408, 32], [1408, 128], [1, 1408]]),
                    in_=AP(etab.tensor, 128, [[2048, 32], [-1, 128], [1, 1408]])), st, 16)
                nc.sync.wait_ge(st.h, st.n)
                cross_max32(PP[:, 4:5], SC[0:1, 0:1])

            with nc.sbuf_tensor("QKST", [1, 2560], F32) as QKST:
                QN2 = QKST[0:1, 0:1024]
                KN2 = QKST[0:1, 1024:2560]
                nc.vector.memset(QN2, 0.0)
                nc.vector.memset(KN2, 0.0)
                nc.vector.drain()
                g.bar()

                def mk_evac(stash, statrow, scale):
                    def evac(ob, tb, ps, mm, ev, j):
                        ebuf = EV[:, (j % 2) * 512:(j % 2) * 512 + 512]
                        nc.vector.wait_ge(mm.h, mm.n)
                        if scale != 1.0:
                            nc.vector.tensor_scalar_mul(ebuf, ps, scale)
                        else:
                            nc.vector.tensor_copy(ebuf, ps)
                        g.inc(nc.vector.drain(), ev)
                        nc.sync.wait_ge(ev.h, ev.n)
                        st = evac.st
                        g.inc(nc.sync.dma_start(
                            out=stash[ob, :, tb * 512:(tb + 1) * 512], in_=ebuf), st, 16)
                        sqb = SQ[:, (j % 2) * 512:(j % 2) * 512 + 512]
                        nc.scalar.wait_ge(ev.h, ev.n)
                        if j >= 2:
                            nc.scalar.wait_ge(evac.mm2.h, evac.mm2.n - 1)
                        g.inc(nc.scalar.activation(sqb, ebuf, AF.Square), evac.sq)
                        nc.tensor.wait_ge(evac.sq.h, evac.sq.n)
                        nc.tensor.wait_ge(evac.s2.h, evac.s2.n)
                        g.inc(nc.tensor.matmul(pst[0][0:1, :], ones, sqb,
                                               start=True, stop=True), evac.mm2)
                        nc.vector.wait_ge(evac.mm2.h, evac.mm2.n)
                        col = statrow[0:1, tb * 512:tb * 512 + 512]
                        nc.vector.tensor_tensor(col, col, pst[0][0:1, :], op=ALU.max)
                        g.inc(nc.vector.drain(), evac.s2)
                    evac.st = g.sem("qkst")
                    evac.sq = g.sem("qksq")
                    evac.mm2 = g.sem("qkmm2")
                    evac.s2 = g.sem("qks2")
                    return evac

                proj(i["l0a_wq"], 16,
                     lambda c, tb: XN[:, c * TW + HALO + tb * 512: c * TW + HALO + tb * 512 + 512],
                     2, mk_evac(qstash, QN2, 0.125))
                proj(i["l0a_wk"], 16,
                     lambda c, tb: XN[:, c * TW + tb * 512: c * TW + tb * 512 + 512],
                     3, mk_evac(kstash, KN2, 1.0))
                proj(i["l0a_wv"], 16,
                     lambda c, tb: XN[:, c * TW + tb * 512: c * TW + tb * 512 + 512],
                     3, mk_evac(vstash, KN2, 1.0))

                sq = g.sem("csq")
                ev = g.sem("cev")
                nc.vector.tensor_reduce(SC[0:1, 1:2], QN2, axis=AXL.X, op=ALU.max)
                nc.vector.drain()
                nc.vector.tensor_reduce(SC[0:1, 2:3], KN2, axis=AXL.X, op=ALU.max)
                g.inc(nc.vector.drain(), ev)
                nc.scalar.wait_ge(ev.h, ev.n)
                nc.scalar.activation(SC[0:1, 1:2], SC[0:1, 1:2], AF.Sqrt)
                nc.scalar.drain()
                g.inc(nc.scalar.activation(SC[0:1, 2:3], SC[0:1, 2:3], AF.Sqrt), sq)
                nc.vector.wait_ge(sq.h, sq.n)
                nc.vector.tensor_tensor(SC[0:1, 3:4], SC[0:1, 1:2], SC[0:1, 2:3], op=ALU.mult)
                nc.vector.drain()
                nc.vector.tensor_tensor(SC[0:1, 3:4], SC[0:1, 3:4], SC[0:1, 0:1], op=ALU.add)
                nc.vector.drain()
                nc.vector.tensor_scalar_mul(SC[0:1, 4:5], SC[0:1, 3:4], -1.0)
                nc.vector.drain()
                nc.vector.tensor_scalar(SC[0:1, 32:64], SC[0:1, 32:64], SC[0:1, 3:4], None,
                                        op0=ALU.subtract)
                g.inc(nc.vector.drain(), ev)
                nc.scalar.wait_ge(ev.h, ev.n)
                g.inc(nc.scalar.activation(SC[0:1, 32:64], SC[0:1, 32:64], AF.Exp), sq)
                nc.vector.wait_ge(sq.h, sq.n)
                scalar_bcast(SC[0:1, 4:5], CREP[:, 0:1], 128)
                scalar_bcast(SC[0:1, 30:31], CREP[:, 1:2], 128)
                g.bar()

            OUT = XN
            with nc.sbuf_tensor("QH_a", [64, TOK], F32) as QH, \
                 nc.sbuf_tensor("KH_a", [64, TW], F32) as KH, \
                 nc.sbuf_tensor("VH_a", [64, TW], F32) as VH, \
                 nc.sbuf_tensor("VTM_a", [128, 12 * 72], F32) as VTM, \
                 nc.sbuf_tensor("ETW_a", [128, 1408], F32) as ETW, \
                 nc.sbuf_tensor("E64_a", [1, 72], F32) as E64:
                PT = EV
                for kt in range(12):
                    nc.vector.memset(VTM[:, kt * 72 + 64:kt * 72 + 65], 1.0)
                nc.vector.drain()
                g.bar()

                hl = g.sem("hl")
                gl = g.sem("gl")
                mm = g.sem("amm")
                mo = g.sem("amo")
                sq = g.sem("asq")
                ev = g.sem("aev")
                for h in range(32):
                    cch = h // 2
                    rbase = 64 * (h % 2)
                    g.inc(nc.sync.dma_start(out=QH[:, :], in_=qstash[cch, rbase:rbase + 64, :]), hl, 16)
                    g.inc(nc.sync.dma_start(out=KH[:, :], in_=kstash[cch, rbase:rbase + 64, :]), hl, 16)
                    g.inc(nc.sync.dma_start(out=ETW[:, :], in_=etab3[h, :, :]), hl, 16)
                    g.inc(nc.gpsimd.dma_start(out=VH[:, :], in_=vstash[cch, rbase:rbase + 64, :]), gl, 16)
                    nc.tensor.wait_ge(gl.h, gl.n)
                    for kt in range(12):
                        nc.tensor.transpose(pb[2][:, 0:64], VH[:, kt * 128:(kt + 1) * 128],
                                            IDENT[0:64, 0:64])
                        g.inc(nc.tensor.drain(), mm)
                        nc.vector.wait_ge(mm.h, mm.n)
                        nc.vector.tensor_copy(VTM[:, kt * 72:kt * 72 + 64], pb[2][:, 0:64])
                        g.inc(nc.vector.drain(), ev)
                        nc.tensor.wait_ge(ev.h, ev.n)
                    nc.sync.wait_ge(hl.h, hl.n)
                    g.bar()

                    for qb in range(2):
                        for kt in range(8):
                            wkt = qb * 4 + kt
                            psb = pb[kt % 2]
                            if kt >= 2:
                                nc.tensor.wait_ge(sq.h, sq.n - 1)
                            nc.tensor.matmul(psb[:, :], KH[:, wkt * 128:(wkt + 1) * 128],
                                             QH[:, qb * 512:(qb + 1) * 512],
                                             start=True, stop=True)
                            g.inc(nc.tensor.drain(), mm)
                            ptile = PT[:, (kt % 2) * 512:(kt % 2) * 512 + 512]
                            nc.scalar.wait_ge(mm.h, mm.n)
                            if kt >= 2:
                                nc.scalar.wait_ge(mo.h, mo.n - 1)
                            g.inc(nc.scalar.activation(ptile, psb[:, :], AF.Exp,
                                                       bias=CREP[:, 0:1]), sq)
                            nc.vector.wait_ge(sq.h, sq.n)
                            nc.vector.tensor_tensor(ptile, ptile,
                                                    ETW[:, 896 - 128 * kt: 1408 - 128 * kt],
                                                    op=ALU.mult)
                            if qb == 0 and kt < 4:
                                nc.vector.drain()
                                nc.vector.tensor_scalar(ptile, ptile, CREP[:, 1:2], None,
                                                        op0=ALU.mult)
                            g.inc(nc.vector.drain(), ev)
                            nc.tensor.wait_ge(ev.h, ev.n)
                            g.inc(nc.tensor.matmul(pb[3][0:65, :],
                                                   VTM[:, wkt * 72:wkt * 72 + 65], ptile,
                                                   start=(kt == 0), stop=(kt == 7),
                                                   skip_group_check=True), mo)
                        nc.vector.wait_ge(mo.h, mo.n)
                        nc.vector.tensor_scalar(RECIP[0:1, :], pb[3][64:65, :],
                                                SC[0:1, 32 + h:33 + h], None, op0=ALU.add)
                        nc.vector.drain()
                        nc.vector.reciprocal(RECIP[0:1, :], RECIP[0:1, :])
                        g.inc(nc.vector.drain(), ev)
                        nc.tensor.wait_ge(ev.h, ev.n)
                        g.inc(nc.tensor.matmul(pb[4][0:64, :], ONESR[0:1, 0:64],
                                               RECIP[0:1, :], start=True, stop=True), mm)
                        nc.vector.wait_ge(mm.h, mm.n)
                        nc.vector.tensor_copy(SQ[64:128, 0:512], pb[4][0:64, :])
                        nc.vector.drain()
                        dsl = slice(cch * TW + qb * 512, cch * TW + qb * 512 + 512)
                        nc.vector.tensor_tensor(OUT[rbase:rbase + 64, dsl],
                                                pb[3][0:64, :], SQ[64:128, 0:512], op=ALU.mult)
                        g.inc(nc.vector.drain(), ev)
                        nc.tensor.wait_ge(ev.h, ev.n)
                    g.bar()

            if DEBUG:
                dst_ = g.sem("dbg")
                g.inc(nc.sync.dma_start(out=outdbg[:, :], in_=OUT[:, :]), dst_, 16)
                g.inc(nc.sync.dma_start(out=scdbg[:, :], in_=SC[:, :]), dst_, 16)
                g.inc(nc.sync.dma_start(out=dbg2[:, 0:2], in_=CREP[:, 0:2]), dst_, 16)
                g.inc(nc.sync.dma_start(out=dbg3[:, :], in_=RECIP[0:1, :]), dst_, 16)
                nc.sync.wait_ge(dst_.h, dst_.n)
                g.bar()

            def evac_res(ob, tb, ps, mm, ev, j):
                nc.vector.wait_ge(mm.h, mm.n)
                sl = slice(ob * TOK + tb * 512, ob * TOK + tb * 512 + 512)
                nc.vector.tensor_tensor(X[:, sl], X[:, sl], ps, op=ALU.add)
                g.inc(nc.vector.drain(), ev)

            proj(i["l0a_wo"], 16,
                 lambda c, tb: OUT[:, c * TW + tb * 512: c * TW + tb * 512 + 512],
                 2, evac_res)


    def s_l0_ffn(first_call):
        with nc.sbuf_tensor(f"XN_f{g.uid()}", [128, KC * TOK], F32) as XN2, \
             nc.sbuf_tensor(f"LST_f{g.uid()}", [1, 2 * TOK], F32) as LST, \
             nc.sbuf_tensor(f"COLS_f{g.uid()}", [128, 5 * KC], F32) as COLS, \
             nc.sbuf_tensor(f"WMX_f{g.uid()}", [128, 2], F32) as WMX:
            ld = g.sem("ldf")
            g.inc(nc.sync.dma_start(out=COLS[:, 0:4 * KC], in_=i["l0f_cols"][:, :]), ld, 16)
            ln_stats(lambda c, tb: X[:, c * TOK + tb * 512: c * TOK + tb * 512 + 512],
                     TOK, LST)
            ln_apply(lambda c, tb: X[:, c * TOK + tb * 512: c * TOK + tb * 512 + 512],
                     lambda c, tb: XN2[:, c * TOK + tb * 512: c * TOK + tb * 512 + 512],
                     TOK, LST)
            ln_stats(lambda c, tb: XN2[:, c * TOK + tb * 512: c * TOK + tb * 512 + 512],
                     TOK, LST)
            nc.vector.wait_ge(ld.h, ld.n)
            g.bar()
            ln_apply(lambda c, tb: XN2[:, c * TOK + tb * 512: c * TOK + tb * 512 + 512],
                     lambda c, tb: XN2[:, c * TOK + tb * 512: c * TOK + tb * 512 + 512],
                     TOK, LST,
                     aff=(COLS[:, 2 * KC:3 * KC], COLS[:, 3 * KC:4 * KC]))

            if first_call:
                mx = g.sem("fmx")
                wl = g.sem("fwl")
                nc.vector.memset(WMX[:, 0:1], 0.0)
                g.inc(nc.vector.drain(), mx)
                with nc.sbuf_tensor(f"WBS{g.uid()}", [128, 2 * KC * 128], F32) as WB:
                    for ob in range(16):
                        buf = WB[:, (ob % 2) * KC * 128:(ob % 2) * KC * 128 + KC * 128]
                        if ob >= 2:
                            nc.sync.wait_ge(mx.h, mx.n - 2)
                        g.inc(nc.sync.dma_start(
                            out=buf, in_=i["l0f_wT"][:, ob * KC * 128:(ob + 1) * KC * 128]), wl, 16)
                        nc.vector.wait_ge(wl.h, wl.off + 16 * (ob + 1))
                        nc.vector.tensor_reduce(WMX[:, 1:2], buf, axis=AXL.X, op=ALU.max,
                                                apply_absolute_value=True)
                        nc.vector.drain()
                        nc.vector.tensor_tensor(WMX[:, 0:1], WMX[:, 0:1], WMX[:, 1:2], op=ALU.max)
                        g.inc(nc.vector.drain(), mx)
                g.bar()
                cross_maxP(WMX[:, 0:1], SC[0:1, 8:9], 128)
            sb = g.sem("fsb")
            scalar_bcast(SC[0:1, 8:9], WMX[:, 0:1], 128)
            nc.vector.tensor_scalar(COLS[:, 4 * KC:5 * KC], COLS[:, 0:KC], WMX[:, 0:1], None,
                                    op0=ALU.mult)
            g.inc(nc.vector.drain(), sb)
            g.bar()

            mm = g.sem("fmm")
            ev = g.sem("fev")
            wl = g.sem("fwl2")
            sg = g.sem("fsg")
            with nc.sbuf_tensor(f"WBF{g.uid()}", [128, 2 * KC * 128], F32) as WB:
                j = 0
                for ob in range(16):
                    buf = WB[:, (ob % 2) * KC * 128:(ob % 2) * KC * 128 + KC * 128]
                    if ob >= 2:
                        nc.sync.wait_ge(mm.h, mm.off + (ob - 1) * 2)
                    g.inc(nc.sync.dma_start(
                        out=buf, in_=i["l0f_wT"][:, ob * KC * 128:(ob + 1) * KC * 128]), wl, 16)
                    nc.scalar.wait_ge(wl.h, wl.off + 16 * (ob + 1))
                    g.inc(nc.scalar.activation(buf, buf, AF.Sign), sg)
                    nc.tensor.wait_ge(sg.h, sg.n)
                    for tb in range(2):
                        psb = pb[j % 2]
                        if j >= 2:
                            nc.tensor.wait_ge(ev.h, ev.off + j - 1)
                        for c in range(KC):
                            nc.tensor.matmul(psb[:, :], buf[:, c * 128:(c + 1) * 128],
                                             XN2[:, c * TOK + tb * 512: c * TOK + tb * 512 + 512],
                                             start=(c == 0), stop=(c == KC - 1))
                        g.inc(nc.tensor.drain(), mm)
                        nc.vector.wait_ge(mm.h, mm.n)
                        ebuf = EV[:, (j % 2) * 512:(j % 2) * 512 + 512]
                        nc.vector.tensor_scalar(ebuf, psb[:, :],
                                                COLS[:, 4 * KC + ob:4 * KC + ob + 1],
                                                COLS[:, KC + ob:KC + ob + 1],
                                                op0=ALU.mult, op1=ALU.add)
                        nc.vector.drain()
                        sl = slice(ob * TOK + tb * 512, ob * TOK + tb * 512 + 512)
                        nc.vector.tensor_tensor(X[:, sl], X[:, sl], ebuf, op=ALU.add)
                        g.inc(nc.vector.drain(), ev)
                        j += 1
            g.bar()

    def s3_l0_attn_b():
        from contextlib import ExitStack
        ld = g.sem("ldb")
        ccs = g.sem("bccs")
        esq2 = ExitStack()
        QU = esq2.enter_context(nc.sbuf_tensor("QU_b", [128, 8 * TOK], F32))
        with nc.sbuf_tensor("XN_b", [128, KC * TOK], F32) as XN3, \
             nc.sbuf_tensor("LST_b", [1, 2 * TOK], F32) as LST:
            ln_stats(lambda c, tb: X[:, c * TOK + tb * 512: c * TOK + tb * 512 + 512],
                     TOK, LST)
            ln_apply(lambda c, tb: X[:, c * TOK + tb * 512: c * TOK + tb * 512 + 512],
                     lambda c, tb: XN3[:, c * TOK + tb * 512: c * TOK + tb * 512 + 512],
                     TOK, LST)

            mm = g.sem("bmm")
            ev = g.sem("bev")
            st = g.sem("bst")
            with nc.sbuf_tensor("WKVD", [128, KC * 256], F32) as WKVD:
                g.inc(nc.sync.dma_start(out=WKVD[:, :], in_=i["l0b_wkvd"][:, :]), ld, 16)
                nc.tensor.wait_ge(ld.h, ld.n)
                for tt in range(8):
                    psb = pb[tt % 2]
                    if tt >= 2:
                        nc.tensor.wait_ge(ev.h, ev.n - 1)
                    for c in range(KC):
                        nc.tensor.matmul(psb[:, 0:256],
                                         XN3[:, c * TOK + tt * 128: c * TOK + tt * 128 + 128],
                                         WKVD[:, c * 256:(c + 1) * 256],
                                         start=(c == 0), stop=(c == KC - 1))
                    g.inc(nc.tensor.drain(), mm)
                    nc.vector.wait_ge(mm.h, mm.n)
                    ebuf = EV[:, (tt % 2) * 512:(tt % 2) * 512 + 256]
                    nc.vector.tensor_copy(ebuf, psb[:, 0:256])
                    g.inc(nc.vector.drain(), ev)
                    nc.sync.wait_ge(ev.h, ev.n)
                    g.inc(nc.sync.dma_start(out=cc_lat_in[tt * 128:(tt + 1) * 128, :],
                                            in_=ebuf), st, 16)
            nc.gpsimd.wait_ge(st.h, st.n)
            g.inc(nc.gpsimd.collective_compute(
                "AllGather", ALU.bypass, replica_groups=RG,
                ins=[cc_lat_in[:, :]], outs=[cc_lat_out[:, :]]), ccs)

            with nc.sbuf_tensor("QL_b", [128, 2 * TOK], F32) as QL:
                def evac_ql(ob, tb, ps, mm_, ev_, j):
                    nc.vector.wait_ge(mm_.h, mm_.n)
                    nc.vector.tensor_copy(QL[:, ob * TOK + tb * 512: ob * TOK + tb * 512 + 512], ps)
                    g.inc(nc.vector.drain(), ev_)
                proj(i["l0b_wqd"], 2,
                     lambda c, tb: XN3[:, c * TOK + tb * 512: c * TOK + tb * 512 + 512],
                     2, evac_ql)

                def evac_qu(ob, tb, ps, mm_, ev_, j):
                    nc.vector.wait_ge(mm_.h, mm_.n)
                    nc.vector.tensor_copy(QU[:, ob * TOK + tb * 512: ob * TOK + tb * 512 + 512], ps)
                    g.inc(nc.vector.drain(), ev_)
                proj(i["l0b_wqu"], 8,
                     lambda c, tb: QL[:, c * TOK + tb * 512: c * TOK + tb * 512 + 512],
                     2, evac_qu, kc=2)

        stq = g.sem("stq")
        for ch in range(8):
            g.inc(nc.sync.dma_start(out=qstash[ch, :, :], in_=QU[:, ch * TOK:(ch + 1) * TOK]),
                  stq, 16)
        nc.sync.wait_ge(stq.h, stq.n)
        g.bar()
        esq2.close()

        es = ExitStack()
        COSQ = es.enter_context(nc.sbuf_tensor("COSQ_b", [64, TOK], F32))
        SINQ = es.enter_context(nc.sbuf_tensor("SINQ_b", [64, TOK], F32))
        COSK = es.enter_context(nc.sbuf_tensor("COSK_b", [64, S], F32))
        SINK = es.enter_context(nc.sbuf_tensor("SINK_b", [64, S], F32))
        LATG = es.enter_context(nc.sbuf_tensor("LATG_b", [128, 2 * S], F32))

        mm = g.sem("rmm")
        ev = g.sem("rev")
        sq = g.sem("rsq")
        with nc.sbuf_tensor("RF", [1, 64], F32) as RF, \
             nc.sbuf_tensor("TRW", [1, S], F32) as TRW:
            g.inc(nc.sync.dma_start(out=SC[0:1, 9:10], in_=i["tpos"][:, :]), ld, 16)
            nc.gpsimd.iota(RF[0:1, 0:32], pattern=[[1, 32]], base=0,
                           channel_multiplier=0, allow_small_or_imprecise_dtypes=True)
            nc.gpsimd.iota(TRW[0:1, :], pattern=[[1, S]], base=0,
                           channel_multiplier=0, allow_small_or_imprecise_dtypes=True)
            gp = g.sem("rgp")
            g.inc(nc.gpsimd.drain(), gp)
            nc.vector.wait_ge(gp.h, gp.n)
            nc.vector.tensor_scalar_mul(RF[0:1, 0:32], RF[0:1, 0:32],
                                        -2.0 / 64.0 * 9.210340371976184)
            g.inc(nc.vector.drain(), ev)
            nc.scalar.wait_ge(ev.h, ev.n)
            g.inc(nc.scalar.activation(RF[0:1, 0:32], RF[0:1, 0:32], AF.Exp), sq)
            nc.vector.wait_ge(sq.h, sq.n)

            def build_cs(cos_dst, sin_dst, pos_row, ncols, pref):
                mm2 = g.sem(pref + "mm")
                ev2 = g.sem(pref + "ev")
                sq2 = g.sem(pref + "sq")
                I32 = mybir.dt.int32
                with nc.sbuf_tensor(f"ITMP{g.uid()}", [32, 512], I32) as IT:
                    for tb in range(ncols // 512):
                        g.inc(nc.vector.drain(), mm2)
                        g.inc(nc.scalar.drain(), mm2)
                        nc.tensor.wait_ge(mm2.h, mm2.n)
                        g.inc(nc.tensor.matmul(pb[4][0:32, :], RF[0:1, 0:32],
                                               pos_row[0:1, tb * 512:(tb + 1) * 512],
                                               start=True, stop=True), mm2)
                        nc.vector.wait_ge(mm2.h, mm2.n)
                        for dst, shiftv in ((sin_dst, 0.0), (cos_dst, 0.25)):
                            wk = SQ[0:32, 0:512]
                            wk2 = SQ[0:32, 512:1024]
                            nc.vector.tensor_scalar(wk, pb[4][0:32, :],
                                                    1.0 / 6.283185307179586, shiftv,
                                                    op0=ALU.mult, op1=ALU.add)
                            nc.vector.drain()
                            nc.vector.tensor_copy(IT[:, :], wk)
                            nc.vector.drain()
                            nc.vector.tensor_copy(wk2, IT[:, :])
                            nc.vector.drain()
                            nc.vector.tensor_tensor(wk, wk, wk2, op=ALU.subtract)
                            nc.vector.drain()
                            nc.vector.tensor_scalar(wk2, wk, 0.5, None, op0=ALU.is_ge)
                            nc.vector.drain()
                            nc.vector.scalar_tensor_tensor(wk, wk2, -1.0, wk,
                                                           op0=ALU.mult, op1=ALU.add)
                            nc.vector.drain()
                            nc.vector.tensor_scalar(wk2, wk, -0.5, None, op0=ALU.is_lt)
                            nc.vector.drain()
                            nc.vector.scalar_tensor_tensor(wk, wk2, 1.0, wk,
                                                           op0=ALU.mult, op1=ALU.add)
                            nc.vector.drain()
                            nc.vector.tensor_scalar_mul(wk, wk, 6.283185307179586)
                            g.inc(nc.vector.drain(), ev2)
                            nc.scalar.wait_ge(ev2.h, ev2.n)
                            g.inc(nc.scalar.activation(dst[0:32, tb * 512:tb * 512 + 512],
                                                       wk, AF.Sin), sq2)
                            nc.vector.wait_ge(sq2.h, sq2.n)
                        g.inc(nc.vector.drain(), mm2)
                        nc.tensor.wait_ge(mm2.h, mm2.n)
                nc.vector.tensor_copy(cos_dst[32:64, 0:ncols], cos_dst[0:32, 0:ncols])
                nc.vector.drain()
                nc.vector.tensor_copy(sin_dst[32:64, 0:ncols], sin_dst[0:32, 0:ncols])
                nc.vector.drain()
                nc.vector.tensor_scalar_mul(sin_dst[0:32, 0:ncols], sin_dst[0:32, 0:ncols],
                                            -1.0)
                nc.vector.drain()

            with nc.sbuf_tensor("TRQ", [1, TOK], F32) as TRQ:
                nc.vector.wait_ge(ld.h, ld.n)
                nc.vector.tensor_scalar(TRQ[0:1, :], TRW[0:1, 0:TOK], SC[0:1, 9:10], None,
                                        op0=ALU.add)
                nc.vector.drain()
                build_cs(COSQ, SINQ, TRQ, TOK, "q")
            build_cs(COSK, SINK, TRW, S, "k")
        nc.vector.tensor_scalar_mul(COSQ[0:64, :], COSQ[0:64, :], 0.125)
        nc.vector.drain()
        nc.vector.tensor_scalar_mul(SINQ[0:64, :], SINQ[0:64, :], 0.125)
        nc.vector.drain()
        g.bar()

        mm = g.sem("tmm")
        ev = g.sem("tev")
        ld2 = g.sem("tld")
        nc.sync.wait_ge(ccs.h, ccs.n)
        with nc.sbuf_tensor("LTT", [128, 2 * 256], F32) as LTT:
            for tt in range(32):
                buf = LTT[:, (tt % 2) * 256:(tt % 2) * 256 + 256]
                if tt >= 2:
                    nc.sync.wait_ge(mm.h, mm.n - 3)
                g.inc(nc.sync.dma_start(out=buf, in_=cc_lat_out[tt * 128:(tt + 1) * 128, :]),
                      ld2, 16)
                nc.tensor.wait_ge(ld2.h, ld2.off + 16 * (tt + 1))
                for c2 in range(2):
                    nc.tensor.transpose(pb[2][:, 0:128], buf[:, c2 * 128:(c2 + 1) * 128],
                                        IDENT[:, :])
                    g.inc(nc.tensor.drain(), mm)
                    nc.vector.wait_ge(mm.h, mm.n)
                    nc.vector.tensor_copy(LATG[:, c2 * S + tt * 128: c2 * S + tt * 128 + 128],
                                          pb[2][:, 0:128])
                    g.inc(nc.vector.drain(), ev)
                    nc.tensor.wait_ge(ev.h, ev.n)
        g.bar()

        with nc.sbuf_tensor("KH_b", [64, S], F32) as KH3, \
             nc.sbuf_tensor("VH_b", [128, 32 * 130], F32) as VH3, \
             nc.sbuf_tensor("WKU", [128, 2 * 128], F32) as WKU, \
             nc.sbuf_tensor("WVU", [128, 2 * 128], F32) as WVU, \
             nc.sbuf_tensor("RSK", [64, 512], F32) as RSK, \
             nc.sbuf_tensor("QHH_b", [64, TOK], F32) as QHH, \
             nc.sbuf_tensor("QSH_b", [64, TOK], F32) as QSH:
            PT = EV
            for kt in range(32):
                nc.vector.memset(VH3[:, kt * 130 + 64:kt * 130 + 65], 1.0)
                nc.vector.memset(VH3[:, kt * 130 + 129:kt * 130 + 130], 1.0)
            nc.vector.drain()
            g.bar()
            hl = g.sem("fhl")
            mm = g.sem("fmm3")
            mo = g.sem("fmo3")
            sq = g.sem("fsq3")
            ev = g.sem("fev3")
            st = g.sem("fst3")
            for h in range(16):
                rbase = 64 * (h % 2)
                cch = h // 2
                g.inc(nc.sync.dma_start(out=WKU[:, :],
                                        in_=i["l0b_wku"][:, (h // 2) * 256:(h // 2 + 1) * 256]),
                      hl, 16)
                g.inc(nc.sync.dma_start(out=WVU[:, :],
                                        in_=i["l0b_wvu"][:, h * 256:(h + 1) * 256]), hl, 16)
                g.inc(nc.sync.dma_start(out=QHH[:, :], in_=qstash[cch, rbase:rbase + 64, :]),
                      hl, 16)
                nc.tensor.wait_ge(hl.h, hl.n)
                for tb in range(8):
                    psb = pb[tb % 2]
                    if tb >= 2:
                        nc.tensor.wait_ge(ev.h, ev.n - 1)
                    for c2 in range(2):
                        nc.tensor.matmul(psb[0:64, :],
                                         WKU[:, c2 * 128 + 64 * (h % 2):
                                             c2 * 128 + 64 * (h % 2) + 64],
                                         LATG[:, c2 * S + tb * 512: c2 * S + tb * 512 + 512],
                                         start=(c2 == 0), stop=(c2 == 1))
                    g.inc(nc.tensor.drain(), mm)
                    nc.vector.wait_ge(mm.h, mm.n)
                    nc.vector.tensor_copy(KH3[:, tb * 512:(tb + 1) * 512], psb[0:64, :])
                    g.inc(nc.vector.drain(), ev)
                for tt in range(32):
                    psb = pb[2 + (tt % 2) * 3]
                    if tt >= 2:
                        nc.tensor.wait_ge(ev.h, ev.n - 1)
                    for c2 in range(2):
                        nc.tensor.matmul(psb[:, 0:128],
                                         LATG[:, c2 * S + tt * 128: c2 * S + tt * 128 + 128],
                                         WVU[:, c2 * 128:(c2 + 1) * 128],
                                         start=(c2 == 0), stop=(c2 == 1))
                    g.inc(nc.tensor.drain(), mm)
                    nc.vector.wait_ge(mm.h, mm.n)
                    nc.vector.tensor_copy(VH3[:, tt * 130:tt * 130 + 64], psb[:, 0:64])
                    nc.vector.drain()
                    nc.vector.tensor_copy(VH3[:, tt * 130 + 65:tt * 130 + 129], psb[:, 64:128])
                    g.inc(nc.vector.drain(), ev)
                g.bar()
                nc.vector.tensor_copy(QSH[0:32, :], QHH[32:64, :])
                nc.vector.drain()
                nc.vector.tensor_copy(QSH[32:64, :], QHH[0:32, :])
                nc.vector.drain()
                nc.vector.tensor_tensor(QHH[:, :], QHH[:, :], COSQ[0:64, :], op=ALU.mult)
                nc.vector.drain()
                nc.vector.tensor_tensor(QSH[:, :], QSH[:, :], SINQ[0:64, :], op=ALU.mult)
                nc.vector.drain()
                nc.vector.tensor_tensor(QHH[:, :], QHH[:, :], QSH[:, :], op=ALU.add)
                nc.vector.drain()
                for tb in range(8):
                    ksl = KH3[:, tb * 512:(tb + 1) * 512]
                    csl = COSK[0:64, tb * 512:(tb + 1) * 512]
                    ssl = SINK[0:64, tb * 512:(tb + 1) * 512]
                    nc.vector.tensor_copy(RSK[0:32, :], ksl[32:64, :])
                    nc.vector.drain()
                    nc.vector.tensor_copy(RSK[32:64, :], ksl[0:32, :])
                    nc.vector.drain()
                    nc.vector.tensor_tensor(ksl[0:64, :], ksl[0:64, :], csl, op=ALU.mult)
                    nc.vector.drain()
                    nc.vector.tensor_tensor(RSK[0:64, :], RSK[0:64, :], ssl, op=ALU.mult)
                    nc.vector.drain()
                    nc.vector.tensor_tensor(ksl[0:64, :], ksl[0:64, :], RSK[0:64, :],
                                            op=ALU.add)
                    nc.vector.drain()
                g.inc(nc.vector.drain(), ev)
                kn = g.sem(f"kn{h}")
                nc.vector.memset(SC[0:1, 10:11], 0.0)
                nc.vector.memset(SC[0:1, 13:14], 0.0)
                g.inc(nc.vector.drain(), kn)
                nc.scalar.wait_ge(ev.h, ev.n)
                for src_, ncols_, slot in ((KH3, 8, 10), (QHH, 2, 13)):
                    for tb in range(ncols_):
                        buf = SQ[0:64, (tb % 2) * 512:(tb % 2) * 512 + 512]
                        if tb >= 2 or slot == 13:
                            nc.scalar.wait_ge(kn.h, kn.n - 1)
                        g.inc(nc.scalar.activation(buf, src_[0:64, tb * 512:(tb + 1) * 512],
                                                   AF.Square), sq)
                        nc.tensor.wait_ge(sq.h, sq.n)
                        nc.tensor.wait_ge(kn.h, kn.n)
                        g.inc(nc.tensor.matmul(pst[0][0:1, :], ones[0:64, :], buf,
                                               start=True, stop=True), mm)
                        nc.vector.wait_ge(mm.h, mm.n)
                        nc.vector.tensor_reduce(SC[0:1, 11:12], pst[0][0:1, :], axis=AXL.X,
                                                op=ALU.max)
                        nc.vector.drain()
                        nc.vector.tensor_tensor(SC[0:1, slot:slot + 1], SC[0:1, slot:slot + 1],
                                                SC[0:1, 11:12], op=ALU.max)
                        g.inc(nc.vector.drain(), kn)
                nc.vector.tensor_tensor(SC[0:1, 12:13], SC[0:1, 10:11], SC[0:1, 13:14],
                                        op=ALU.mult)
                g.inc(nc.vector.drain(), ev)
                nc.scalar.wait_ge(ev.h, ev.n)
                g.inc(nc.scalar.activation(SC[0:1, 12:13], SC[0:1, 12:13], AF.Sqrt), sq)
                nc.vector.wait_ge(sq.h, sq.n)
                nc.vector.tensor_scalar_mul(SC[0:1, 12:13], SC[0:1, 12:13], -1.0)
                nc.vector.drain()
                scalar_bcast(SC[0:1, 12:13], CREP[:, 0:1], 128)
                g.bar()

                for qb in range(2):
                    for kt in range(32):
                        psb = pb[kt % 2]
                        if kt >= 2:
                            nc.tensor.wait_ge(sq.h, sq.n - 1)
                        nc.tensor.matmul(psb[:, :], KH3[:, kt * 128:(kt + 1) * 128],
                                         QHH[:, qb * 512:(qb + 1) * 512],
                                         start=True, stop=True)
                        g.inc(nc.tensor.drain(), mm)
                        ptile = PT[:, (kt % 2) * 512:(kt % 2) * 512 + 512]
                        nc.scalar.wait_ge(mm.h, mm.n)
                        if kt >= 2:
                            nc.scalar.wait_ge(mo.h, mo.n - 2)
                        g.inc(nc.scalar.activation(ptile, psb[:, :], AF.Exp,
                                                   bias=CREP[:, 0:1]), sq)
                        nc.tensor.wait_ge(sq.h, sq.n)
                        g.inc(nc.tensor.matmul(pb[3][0:65, :],
                                               VH3[:, kt * 130:kt * 130 + 65],
                                               ptile, start=(kt == 0), stop=(kt == 31),
                                               skip_group_check=True), mo)
                        g.inc(nc.tensor.matmul(pb[4][0:65, :],
                                               VH3[:, kt * 130 + 65:kt * 130 + 130],
                                               ptile, start=(kt == 0), stop=(kt == 31),
                                               skip_group_check=True), mo)
                    nc.vector.wait_ge(mo.h, mo.n)
                    nc.vector.reciprocal(RECIP[0:1, :], pb[3][64:65, :])
                    g.inc(nc.vector.drain(), ev)
                    nc.tensor.wait_ge(ev.h, ev.n)
                    g.inc(nc.tensor.matmul(pb[5][0:64, :], ONESR[0:1, 0:64],
                                           RECIP[0:1, :], start=True, stop=True), mm)
                    nc.vector.wait_ge(mm.h, mm.n)
                    nc.vector.tensor_copy(SQ[64:128, 0:512], pb[5][0:64, :])
                    nc.vector.drain()
                    obuf = EV[:, 0:512]
                    nc.vector.tensor_tensor(obuf[0:64, :], pb[3][0:64, :], SQ[64:128, 0:512],
                                            op=ALU.mult)
                    nc.vector.drain()
                    nc.vector.tensor_tensor(obuf[64:128, :], pb[4][0:64, :], SQ[64:128, 0:512],
                                            op=ALU.mult)
                    g.inc(nc.vector.drain(), ev)
                    nc.tensor.wait_ge(ev.h, ev.n)
                    nc.sync.wait_ge(ev.h, ev.n)
                    g.inc(nc.sync.dma_start(out=bstash[h, :, qb * 512:(qb + 1) * 512],
                                            in_=obuf), st, 16)
                    nc.scalar.wait_ge(st.h, st.n)
                g.bar()
        es.close()
        g.bar()

        with nc.sbuf_tensor("OST_b", [128, KC * TOK], F32) as OST:
            ld3 = g.sem("wold")
            g.inc(nc.sync.dma_start(
                out=OST[:].rearrange("p (c t) -> p c t", c=KC),
                in_=AP(bstash.tensor, 0, [[TOK, 128], [128 * TOK, 16], [1, TOK]])), ld3, 16)
            nc.sync.wait_ge(ld3.h, ld3.n)
            g.bar()

            def evac_res3(ob, tb, ps, mm_, ev_, j):
                nc.vector.wait_ge(mm_.h, mm_.n)
                sl = slice(ob * TOK + tb * 512, ob * TOK + tb * 512 + 512)
                nc.vector.tensor_tensor(X[:, sl], X[:, sl], ps, op=ALU.add)
                g.inc(nc.vector.drain(), ev_)

            proj(i["l0b_wo"], 16,
                 lambda c, tb: OST[:, c * TOK + tb * 512: c * TOK + tb * 512 + 512],
                 2, evac_res3)


    def s5_l1_attn_a():
        from contextlib import ExitStack
        ccs = g.sem("a1cc")
        with nc.sbuf_tensor("XN_5", [128, KC * TOK], F32) as XN5, \
             nc.sbuf_tensor("LST_5", [1, 2 * TOK], F32) as LST:
            ln_stats(lambda c, tb: X[:, c * TOK + tb * 512: c * TOK + tb * 512 + 512],
                     TOK, LST)
            ln_apply(lambda c, tb: X[:, c * TOK + tb * 512: c * TOK + tb * 512 + 512],
                     lambda c, tb: XN5[:, c * TOK + tb * 512: c * TOK + tb * 512 + 512],
                     TOK, LST)
            st = g.sem("a1st")
            g.inc(nc.sync.dma_start(
                out=AP(cc_xn_in.tensor, 0, [[TOK, 128], [128 * TOK, KC], [1, TOK]]),
                in_=XN5[:].rearrange("p (c t) -> p c t", c=KC)), st, 16)
            nc.gpsimd.wait_ge(st.h, st.n)
            g.inc(nc.gpsimd.collective_compute(
                "AllGather", ALU.bypass, replica_groups=RG,
                ins=[cc_xn_in[:, :]], outs=[cc_xn_out[:, :]]), ccs)
        g.bar()

        with nc.sbuf_tensor("WQ5", [128, 4 * KC * 128], F32) as WQ5, \
             nc.sbuf_tensor("WK5", [128, 4 * KC * 128], F32) as WK5, \
             nc.sbuf_tensor("WV5", [128, 4 * KC * 128], F32) as WV5, \
             nc.sbuf_tensor("WIL5", [128, KC * 8], F32) as WIL5, \
             nc.sbuf_tensor("XG5", [128, KC * 512], F32) as XG5:
            ld = g.sem("a1ld")
            mm = g.sem("a1mm")
            ev = g.sem("a1ev")
            st = g.sem("a1st2")
            xl = g.sem("a1xl")
            for wt, nm in ((WQ5, "a1_wq"), (WK5, "a1_wk"), (WV5, "a1_wv"),
                           (WIL5, "a1_wilr")):
                g.inc(nc.sync.dma_start(out=wt[:], in_=i[nm][:, :]), ld, 16)
            nc.sync.wait_ge(ccs.h, ccs.n)
            nc.tensor.wait_ge(ld.h, ld.n)
            for gt in range(8):
                Q_, tb = divmod(gt, 2)
                buf = XG5[:, 0:KC * 512]
                if gt >= 1:
                    nc.sync.wait_ge(mm.h, mm.off + gt * 13)
                g.inc(nc.sync.dma_start(
                    out=buf.rearrange("p (c t) -> p c t", c=KC),
                    in_=AP(cc_xn_out.tensor, Q_ * D * TOK + tb * 512,
                           [[TOK, 128], [128 * TOK, KC], [1, 512]])), xl, 16)
                nc.tensor.wait_ge(xl.h, xl.off + 16 * (gt + 1))
                for wt, dst in ((WQ5, xcfm), (WK5, xbfm), (WV5, xafm)):
                    for ob in range(4):
                        psb = pb[ob % 2]
                        nc.tensor.wait_ge(ev.h, max(ev.off, ev.n - 1))
                        for c in range(KC):
                            nc.tensor.matmul(
                                psb[:, :],
                                wt[:, (ob * KC + c) * 128:(ob * KC + c) * 128 + 128],
                                buf[:, c * 512:(c + 1) * 512],
                                start=(c == 0), stop=(c == KC - 1))
                        g.inc(nc.tensor.drain(), mm)
                        nc.vector.wait_ge(mm.h, mm.n)
                        ebuf = EV[:, (ob % 2) * 512:(ob % 2) * 512 + 512]
                        nc.vector.tensor_copy(ebuf, psb[:, :])
                        g.inc(nc.vector.drain(), ev)
                        nc.sync.wait_ge(ev.h, ev.n)
                        g.inc(nc.sync.dma_start(
                            out=dst[ob, :, Q_ * TOK + tb * 512: Q_ * TOK + tb * 512 + 512],
                            in_=ebuf), st, 16)
                nc.tensor.wait_ge(ev.h, max(ev.off, ev.n - 1))
                for c in range(KC):
                    nc.tensor.matmul(pb[2][0:8, :], WIL5[:, c * 8:(c + 1) * 8],
                                     buf[:, c * 512:(c + 1) * 512],
                                     start=(c == 0), stop=(c == KC - 1))
                g.inc(nc.tensor.drain(), mm)
                nc.vector.wait_ge(mm.h, mm.n)
                nc.vector.tensor_copy(EV[0:8, 0:512], pb[2][0:8, :])
                g.inc(nc.vector.drain(), ev)
                nc.sync.wait_ge(ev.h, ev.n)
                g.inc(nc.sync.dma_start(
                    out=ilrst[:, Q_ * TOK + tb * 512: Q_ * TOK + tb * 512 + 512],
                    in_=EV[0:8, 0:512]), st, 16)
            nc.sync.wait_ge(st.h, st.n)
        g.bar()

        with nc.sbuf_tensor("CN5", [128, 21 * 128], F32) as CN, \
             nc.sbuf_tensor("CL5", [128, 24], F32) as CL, \
             nc.sbuf_tensor("SB5", [128, 16 * 64], F32) as SB, \
             nc.sbuf_tensor("XGB5", [128, 2 * 4 * 1024], F32) as XGB, \
             nc.sbuf_tensor("XGC5", [128, 2 * 4 * 1024], F32) as XGC, \
             nc.sbuf_tensor("XGA5", [128, 2 * 4 * 1024], F32) as XGA, \
             nc.sbuf_tensor("XCW5", [128, 4 * 1024], F32) as XCW:
            ld = g.sem("a1ld2")
            mm = g.sem("a1mm2")
            ev = g.sem("a1ev2")
            sq = g.sem("a1sq2")
            st = g.sem("a1st3")
            cld = g.sem("a1cld")
            gp = g.sem("a1gp")
            W1ST = SB[:, 0:256]
            B1ST = SB[0:8, 256:320]
            LNW = SB[:, 5 * 64:6 * 64]
            LNB = SB[:, 6 * 64:7 * 64]
            XBT = SB[:, 7 * 64:8 * 64]
            XCT = SB[:, 8 * 64:9 * 64]
            TGT = SB[:, 9 * 64:10 * 64]
            ZC = SB[:, 10 * 64:11 * 64]
            XH = SB[:, 11 * 64:12 * 64]
            GG = SB[:, 12 * 64:13 * 64]
            GRD = SB[:, 13 * 64:14 * 64]
            GRDC = SB[:, 14 * 64:15 * 64]
            OUTT = SB[:, 15 * 64:16 * 64]
            DGB = CN[:, 12 * 128:16 * 128]
            TRILM = CN[:, 8 * 128:9 * 128]
            I2 = CN[:, 9 * 128:9 * 128 + 64]
            IOTC = CN[:, 10 * 128:11 * 128]
            TSEL = CN[:, 11 * 128:11 * 128 + 16]
            HSELT = CN[:, 11 * 128 + 16:11 * 128 + 24]
            SEL15 = CN[:, 11 * 128 + 24:11 * 128 + 32]
            HSEL2 = CN[0:8, 11 * 128 + 32:11 * 128 + 32 + 96][0:8, 0:0 + 0] if False else None
            for nm, col in (("idxcol", 0), ("pcol", 1), ("phcol", 2), ("pm16col", 3)):
                g.inc(nc.sync.dma_start(out=CL[:, col:col + 1], in_=i[nm][:, :]), ld, 16)
            g.inc(nc.sync.dma_start(out=CL[0:8, 4:5], in_=i["a1_bilr8"][:, :]), ld, 16)
            g.inc(nc.sync.dma_start(out=W1ST, in_=i["a1_w1init"][:, :]), ld, 16)
            g.inc(nc.sync.dma_start(out=B1ST, in_=i["a1_b1init"][:, :]), ld, 16)
            g.inc(nc.sync.dma_start(out=SB[0:8, 5 * 64:6 * 64], in_=i["a1_lnw"][:, :]), ld, 16)
            g.inc(nc.sync.dma_start(out=SB[0:8, 6 * 64:7 * 64], in_=i["a1_lnb"][:, :]), ld, 16)
            nc.vector.memset(CL[:, 5:6], 1e-6)
            nc.vector.wait_ge(ld.h, ld.n)
            for c in range(4):
                m3 = CN[:, c * 128:(c + 1) * 128]
                nc.vector.memset(m3, 0.0)
                nc.vector.drain()
                nc.vector.memset(m3[0:64, 16 * (2 * c):16 * (2 * c) + 16], 1.0)
                nc.vector.drain()
                nc.vector.memset(m3[64:128, 16 * (2 * c + 1):16 * (2 * c + 1) + 16], 1.0)
                nc.vector.drain()
            nc.gpsimd.iota(IOTC, pattern=[[1, 128]], base=0, channel_multiplier=0,
                           allow_small_or_imprecise_dtypes=True)
            g.inc(nc.gpsimd.drain(), gp)
            nc.vector.wait_ge(gp.h, gp.n)
            nc.vector.tensor_scalar(CL[:, 21:22], CL[:, 2:3], 16.0, 16.0,
                                    op0=ALU.mult, op1=ALU.add)
            nc.vector.drain()
            nc.vector.tensor_scalar(TRILM, IOTC, CL[:, 1:2], None, op0=ALU.is_ge)
            nc.vector.drain()
            nc.vector.tensor_scalar(CN[:, 9 * 128:10 * 128], IOTC, CL[:, 21:22], None,
                                    op0=ALU.is_lt)
            nc.vector.drain()
            nc.vector.tensor_tensor(TRILM, TRILM, CN[:, 9 * 128:10 * 128], op=ALU.mult)
            nc.vector.drain()
            HI64 = CN[:, 9 * 128:10 * 128]
            nc.vector.tensor_scalar(HI64, IOTC, 63.5, None, op0=ALU.is_ge)
            nc.vector.drain()
            for c in range(4):
                mo = CN[:, (4 + c) * 128:(5 + c) * 128]
                nc.vector.tensor_scalar(mo, HI64, float(2 * c), None, op0=ALU.add)
                nc.vector.drain()
                nc.vector.tensor_scalar(mo, mo, CL[:, 2:3], None, op0=ALU.is_equal)
                nc.vector.drain()
            nc.vector.tensor_copy(I2[0:64, :], IDENT[0:64, 0:64])
            nc.vector.drain()
            nc.vector.tensor_copy(I2[64:128, :], IDENT[0:64, 0:64])
            nc.vector.drain()
            nc.vector.tensor_scalar(TSEL, IOTC[:, 0:16], CL[:, 3:4], None, op0=ALU.is_equal)
            nc.vector.drain()
            nc.vector.tensor_scalar(HSELT, IOTC[:, 0:8], CL[:, 2:3], None, op0=ALU.is_equal)
            nc.vector.drain()
            nc.vector.tensor_scalar(SEL15, IOTC[:, 0:8], 16.0, 15.0, op0=ALU.mult,
                                    op1=ALU.add)
            nc.vector.drain()
            nc.vector.tensor_scalar(SEL15, SEL15, CL[:, 1:2], None, op0=ALU.is_equal)
            nc.vector.drain()
            BM16 = CN[:, 16 * 128:17 * 128]
            PHJ = CN[:, 17 * 128:17 * 128 + 8]
            RAW8 = CN[0:8, 18 * 128:19 * 128]
            SIG8 = CN[0:8, 17 * 128 + 16:17 * 128 + 16 + 128]
            TOKT = CN[:, 18 * 128 + 64:18 * 128 + 72]
            COTMP = CN[:, 19 * 128:20 * 128]
            nc.gpsimd.iota(BM16, pattern=[[0, 8], [1, 16]], base=0, channel_multiplier=0,
                           allow_small_or_imprecise_dtypes=True)
            g.inc(nc.gpsimd.drain(), gp)
            nc.vector.wait_ge(gp.h, gp.n)
            nc.vector.tensor_scalar(BM16, BM16, CL[:, 3:4], None, op0=ALU.is_equal)
            nc.vector.drain()
            nc.vector.tensor_scalar(PHJ, IOTC[:, 0:8], CL[:, 2:3], None, op0=ALU.is_equal)
            nc.vector.drain()
            HSEL2 = CN[0:8, 9 * 128 + 64:9 * 128 + 64 + 128][0:8, :] if False else None
            HSEL2 = CN[0:8, 20 * 128:21 * 128]
            nc.gpsimd.iota(HSEL2, pattern=[[1, 128]], base=0, channel_multiplier=-16,
                           allow_small_or_imprecise_dtypes=True)
            g.inc(nc.gpsimd.drain(), gp)
            nc.vector.wait_ge(gp.h, gp.n)
            nc.vector.tensor_scalar(CN[0:8, 12 * 128:13 * 128], HSEL2, -0.5, None,
                                    op0=ALU.is_ge)
            nc.vector.drain()
            nc.vector.tensor_scalar(HSEL2, HSEL2, 15.5, None, op0=ALU.is_le)
            nc.vector.drain()
            nc.vector.tensor_tensor(HSEL2, HSEL2, CN[0:8, 12 * 128:13 * 128], op=ALU.mult)
            g.inc(nc.vector.drain(), ev)
            nc.tensor.wait_ge(ev.h, ev.n)
            g.inc(nc.tensor.matmul(pb[0][:, 0:64], HSEL2, SB[0:8, 5 * 64:6 * 64],
                                   start=True, stop=True), mm)
            g.inc(nc.tensor.matmul(pb[1][:, 0:64], HSEL2, SB[0:8, 6 * 64:7 * 64],
                                   start=True, stop=True), mm)
            nc.vector.wait_ge(mm.h, mm.n)
            nc.vector.tensor_copy(LNW, pb[0][:, 0:64])
            nc.vector.drain()
            nc.vector.tensor_copy(LNB, pb[1][:, 0:64])
            nc.vector.drain()
            g.bar()

            for grp in range(4):
                gbuf = grp % 2
                for dstb, srct in ((XGB, xbfm), (XGC, xcfm), (XGA, xafm)):
                    g.inc(nc.sync.dma_start(
                        out=dstb[:, gbuf * 4096:(gbuf + 1) * 4096].rearrange(
                            "p (c t) -> p c t", c=4),
                        in_=AP(srct.tensor, grp * 1024,
                               [[S, 128], [128 * S, 4], [1, 1024]])), cld, 16)
                nc.sync.wait_ge(cld.h, cld.n)
                g.bar()
                for nn in range(64):
                    n = grp * 64 + nn
                    cof = nn % 8
                    if cof == 0:
                        nc.sync.wait_ge(sq.h, sq.n)
                        g.inc(nc.sync.dma_start(out=RAW8,
                                                in_=ilrst[:, n * 16:n * 16 + 128]), cld, 16)
                        nc.scalar.wait_ge(cld.h, cld.n)
                        nc.scalar.wait_ge(mm.h, mm.n)
                        g.inc(nc.scalar.activation(SIG8, RAW8, AF.Sigmoid,
                                                   bias=CL[0:8, 4:5]), sq)
                        nc.tensor.wait_ge(sq.h, sq.n)
                        nc.tensor.transpose(pb[3][:, 2:10], SIG8, IDENT[0:8, 0:8])
                        g.inc(nc.tensor.drain(), mm)
                        nc.vector.wait_ge(mm.h, mm.n)
                        nc.vector.tensor_copy(TOKT, pb[3][:, 2:10])
                        g.inc(nc.vector.drain(), ev)
                        nc.tensor.wait_ge(ev.h, ev.n)
                    nc.vector.tensor_tensor(
                        COTMP,
                        AP(TOKT.tensor, TOKT.offset, [list(TOKT.ap[0]), [1, 8], [0, 16]]),
                        BM16, op=ALU.mult)
                    nc.vector.drain()
                    nc.vector.tensor_scalar(COTMP, COTMP, PHJ[:, cof:cof + 1], None,
                                            op0=ALU.mult)
                    g.inc(nc.vector.drain(), ev)
                    nc.tensor.wait_ge(ev.h, ev.n)
                    g.inc(nc.tensor.matmul(pb[3][:, 10:11], COTMP, ones[0:128, 0:1],
                                           start=True, stop=True,
                                           skip_group_check=True), mm)
                    nc.vector.wait_ge(mm.h, mm.n)
                    nc.vector.tensor_scalar(CL[:, 10:11] if False else CL[:, 12:13],
                                            pb[3][:, 10:11], CL[:, 0:1], None, op0=ALU.mult)
                    g.inc(nc.vector.drain(), ev)
                    co_col = CL[:, 12:13]

                    def xsl(bufT, c):
                        base = gbuf * 4096 + c * 1024 + nn * 16
                        return bufT[:, base:base + 16]

                    def xbc(bufT, c):
                        s_ = xsl(bufT, c)
                        return AP(s_.tensor, s_.offset, [list(s_.ap[0]), [0, 8], [1, 16]])

                    for c in range(4):
                        nc.vector.tensor_tensor(DGB[:, c * 128:(c + 1) * 128],
                                                xbc(XGB, c), CN[:, c * 128:(c + 1) * 128],
                                                op=ALU.mult)
                        nc.vector.drain()
                    g.inc(nc.vector.drain(), ev)
                    nc.tensor.wait_ge(ev.h, ev.n)
                    for c in range(4):
                        dg = DGB[:, c * 128:(c + 1) * 128]
                        nc.tensor.matmul(pb[0][:, 0:64], dg, W1ST[:, c * 64:(c + 1) * 64],
                                         start=(c == 0), stop=False, skip_group_check=True)
                        nc.tensor.matmul(pb[4][:, 0:16], dg, xsl(XGC, c),
                                         start=(c == 0), stop=(c == 3),
                                         skip_group_check=True)
                        nc.tensor.matmul(pb[1][:, 0:64], dg, I2,
                                         start=(c == 0), stop=(c == 3),
                                         skip_group_check=True)
                    nc.tensor.matmul(pb[0][:, 0:64], HSEL2, B1ST, start=False, stop=True,
                                     skip_group_check=True)
                    g.inc(nc.tensor.drain(), mm)
                    nc.vector.wait_ge(mm.h, mm.n)
                    for c in range(4):
                        nc.vector.tensor_tensor(DGB[:, c * 128:(c + 1) * 128],
                                                xbc(XGC, c), CN[:, c * 128:(c + 1) * 128],
                                                op=ALU.mult)
                        nc.vector.drain()
                    g.inc(nc.vector.drain(), ev)
                    nc.tensor.wait_ge(ev.h, ev.n)
                    for c in range(4):
                        dg = DGB[:, c * 128:(c + 1) * 128]
                        nc.tensor.matmul(pb[1][:, 64:128], dg, I2,
                                         start=(c == 0), stop=(c == 3),
                                         skip_group_check=True)
                        nc.tensor.matmul(pb[0][:, 128:192], dg,
                                         W1ST[:, c * 64:(c + 1) * 64],
                                         start=(c == 0), stop=False,
                                         skip_group_check=True)
                    nc.tensor.matmul(pb[0][:, 128:192], HSEL2, B1ST, start=False,
                                     stop=True, skip_group_check=True)
                    g.inc(nc.tensor.drain(), mm)
                    nc.vector.wait_ge(mm.h, mm.n)
                    for c in range(4):
                        nc.vector.tensor_tensor(DGB[:, c * 128:(c + 1) * 128],
                                                xbc(XGA, c), CN[:, c * 128:(c + 1) * 128],
                                                op=ALU.mult)
                        nc.vector.drain()
                    g.inc(nc.vector.drain(), ev)
                    nc.tensor.wait_ge(ev.h, ev.n)
                    for c in range(4):
                        nc.tensor.matmul(pb[1][:, 128:192], DGB[:, c * 128:(c + 1) * 128],
                                         I2, start=(c == 0), stop=(c == 3),
                                         skip_group_check=True)
                    g.inc(nc.tensor.drain(), mm)
                    nc.vector.wait_ge(mm.h, mm.n)
                    nc.vector.tensor_copy(XBT, pb[1][:, 0:64])
                    nc.vector.drain()
                    nc.vector.tensor_copy(XCT, pb[1][:, 64:128])
                    nc.vector.drain()
                    nc.vector.tensor_tensor(TGT, pb[1][:, 128:192], XBT, op=ALU.subtract)
                    nc.vector.drain()
                    nc.vector.tensor_reduce(CL[:, 6:7], pb[0][:, 0:64], axis=AXL.X,
                                            op=ALU.add)
                    nc.vector.drain()
                    nc.vector.tensor_scalar_mul(CL[:, 6:7], CL[:, 6:7], 1.0 / 64.0)
                    nc.vector.drain()
                    nc.vector.tensor_scalar(ZC, pb[0][:, 0:64], CL[:, 6:7], None,
                                            op0=ALU.subtract)
                    g.inc(nc.vector.drain(), ev)
                    nc.scalar.wait_ge(ev.h, ev.n)
                    g.inc(nc.scalar.activation(XH, ZC, AF.Square, accum_out=CL[:, 11:12]),
                          sq)
                    nc.vector.wait_ge(sq.h, sq.n)
                    nc.vector.tensor_scalar_mul(CL[:, 11:12], CL[:, 11:12], 1.0 / 64.0)
                    g.inc(nc.vector.drain(), ev)
                    nc.scalar.wait_ge(ev.h, ev.n)
                    g.inc(nc.scalar.activation(CL[:, 7:8], CL[:, 11:12], AF.Sqrt,
                                               bias=CL[:, 5:6]), sq)
                    nc.vector.wait_ge(sq.h, sq.n)
                    nc.vector.reciprocal(CL[:, 7:8], CL[:, 7:8])
                    nc.vector.drain()
                    nc.vector.tensor_scalar(XH, ZC, CL[:, 7:8], None, op0=ALU.mult)
                    nc.vector.drain()
                    nc.vector.tensor_tensor(GG, XH, LNW, op=ALU.mult)
                    nc.vector.drain()
                    nc.vector.tensor_tensor(GG, GG, LNB, op=ALU.add)
                    nc.vector.drain()
                    nc.vector.tensor_tensor(GG, GG, TGT, op=ALU.subtract)
                    nc.vector.drain()
                    nc.vector.tensor_tensor(GG, GG, LNW, op=ALU.mult)
                    nc.vector.drain()
                    nc.vector.tensor_scalar(GRD, GG, 1.0, 0.0, op0=ALU.mult,
                                            op1=ALU.add, accum_out=CL[:, 8:9])
                    nc.vector.drain()
                    nc.vector.tensor_tensor(GRDC, GG, XH, op=ALU.mult)
                    nc.vector.drain()
                    nc.vector.tensor_scalar(GRDC, GRDC, 1.0, 0.0, op0=ALU.mult,
                                            op1=ALU.add, accum_out=CL[:, 9:10])
                    nc.vector.drain()
                    nc.vector.tensor_tensor(CL[:, 11:12], CL[:, 8:9], CL[:, 7:8],
                                            op=ALU.mult)
                    nc.vector.drain()
                    nc.vector.tensor_scalar_mul(CL[:, 8:9], CL[:, 11:12], 1.0 / 64.0)
                    nc.vector.drain()
                    nc.vector.tensor_tensor(CL[:, 11:12], CL[:, 9:10], CL[:, 7:8],
                                            op=ALU.mult)
                    nc.vector.drain()
                    nc.vector.tensor_scalar_mul(CL[:, 9:10], CL[:, 11:12], 1.0 / 64.0)
                    nc.vector.drain()
                    nc.vector.tensor_scalar(GRD, GG, CL[:, 7:8], None, op0=ALU.mult)
                    nc.vector.drain()
                    nc.vector.tensor_scalar(GRDC, XH, CL[:, 9:10], None, op0=ALU.mult)
                    nc.vector.drain()
                    nc.vector.tensor_tensor(GRD, GRD, GRDC, op=ALU.subtract)
                    nc.vector.drain()
                    nc.vector.tensor_scalar(GRD, GRD, CL[:, 8:9], None, op0=ALU.subtract)
                    nc.vector.drain()
                    nc.vector.tensor_tensor(
                        DGB[:, 0:128],
                        AP(pb[4].ap().tensor, pb[4].ap().offset,
                           [list(pb[4].ap().ap[0]), [0, 8], [1, 16]]),
                        TRILM, op=ALU.mult)
                    g.inc(nc.vector.drain(), ev)
                    nc.tensor.wait_ge(ev.h, ev.n)
                    nc.tensor.matmul(pb[0][:, 192:256], DGB[:, 0:128], GRD,
                                     start=True, stop=False, skip_group_check=True)
                    nc.tensor.matmul(pb[0][:, 192:256], TRILM, GRD,
                                     start=False, stop=True, skip_group_check=True)
                    g.inc(nc.tensor.drain(), mm)
                    nc.vector.wait_ge(mm.h, mm.n)
                    nc.vector.tensor_scalar(XH, pb[0][:, 192:256], co_col, None,
                                            op0=ALU.mult)
                    nc.vector.drain()
                    nc.vector.tensor_tensor(ZC, pb[0][:, 128:192], XH, op=ALU.subtract)
                    nc.vector.drain()
                    nc.vector.tensor_reduce(CL[:, 6:7], ZC, axis=AXL.X, op=ALU.add)
                    nc.vector.drain()
                    nc.vector.tensor_scalar_mul(CL[:, 6:7], CL[:, 6:7], 1.0 / 64.0)
                    nc.vector.drain()
                    nc.vector.tensor_scalar(ZC, ZC, CL[:, 6:7], None, op0=ALU.subtract)
                    g.inc(nc.vector.drain(), ev)
                    nc.scalar.wait_ge(ev.h, ev.n)
                    g.inc(nc.scalar.activation(XH, ZC, AF.Square, accum_out=CL[:, 11:12]),
                          sq)
                    nc.vector.wait_ge(sq.h, sq.n)
                    nc.vector.tensor_scalar_mul(CL[:, 11:12], CL[:, 11:12], 1.0 / 64.0)
                    g.inc(nc.vector.drain(), ev)
                    nc.scalar.wait_ge(ev.h, ev.n)
                    g.inc(nc.scalar.activation(CL[:, 7:8], CL[:, 11:12], AF.Sqrt,
                                               bias=CL[:, 5:6]), sq)
                    nc.vector.wait_ge(sq.h, sq.n)
                    nc.vector.reciprocal(CL[:, 7:8], CL[:, 7:8])
                    nc.vector.drain()
                    nc.vector.tensor_scalar(XH, ZC, CL[:, 7:8], None, op0=ALU.mult)
                    nc.vector.drain()
                    nc.vector.tensor_tensor(OUTT, XH, LNW, op=ALU.mult)
                    nc.vector.drain()
                    nc.vector.tensor_tensor(OUTT, OUTT, LNB, op=ALU.add)
                    nc.vector.drain()
                    nc.vector.tensor_tensor(OUTT, OUTT, XCT, op=ALU.add)
                    nc.vector.drain()
                    for c in range(4):
                        nc.vector.tensor_tensor(
                            DGB[:, c * 128:(c + 1) * 128],
                            AP(OUTT.tensor, OUTT.offset,
                               [list(OUTT.ap[0]), [0, 2], [1, 64]]),
                            CN[:, (4 + c) * 128:(5 + c) * 128], op=ALU.mult)
                        nc.vector.drain()
                    g.inc(nc.vector.drain(), ev)
                    nc.tensor.wait_ge(ev.h, ev.n)
                    for c in range(4):
                        nc.tensor.matmul(pb[1][:, 192 + 16 * c:208 + 16 * c],
                                         DGB[:, c * 128:(c + 1) * 128], TSEL,
                                         start=True, stop=True, skip_group_check=True)
                    nc.tensor.matmul(pb[3][0:8, 0:1], SEL15, co_col, start=True,
                                     stop=True, skip_group_check=True)
                    g.inc(nc.tensor.drain(), mm)
                    nc.vector.wait_ge(mm.h, mm.n)
                    for c in range(4):
                        nc.vector.tensor_copy(
                            XCW[:, c * 1024 + nn * 16: c * 1024 + nn * 16 + 16],
                            pb[1][:, 192 + 16 * c:208 + 16 * c])
                        nc.vector.drain()
                    nc.vector.tensor_copy(CL[0:8, 20:21], pb[3][0:8, 0:1])
                    g.inc(nc.vector.drain(), ev)
                    nc.tensor.wait_ge(ev.h, ev.n)
                    nc.tensor.matmul(pb[3][:, 1:2], HSEL2, CL[0:8, 20:21],
                                     start=True, stop=True, skip_group_check=True)
                    g.inc(nc.tensor.drain(), mm)
                    nc.vector.wait_ge(mm.h, mm.n)
                    nc.vector.tensor_copy(CL[:, 10:11], pb[3][:, 1:2])
                    nc.vector.drain()
                    nc.vector.tensor_scalar(GRDC, GRD, CL[:, 10:11], None, op0=ALU.mult)
                    nc.vector.drain()
                    for c in range(4):
                        nc.vector.tensor_tensor(
                            DGB[:, c * 128:(c + 1) * 128],
                            AP(XBT.tensor, XBT.offset,
                               [list(XBT.ap[0]), [0, 2], [1, 64]]),
                            CN[:, (4 + c) * 128:(5 + c) * 128], op=ALU.mult)
                        nc.vector.drain()
                    g.inc(nc.vector.drain(), ev)
                    nc.tensor.wait_ge(ev.h, ev.n)
                    for c in range(4):
                        nc.tensor.matmul(pb[2][:, c * 64:(c + 1) * 64],
                                         DGB[:, c * 128:(c + 1) * 128], GRDC,
                                         start=True, stop=True, skip_group_check=True)
                    nc.tensor.matmul(pb[3][0:8, 72:136], HSELT, GRDC,
                                     start=True, stop=True, skip_group_check=True)
                    g.inc(nc.tensor.drain(), mm)
                    nc.vector.wait_ge(mm.h, mm.n)
                    for c in range(4):
                        nc.vector.tensor_tensor(W1ST[:, c * 64:(c + 1) * 64],
                                                W1ST[:, c * 64:(c + 1) * 64],
                                                pb[2][:, c * 64:(c + 1) * 64],
                                                op=ALU.subtract)
                        nc.vector.drain()
                    nc.vector.tensor_tensor(B1ST, B1ST, pb[3][0:8, 72:136], op=ALU.subtract)
                    g.inc(nc.vector.drain(), ev)
                    nc.tensor.wait_ge(ev.h, ev.n)
                    if DEBUG and n == 8:
                        g.bar()
                        dsb = g.sem("a1sbd")
                        g.inc(nc.sync.dma_start(out=sbdump[:, 0:16 * 64], in_=SB[:]), dsb, 16)
                        g.inc(nc.sync.dma_start(out=sbdump[:, 16 * 64:], in_=CL[:]), dsb, 16)
                        g.inc(nc.sync.dma_start(out=cndump[:, :],
                                                in_=CN[:, 16 * 128:20 * 128]), dsb, 16)
                        nc.sync.wait_ge(dsb.h, dsb.n)
                        g.bar()
                nc.sync.wait_ge(ev.h, ev.n)
                g.inc(nc.sync.dma_start(
                    out=AP(cc_s_in.tensor, grp * 1024,
                           [[S, 128], [128 * S, 4], [1, 1024]]),
                    in_=XCW[:].rearrange("p (c t) -> p c t", c=4)), st, 16)
                nc.sync.wait_ge(st.h, st.n)
            g.bar()

        if DEBUG:
            dsem = g.sem("a1dbg")
            g.inc(nc.sync.dma_start(out=scandump[:, :], in_=cc_s_in[:, :]), dsem, 16)
            nc.sync.wait_ge(dsem.h, dsem.n)
            g.bar()
        ccs2 = g.sem("a1cc2")
        g.inc(nc.gpsimd.collective_compute(
            "AllGather", ALU.bypass, replica_groups=RG,
            ins=[cc_s_in[:, :]], outs=[cc_s_out[:, :]]), ccs2)
        with nc.sbuf_tensor("OST_5", [128, KC * TOK], F32) as OST, \
             nc.sbuf_tensor("QSL_5", [128, 4], F32) as QSL, \
             nc.sbuf_tensor("OC4_5", [128, 4 * 1024], F32) as OC4:
            ld4 = g.sem("a1ld4")
            ev4 = g.sem("a1ev4")
            g.inc(nc.sync.dma_start(out=QSL[:, :], in_=i["qselcol"][:, :]), ld4, 16)
            nc.sync.wait_ge(ccs2.h, ccs2.n)
            for c in range(KC):
                if c >= 1:
                    nc.sync.wait_ge(ev4.h, ev4.n)
                for k in range(4):
                    g.inc(nc.sync.dma_start(
                        out=OC4[:, k * 1024:(k + 1) * 1024],
                        in_=cc_s_out[c * 128:(c + 1) * 128, k * 1024:(k + 1) * 1024]),
                        ld4, 16)
                nc.vector.wait_ge(ld4.h, ld4.n)
                nc.vector.tensor_scalar(OST[:, c * TOK:(c + 1) * TOK],
                                        OC4[:, 0:1024], QSL[:, 0:1], None, op0=ALU.mult)
                nc.vector.drain()
                for k in range(1, 4):
                    nc.vector.scalar_tensor_tensor(OST[:, c * TOK:(c + 1) * TOK],
                                                   OC4[:, k * 1024:(k + 1) * 1024],
                                                   QSL[:, k:k + 1],
                                                   OST[:, c * TOK:(c + 1) * TOK],
                                                   op0=ALU.mult, op1=ALU.add)
                    nc.vector.drain()
                g.inc(nc.vector.drain(), ev4)
            g.bar()

            def evac_res5(ob, tb, ps, mm_, ev_, j):
                nc.vector.wait_ge(mm_.h, mm_.n)
                sl = slice(ob * TOK + tb * 512, ob * TOK + tb * 512 + 512)
                nc.vector.tensor_tensor(X[:, sl], X[:, sl], ps, op=ALU.add)
                g.inc(nc.vector.drain(), ev_)

            proj(i["a1_wo"], 16,
                 lambda c, tb: OST[:, c * TOK + tb * 512: c * TOK + tb * 512 + 512],
                 2, evac_res5)


    def s_l1_ffn(first_call):
        FF = 8192
        FOB = FF // 128
        if first_call:
            for (wsrc, wdst, nel, nblk) in ((i["f1_w1t"], w1q, 2048 * FF, 64),
                                            (i["f1_w2t"], w2q, FF * 2048, 64)):
                ld = g.sem("qld")
                ev = g.sem("qev")
                st = g.sem("qst")
                mmq = g.sem("qmm")
                blkcols = wsrc.shape[1] // nblk
                with nc.sbuf_tensor(f"QW{g.uid()}", [128, 2 * blkcols], F32) as QW, \
                     nc.sbuf_tensor(f"QA{g.uid()}", [128, 4], F32) as QA, \
                     nc.sbuf_tensor(f"QT{g.uid()}", [128, blkcols], F32) as QT, \
                     nc.sbuf_tensor(f"QB{g.uid()}", [128, 2 * blkcols], BF16) as QB:
                    nc.vector.memset(QA[:, 0:1], 0.0)
                    g.inc(nc.vector.drain(), ev)
                    for bk in range(nblk):
                        buf = QW[:, (bk % 2) * blkcols:(bk % 2) * blkcols + blkcols]
                        if bk >= 2:
                            nc.sync.wait_ge(ev.h, ev.off + bk)
                        g.inc(nc.sync.dma_start(
                            out=buf, in_=wsrc[:, bk * blkcols:(bk + 1) * blkcols]), ld, 16)
                        nc.vector.wait_ge(ld.h, ld.off + 16 * (bk + 1))
                        nc.vector.tensor_reduce(QA[:, 1:2], buf, axis=AXL.X, op=ALU.add,
                                                apply_absolute_value=True)
                        nc.vector.drain()
                        nc.vector.tensor_tensor(QA[:, 0:1], QA[:, 0:1], QA[:, 1:2],
                                                op=ALU.add)
                        g.inc(nc.vector.drain(), ev)
                    nc.tensor.wait_ge(ev.h, ev.n)
                    g.inc(nc.tensor.matmul(pst[0][0:1, 0:1], ones, QA[:, 0:1],
                                           start=True, stop=True), mmq)
                    nc.vector.wait_ge(mmq.h, mmq.n)
                    nc.vector.tensor_scalar_mul(SC[0:1, 16:17], pst[0][0:1, 0:1],
                                                0.7 / float(nel))
                    nc.vector.drain()
                    nc.vector.tensor_scalar_mul(SC[0:1, 17:18], SC[0:1, 16:17], -1.0)
                    nc.vector.drain()
                    scalar_bcast(SC[0:1, 16:17], QA[:, 2:3], 128)
                    scalar_bcast(SC[0:1, 17:18], QA[:, 3:4], 128)
                    g.bar()
                    for bk in range(nblk):
                        buf = QW[:, (bk % 2) * blkcols:(bk % 2) * blkcols + blkcols]
                        obuf = QB[:, (bk % 2) * blkcols:(bk % 2) * blkcols + blkcols]
                        if bk >= 2:
                            nc.sync.wait_ge(ev.h, ev.n - 1)
                        g.inc(nc.sync.dma_start(
                            out=buf, in_=wsrc[:, bk * blkcols:(bk + 1) * blkcols]), ld, 16)
                        nc.vector.wait_ge(ld.h, ld.off + 16 * (nblk + bk + 1))
                        if bk >= 2:
                            nc.vector.wait_ge(st.h, st.n - 16)
                        nc.vector.tensor_scalar(QT[:, :], buf, QA[:, 2:3], None,
                                                op0=ALU.is_ge)
                        nc.vector.drain()
                        nc.vector.tensor_scalar(buf, buf, QA[:, 3:4], None, op0=ALU.is_le)
                        nc.vector.drain()
                        nc.vector.tensor_tensor(obuf, QT[:, :], buf, op=ALU.subtract)
                        g.inc(nc.vector.drain(), ev)
                        nc.sync.wait_ge(ev.h, ev.n)
                        g.inc(nc.sync.dma_start(
                            out=wdst[:, bk * blkcols:(bk + 1) * blkcols], in_=obuf), st, 16)
                    nc.sync.wait_ge(st.h, st.n)
                g.bar()

        with nc.sbuf_tensor(f"LSTf{g.uid()}", [1, 2 * TOK], F32) as LST, \
             nc.sbuf_tensor(f"B12{g.uid()}", [128, 160], F32) as BB, \
             nc.sbuf_tensor(f"XBH{g.uid()}", [128, KC * 512], BF16) as XBH, \
             nc.sbuf_tensor(f"XBL{g.uid()}", [128, KC * 512], BF16) as XBL, \
             nc.sbuf_tensor(f"HQH{g.uid()}", [128, 16 * 512], BF16) as HQH, \
             nc.sbuf_tensor(f"HQL{g.uid()}", [128, 16 * 512], BF16) as HQL, \
             nc.sbuf_tensor(f"YAC{g.uid()}", [128, KC * 512], F32) as YAC, \
             nc.sbuf_tensor(f"GEL{g.uid()}", [128, 3 * 512], F32) as GEL, \
             nc.sbuf_tensor(f"WBF{g.uid()}", [128, 2 * KC * 128], BF16) as WB:
            ldc = g.sem("fldc")
            g.inc(nc.sync.dma_start(out=BB[:, 0:64], in_=i["f1_b1c"][:, :]), ldc, 16)
            g.inc(nc.sync.dma_start(out=BB[:, 64:80], in_=i["f1_b2c"][:, :]), ldc, 16)
            ln_stats(lambda c, tb: X[:, c * TOK + tb * 512: c * TOK + tb * 512 + 512],
                     TOK, LST)
            nc.vector.wait_ge(ldc.h, ldc.n)
            nc.vector.tensor_scalar_mul(BB[:, 80:144], BB[:, 0:64], 0.7071067811865476)
            nc.vector.drain()
            g.bar()
            mm = g.sem("fm6")
            ev = g.sem("fe6")
            sq = g.sem("fs6")
            wl = g.sem("fw6")
            for tb in range(2):
                lam = g.sem("flam")
                mcol = LST[0:1, tb * 512:tb * 512 + 512]
                rcol = LST[0:1, TOK + tb * 512:TOK + tb * 512 + 512]
                nc.tensor.wait_ge(ev.h, ev.n)
                nc.tensor.matmul(pb[4][:, :], ONESR[0:1, :], mcol, start=True, stop=True)
                g.inc(nc.tensor.matmul(pb[5][:, :], ONESR[0:1, :], rcol,
                                       start=True, stop=True), lam)
                nc.vector.wait_ge(lam.h, lam.n)
                for c in range(KC):
                    ssl = slice(c * TOK + tb * 512, c * TOK + tb * 512 + 512)
                    dsl = slice(c * 512, c * 512 + 512)
                    xnt = GEL[:, 0:512]
                    nc.vector.tensor_tensor(xnt, X[:, ssl], pb[4][:, :], op=ALU.subtract)
                    nc.vector.drain()
                    nc.vector.tensor_tensor(xnt, xnt, pb[5][:, :], op=ALU.mult)
                    nc.vector.drain()
                    nc.vector.tensor_copy(XBH[:, dsl], xnt)
                    nc.vector.drain()
                    nc.vector.tensor_tensor(XBL[:, dsl], xnt, XBH[:, dsl],
                                            op=ALU.subtract)
                    nc.vector.drain()
                nc.vector.memset(YAC[:, :], 0.0)
                g.inc(nc.vector.drain(), lam)
                nc.tensor.wait_ge(lam.h, lam.n)
                g.bar()
                for fq in range(4):
                    j = 0
                    for fb in range(16):
                        gfb = fq * 16 + fb
                        buf = WB[:, (fb % 2) * KC * 128:(fb % 2) * KC * 128 + KC * 128]
                        if fb >= 2:
                            nc.sync.wait_ge(mm.h, mm.n - 1)
                        g.inc(nc.sync.dma_start(
                            out=buf, in_=w1q[:, gfb * KC * 128:(gfb + 1) * KC * 128]),
                            wl, 16)
                        nc.tensor.wait_ge(wl.h, wl.n)
                        psb = pb[j % 2]
                        if j >= 2:
                            nc.tensor.wait_ge(sq.h, sq.n - 2)
                        for c in range(KC):
                            wslc = buf[:, c * 128:(c + 1) * 128]
                            nc.tensor.matmul(psb[:, :], wslc,
                                             XBH[:, c * 512:(c + 1) * 512],
                                             start=(c == 0), stop=False,
                                             skip_group_check=True)
                            nc.tensor.matmul(psb[:, :], wslc,
                                             XBL[:, c * 512:(c + 1) * 512],
                                             start=False, stop=(c == KC - 1),
                                             skip_group_check=True)
                        g.inc(nc.tensor.drain(), mm)
                        nc.scalar.wait_ge(mm.h, mm.n)
                        g.inc(nc.scalar.activation(GEL[:, 0:512], psb[:, :], AF.Erf,
                                                   bias=BB[:, 80 + gfb:81 + gfb],
                                                   scale=0.7071067811865476), sq)
                        g.inc(nc.scalar.activation(GEL[:, 512:1024], psb[:, :], AF.Identity,
                                                   bias=BB[:, gfb:gfb + 1]), sq)
                        nc.vector.wait_ge(sq.h, sq.n)
                        nc.vector.tensor_tensor(GEL[:, 1024:1536],
                                                GEL[:, 0:512], GEL[:, 512:1024],
                                                op=ALU.mult)
                        nc.vector.drain()
                        nc.vector.tensor_tensor(GEL[:, 1024:1536],
                                                GEL[:, 1024:1536],
                                                GEL[:, 512:1024], op=ALU.add)
                        nc.vector.drain()
                        nc.vector.tensor_copy(HQH[:, fb * 512:(fb + 1) * 512],
                                              GEL[:, 1024:1536])
                        nc.vector.drain()
                        nc.vector.tensor_tensor(HQL[:, fb * 512:(fb + 1) * 512],
                                                GEL[:, 1024:1536],
                                                HQH[:, fb * 512:(fb + 1) * 512],
                                                op=ALU.subtract)
                        g.inc(nc.vector.drain(), ev)
                        j += 1
                    g.bar()
                    j = 0
                    for ob in range(16):
                        buf = WB[:, (ob % 2) * KC * 128:(ob % 2) * KC * 128 + KC * 128]
                        if ob >= 2:
                            nc.sync.wait_ge(mm.h, mm.n - 1)
                        g.inc(nc.sync.dma_start(
                            out=buf,
                            in_=w2q[:, (ob * 64 + fq * 16) * 128:
                                    (ob * 64 + fq * 16 + 16) * 128]), wl, 16)
                        nc.tensor.wait_ge(wl.h, wl.n)
                        psb = pb[2 + j % 2]
                        if j >= 2:
                            nc.tensor.wait_ge(ev.h, ev.n - 1)
                        for c in range(16):
                            wslc = buf[:, c * 128:(c + 1) * 128]
                            nc.tensor.matmul(psb[:, :], wslc,
                                             HQH[:, c * 512:(c + 1) * 512],
                                             start=(c == 0), stop=False,
                                             skip_group_check=True)
                            nc.tensor.matmul(psb[:, :], wslc,
                                             HQL[:, c * 512:(c + 1) * 512],
                                             start=False, stop=(c == 15),
                                             skip_group_check=True)
                        g.inc(nc.tensor.drain(), mm)
                        nc.vector.wait_ge(mm.h, mm.n)
                        nc.vector.tensor_tensor(YAC[:, ob * 512:(ob + 1) * 512],
                                                YAC[:, ob * 512:(ob + 1) * 512],
                                                psb[:, :], op=ALU.add)
                        g.inc(nc.vector.drain(), ev)
                        j += 1
                    g.bar()
                for c in range(KC):
                    sl = slice(c * TOK + tb * 512, c * TOK + tb * 512 + 512)
                    nc.vector.scalar_tensor_tensor(GEL[:, 0:512], YAC[:, c * 512:(c + 1) * 512],
                                                   0.5, X[:, sl], op0=ALU.mult, op1=ALU.add)
                    nc.vector.drain()
                    nc.vector.tensor_scalar(X[:, sl], GEL[:, 0:512], BB[:, 64 + c:65 + c],
                                            None, op0=ALU.add)
                    nc.vector.drain()
                g.bar()

    def s7_l1_attn_b():
        ccs = g.sem("b7cc")
        with nc.sbuf_tensor("XN_7", [128, KC * TOK], F32) as XN7, \
             nc.sbuf_tensor("LST_7", [1, 2 * TOK], F32) as LST:
            ln_stats(lambda c, tb: X[:, c * TOK + tb * 512: c * TOK + tb * 512 + 512],
                     TOK, LST)
            ln_apply(lambda c, tb: X[:, c * TOK + tb * 512: c * TOK + tb * 512 + 512],
                     lambda c, tb: XN7[:, c * TOK + tb * 512: c * TOK + tb * 512 + 512],
                     TOK, LST)
            st = g.sem("b7st")
            g.inc(nc.sync.dma_start(
                out=AP(cc_xn_in.tensor, 0, [[TOK, 128], [128 * TOK, KC], [1, TOK]]),
                in_=XN7[:].rearrange("p (c t) -> p c t", c=KC)), st, 16)
            nc.gpsimd.wait_ge(st.h, st.n)
            g.inc(nc.gpsimd.collective_compute(
                "AllGather", ALU.bypass, replica_groups=RG,
                ins=[cc_xn_in[:, :]], outs=[cc_xn_out[:, :]]), ccs)
        g.bar()

        with nc.sbuf_tensor("QH_7", [128, S], F32) as QH, \
             nc.sbuf_tensor("KH_7", [128, S], F32) as KH, \
             nc.sbuf_tensor("VH_7", [128, 32 * 130], F32) as VH, \
             nc.sbuf_tensor("WQKV7", [128, 3 * KC * 128], F32) as WQKV, \
             nc.sbuf_tensor("XG7", [128, KC * 512], F32) as XG:
            PT = EV
            for kt in range(32):
                nc.vector.memset(VH[:, kt * 130 + 64:kt * 130 + 65], 1.0)
                nc.vector.memset(VH[:, kt * 130 + 129:kt * 130 + 130], 1.0)
            nc.vector.drain()
            hl = g.sem("b7hl")
            mm = g.sem("b7mm")
            mo = g.sem("b7mo")
            sq = g.sem("b7sq")
            ev = g.sem("b7ev")
            st = g.sem("b7st2")
            xl = g.sem("b7xl")
            nc.sync.wait_ge(ccs.h, ccs.n)
            g.bar()
            for h in range(4):
                for wi, nm in ((0, "b1_wq"), (1, "b1_wk"), (2, "b1_wv")):
                    g.inc(nc.sync.dma_start(
                        out=WQKV[:, wi * KC * 128:(wi + 1) * KC * 128],
                        in_=i[nm][:, h * KC * 128:(h + 1) * KC * 128]), hl, 16)
                nc.tensor.wait_ge(hl.h, hl.n)
                for gt in range(8):
                    if True:
                        g.inc(nc.sync.dma_start(
                            out=XG[:].rearrange("p (c t) -> p c t", c=KC),
                            in_=AP(cc_xn_out.tensor, (gt // 2) * D * TOK + (gt % 2) * 512,
                                   [[TOK, 128], [128 * TOK, KC], [1, 512]])), xl, 16)
                        nc.tensor.wait_ge(xl.h, xl.n)
                    for wi, dstb in ((0, QH), (1, KH)):
                        psb = pb[wi]
                        for c in range(KC):
                            nc.tensor.matmul(
                                psb[:, :],
                                WQKV[:, wi * KC * 128 + c * 128:wi * KC * 128 + c * 128 + 128],
                                XG[:, c * 512:(c + 1) * 512],
                                start=(c == 0), stop=(c == KC - 1))
                        g.inc(nc.tensor.drain(), mm)
                        nc.vector.wait_ge(mm.h, mm.n)
                        if wi == 0:
                            nc.vector.tensor_scalar_mul(dstb[:, gt * 512:(gt + 1) * 512],
                                                        psb[:, :], 0.08838834764831845)
                        else:
                            nc.vector.tensor_copy(dstb[:, gt * 512:(gt + 1) * 512], psb[:, :])
                        g.inc(nc.vector.drain(), ev)
                    for tt in range(4):
                        gtt = gt * 4 + tt
                        psb = pb[2 + tt % 2]
                        for c in range(KC):
                            nc.tensor.matmul(
                                psb[:, 0:128],
                                XG[:, c * 512 + tt * 128: c * 512 + tt * 128 + 128],
                                WQKV[:, 2 * KC * 128 + c * 128:2 * KC * 128 + c * 128 + 128],
                                start=(c == 0), stop=(c == KC - 1))
                        g.inc(nc.tensor.drain(), mm)
                        nc.vector.wait_ge(mm.h, mm.n)
                        nc.vector.tensor_copy(VH[:, gtt * 130:gtt * 130 + 64], psb[:, 0:64])
                        nc.vector.drain()
                        nc.vector.tensor_copy(VH[:, gtt * 130 + 65:gtt * 130 + 129],
                                              psb[:, 64:128])
                        g.inc(nc.vector.drain(), ev)
                    nc.tensor.wait_ge(ev.h, ev.n)
                g.bar()
                kn = g.sem(f"b7kn{h}")
                nc.vector.memset(SC[0:1, 10:11], 0.0)
                nc.vector.memset(SC[0:1, 13:14], 0.0)
                g.inc(nc.vector.drain(), kn)
                for src_, slot in ((KH, 10), (QH, 13)):
                    for tbb in range(8):
                        buf = SQ[:, (tbb % 2) * 512:(tbb % 2) * 512 + 512]
                        if tbb >= 2 or slot == 13:
                            nc.scalar.wait_ge(kn.h, kn.n - 1)
                        g.inc(nc.scalar.activation(buf, src_[:, tbb * 512:(tbb + 1) * 512],
                                                   AF.Square), sq)
                        nc.tensor.wait_ge(sq.h, sq.n)
                        nc.tensor.wait_ge(kn.h, kn.n)
                        g.inc(nc.tensor.matmul(pst[0][0:1, :], ones, buf,
                                               start=True, stop=True), mm)
                        nc.vector.wait_ge(mm.h, mm.n)
                        nc.vector.tensor_reduce(SC[0:1, 11:12], pst[0][0:1, :], axis=AXL.X,
                                                op=ALU.max)
                        nc.vector.drain()
                        nc.vector.tensor_tensor(SC[0:1, slot:slot + 1],
                                                SC[0:1, slot:slot + 1], SC[0:1, 11:12],
                                                op=ALU.max)
                        g.inc(nc.vector.drain(), kn)
                nc.vector.tensor_tensor(SC[0:1, 12:13], SC[0:1, 10:11], SC[0:1, 13:14],
                                        op=ALU.mult)
                g.inc(nc.vector.drain(), ev)
                nc.scalar.wait_ge(ev.h, ev.n)
                g.inc(nc.scalar.activation(SC[0:1, 12:13], SC[0:1, 12:13], AF.Sqrt), sq)
                nc.vector.wait_ge(sq.h, sq.n)
                nc.vector.tensor_scalar_mul(SC[0:1, 12:13], SC[0:1, 12:13], -1.0)
                nc.vector.drain()
                scalar_bcast(SC[0:1, 12:13], CREP[:, 0:1], 128)
                g.bar()
                for qb in range(8):
                    for kt in range(32):
                        psb = pb[kt % 2]
                        if kt >= 2:
                            nc.tensor.wait_ge(sq.h, sq.n - 1)
                        nc.tensor.matmul(psb[:, :], KH[:, kt * 128:(kt + 1) * 128],
                                         QH[:, qb * 512:(qb + 1) * 512],
                                         start=True, stop=True)
                        g.inc(nc.tensor.drain(), mm)
                        ptile = PT[:, (kt % 2) * 512:(kt % 2) * 512 + 512]
                        nc.scalar.wait_ge(mm.h, mm.n)
                        if kt >= 2:
                            nc.scalar.wait_ge(mo.h, mo.n - 2)
                        g.inc(nc.scalar.activation(ptile, psb[:, :], AF.Exp,
                                                   bias=CREP[:, 0:1]), sq)
                        nc.tensor.wait_ge(sq.h, sq.n)
                        g.inc(nc.tensor.matmul(pb[3][0:65, :],
                                               VH[:, kt * 130:kt * 130 + 65],
                                               ptile, start=(kt == 0), stop=(kt == 31),
                                               skip_group_check=True), mo)
                        g.inc(nc.tensor.matmul(pb[4][0:65, :],
                                               VH[:, kt * 130 + 65:kt * 130 + 130],
                                               ptile, start=(kt == 0), stop=(kt == 31),
                                               skip_group_check=True), mo)
                    nc.vector.wait_ge(mo.h, mo.n)
                    nc.vector.reciprocal(RECIP[0:1, :], pb[3][64:65, :])
                    g.inc(nc.vector.drain(), ev)
                    nc.tensor.wait_ge(ev.h, ev.n)
                    g.inc(nc.tensor.matmul(pb[5][0:64, :], ONESR[0:1, 0:64],
                                           RECIP[0:1, :], start=True, stop=True), mm)
                    nc.vector.wait_ge(mm.h, mm.n)
                    nc.vector.tensor_copy(SQ[64:128, 0:512], pb[5][0:64, :])
                    nc.vector.drain()
                    obuf = EV[:, 0:512]
                    nc.vector.tensor_tensor(obuf[0:64, :], pb[3][0:64, :],
                                            SQ[64:128, 0:512], op=ALU.mult)
                    nc.vector.drain()
                    nc.vector.tensor_tensor(obuf[64:128, :], pb[4][0:64, :],
                                            SQ[64:128, 0:512], op=ALU.mult)
                    g.inc(nc.vector.drain(), ev)
                    nc.tensor.wait_ge(ev.h, ev.n)
                    nc.sync.wait_ge(ev.h, ev.n)
                    g.inc(nc.sync.dma_start(
                        out=cc_s_in[h * 128:(h + 1) * 128, qb * 512:(qb + 1) * 512],
                        in_=obuf), st, 16)
                    nc.scalar.wait_ge(st.h, st.n)
                g.bar()

        ccs2 = g.sem("b7cc2")
        g.inc(nc.gpsimd.collective_compute(
            "AllGather", ALU.bypass, replica_groups=RG,
            ins=[cc_s_in[:, :]], outs=[cc_s_out[:, :]]), ccs2)
        with nc.sbuf_tensor("OST_7", [128, KC * TOK], F32) as OST, \
             nc.sbuf_tensor("QSL_7", [128, 4], F32) as QSL, \
             nc.sbuf_tensor("OC4_7", [128, 4 * 1024], F32) as OC4:
            ld4 = g.sem("b7ld4")
            ev4 = g.sem("b7ev4")
            g.inc(nc.sync.dma_start(out=QSL[:, :], in_=i["qselcol"][:, :]), ld4, 16)
            nc.sync.wait_ge(ccs2.h, ccs2.n)
            for c in range(KC):
                if c >= 1:
                    nc.sync.wait_ge(ev4.h, ev4.n)
                for k in range(4):
                    g.inc(nc.sync.dma_start(
                        out=OC4[:, k * 1024:(k + 1) * 1024],
                        in_=cc_s_out[c * 128:(c + 1) * 128, k * 1024:(k + 1) * 1024]),
                        ld4, 16)
                nc.vector.wait_ge(ld4.h, ld4.n)
                nc.vector.tensor_scalar(OST[:, c * TOK:(c + 1) * TOK],
                                        OC4[:, 0:1024], QSL[:, 0:1], None, op0=ALU.mult)
                nc.vector.drain()
                for k in range(1, 4):
                    nc.vector.scalar_tensor_tensor(OST[:, c * TOK:(c + 1) * TOK],
                                                   OC4[:, k * 1024:(k + 1) * 1024],
                                                   QSL[:, k:k + 1],
                                                   OST[:, c * TOK:(c + 1) * TOK],
                                                   op0=ALU.mult, op1=ALU.add)
                    nc.vector.drain()
                g.inc(nc.vector.drain(), ev4)
            g.bar()

            def evac_res7(ob, tb, ps, mm_, ev_, j):
                nc.vector.wait_ge(mm_.h, mm_.n)
                sl = slice(ob * TOK + tb * 512, ob * TOK + tb * 512 + 512)
                nc.vector.tensor_tensor(X[:, sl], X[:, sl], ps, op=ALU.add)
                g.inc(nc.vector.drain(), ev_)

            proj(i["b1_wo"], 16,
                 lambda c, tb: OST[:, c * TOK + tb * 512: c * TOK + tb * 512 + 512],
                 2, evac_res7)

    if n_sub >= 1:
        s1_l0_attn_a()
    if n_sub >= 2:
        s_l0_ffn(True)
    if n_sub >= 3:
        s3_l0_attn_b()
    if n_sub >= 4:
        s_l0_ffn(False)
    if n_sub >= 5:
        s5_l1_attn_a()
    if n_sub >= 6:
        s_l1_ffn(True)
    if n_sub >= 7:
        s7_l1_attn_b()
    if n_sub >= 8:
        s_l1_ffn(False)
    if n_sub < 1:
        ld0 = g.sem("ld0")
        g.inc(nc.sync.dma_start(out=X[:], in_=xin[:, :]), ld0, 16)
        nc.sync.wait_ge(ld0.h, ld0.n)
        g.bar()

    fin = g.sem("fin")
    g.inc(nc.sync.dma_start(out=y[:, :], in_=X[:]), fin, 16)
    nc.sync.wait_ge(fin.h, fin.n)
    return nc



def kernel(**inputs):
    ins = {k: np.ascontiguousarray(np.asarray(v, np.float32)) for k, v in inputs.items()}
    x = ins["L0_attn_a_x"]

    dogp = np.zeros((32, 8), np.float32)
    dogp[:, 0:7] = ins["L0_attn_a_dog_params"]
    shared = {
        "l0a_wq": wprep(ins["L0_attn_a_Wq"]),
        "l0a_wk": wprep(ins["L0_attn_a_Wk"]),
        "l0a_wv": wprep(ins["L0_attn_a_Wv"]),
        "l0a_wo": wprep(ins["L0_attn_a_Wo"]),
        "l0a_dog": dogp,
        "l0a_sinks": ins["L0_attn_a_sinks"].reshape(1, 32),
    }
    if N_SUB >= 2:
        shared["l0f_wT"] = wprep(np.ascontiguousarray(ins["L0_ffn_weight"].T))
        shared["l0f_cols"] = np.concatenate(
            [colprep(ins["L0_ffn_gamma"]), colprep(ins["L0_ffn_beta"]),
             colprep(ins["L0_ffn_ln_weight"]), colprep(ins["L0_ffn_ln_bias"])], axis=1)
    if N_SUB >= 3:
        shared["l0b_wqd"] = wprep(ins["L0_attn_b_wq_d"])
        shared["l0b_wkvd"] = wprep(ins["L0_attn_b_wkv_d"], obs=256)
        shared["l0b_wqu"] = wprep(ins["L0_attn_b_wq_u"])
        shared["l0b_wku"] = wprep(ins["L0_attn_b_wk_u"])
        shared["l0b_wvu"] = wprep(ins["L0_attn_b_wv_u"])
        shared["l0b_wo"] = wprep(ins["L0_attn_b_wo"])
    if N_SUB >= 5:
        p_ = np.arange(128)
        shared["idxcol"] = (1.0 / ((p_ % 16) + 1) / 64.0).reshape(128, 1).astype(np.float32)
        shared["pcol"] = p_.reshape(128, 1).astype(np.float32)
        shared["phcol"] = (p_ // 16).reshape(128, 1).astype(np.float32)
        shared["pm16col"] = (p_ % 16).reshape(128, 1).astype(np.float32)
        shared["a1_wo"] = wprep(ins["L1_attn_a_Wo"])
    if N_SUB >= 6:
        shared["f1_w1t"] = wprep(np.ascontiguousarray(ins["L1_ffn_w1"].T))
        shared["f1_w2t"] = wprep(np.ascontiguousarray(ins["L1_ffn_w2"].T))
        shared["f1_b1c"] = colprep(ins["L1_ffn_b1"])
        shared["f1_b2c"] = colprep(ins["L1_ffn_b2"])
    if N_SUB >= 7:
        shared["b1_wo"] = wprep(ins["L1_attn_b_wo"])
    if N_SUB < 1:
        shared = {}

    in_maps = []
    for c in range(NCORES):
        b, q = divmod(c, 4)
        t0 = q * TOK
        m = dict(shared)
        m["xin"] = fmprep(x[b, t0:t0 + TOK])
        if q == 0:
            m["xhalo"] = np.zeros((128, KC * HALO), np.float32)
            m["hk"] = np.zeros((1, 1), np.float32)
        else:
            m["xhalo"] = fmprep(x[b, t0 - HALO:t0])
            m["hk"] = np.ones((1, 1), np.float32)
        if N_SUB >= 3:
            m["tpos"] = np.full((1, 1), float(t0), np.float32)
        if N_SUB >= 5:
            hs = slice(q * 512, q * 512 + 512)
            m["a1_wq"] = wprep(np.ascontiguousarray(ins["L1_attn_a_Wq"][:, hs]))
            m["a1_wk"] = wprep(np.ascontiguousarray(ins["L1_attn_a_Wk"][:, hs]))
            m["a1_wv"] = wprep(np.ascontiguousarray(ins["L1_attn_a_Wv"][:, hs]))
            m["a1_wilr"] = wprep(np.ascontiguousarray(
                ins["L1_attn_a_W_ilr"][:, 8 * q:8 * q + 8]), obs=8)
            m["a1_bilr8"] = np.ascontiguousarray(
                ins["L1_attn_a_b_ilr"][8 * q:8 * q + 8].reshape(8, 1))
            m["a1_lnw"] = np.ascontiguousarray(ins["L1_attn_a_ln_weight"][8 * q:8 * q + 8])
            m["a1_lnb"] = np.ascontiguousarray(ins["L1_attn_a_ln_bias"][8 * q:8 * q + 8])
            m["a1_w1init"] = np.ascontiguousarray(
                ins["L1_attn_a_W1_init"][8 * q:8 * q + 8].reshape(4, 2, 64, 64)
                .transpose(1, 2, 0, 3).reshape(128, 256))
            m["a1_b1init"] = np.ascontiguousarray(
                ins["L1_attn_a_b1_init"][8 * q:8 * q + 8].reshape(8, 64))
            qsel = np.zeros((128, 4), np.float32)
            qsel[:, q] = 1.0
            m["qselcol"] = qsel
        if N_SUB >= 7:
            hs2 = slice(q * 512, q * 512 + 512)
            m["b1_wq"] = wprep(np.ascontiguousarray(ins["L1_attn_b_wq"][:, hs2]))
            m["b1_wk"] = wprep(np.ascontiguousarray(ins["L1_attn_b_wk"][:, hs2]))
            m["b1_wv"] = wprep(np.ascontiguousarray(ins["L1_attn_b_wv"][:, hs2]))
        in_maps.append(m)

    nc = build_program(N_SUB)
    res = run_bass_kernel_spmd(nc, in_maps, core_ids=list(range(NCORES)))
    if DEBUG:
        globals()["dbg"] = res.results
    out = np.empty((B, S, D), np.float32)
    for c in range(NCORES):
        b, q = divmod(c, 4)
        yc = res.results[c]["y"]
        out[b, q * TOK:(q + 1) * TOK] = yc.reshape(128, KC, TOK).transpose(1, 0, 2).reshape(D, TOK).T
    return out


# version 06
# speedup vs baseline: 1.0164x; 1.0164x over previous
import numpy as np
import concourse.bass as bass
import concourse.mybir as mybir
from concourse.ap import AP
from concourse.bass_utils import run_bass_kernel_spmd
from concourse.replica_groups import maybe_share_collective_output_space

F32 = mybir.dt.float32
BF16 = mybir.dt.bfloat16
AF = mybir.ActivationFunctionType
ALU = mybir.AluOpType
AXL = mybir.AxisListType

B, S, D = 2, 4096, 2048
NCORES = 8
TOK = 1024
HALO = 512
KC = D // 128
RG = [[0, 1, 2, 3], [4, 5, 6, 7]]
LN_EPS = 1e-5

N_SUB = 8
DEBUG = False



def wprep(W, obs=128):
    Din, M = W.shape
    kcw = Din // 128
    nob = M // obs
    return np.ascontiguousarray(
        W.reshape(kcw, 128, nob, obs).transpose(1, 2, 0, 3).reshape(128, nob * kcw * obs))


def fmprep(Xtok):
    T = Xtok.shape[0]
    return np.ascontiguousarray(
        Xtok.T.reshape(KC, 128, T).transpose(1, 0, 2).reshape(128, KC * T))


def colprep(v):
    return np.ascontiguousarray(v.reshape(-1, 128).T)



class Sem:
    __slots__ = ("h", "n", "off", "g")

    def __init__(self, h, off, g):
        self.h = h
        self.n = off
        self.off = off
        self.g = g


class G:
    NPOOL = 40

    def __init__(self, nc):
        self.nc = nc
        self._uid = 0
        self._pool = []
        self._counts = {}
        self._rr = 0

    def sem(self, prefix="s"):
        if len(self._pool) < self.NPOOL:
            h = self.nc.alloc_semaphore(name=f"pool{len(self._pool)}")
            self._pool.append(h)
            self._counts[id(h)] = 0
            hh = h
        else:
            hh = self._pool[self._rr % self.NPOOL]
            self._rr += 1
        return Sem(hh, self._counts[id(hh)], self)

    def uid(self):
        self._uid += 1
        return self._uid

    def inc(self, ins, sem, k=1):
        ins.then_inc(sem.h, k)
        sem.n += k
        self._counts[id(sem.h)] = sem.n

    def bar(self):
        self.nc.all_engine_barrier()


def build_program(n_sub):
    nc = bass.Bass("TRN2", target_bir_lowering=False)
    g = G(nc)

    def MM(*a, **k):
        ins = nc.tensor.matmul(*a, **k)
        g.lastT = ins
        return ins

    def TR(*a, **k):
        ins = nc.tensor.transpose(*a, **k)
        g.lastT = ins
        return ins

    def din(name, shape):
        return nc.dram_tensor(name, shape, F32, kind="ExternalInput").ap()

    def dint(name, shape):
        kind = "ExternalOutput" if DEBUG else "Internal"
        return nc.dram_tensor(name, shape, F32, kind=kind).ap()

    xin = din("xin", [128, KC * TOK])
    xhalo = din("xhalo", [128, KC * HALO])
    hk = din("hk", [1, 1])
    i = {}
    i["l0a_wq"] = din("l0a_wq", [128, 16 * KC * 128])
    i["l0a_wk"] = din("l0a_wk", [128, 16 * KC * 128])
    i["l0a_wv"] = din("l0a_wv", [128, 16 * KC * 128])
    i["l0a_wo"] = din("l0a_wo", [128, 16 * KC * 128])
    i["l0a_dog"] = din("l0a_dog", [32, 8])
    i["l0a_sinks"] = din("l0a_sinks", [1, 32])
    if n_sub >= 2:
        i["l0f_wT"] = din("l0f_wT", [128, 16 * KC * 128])
        i["l0f_cols"] = din("l0f_cols", [128, 4 * KC])
    if n_sub >= 3:
        i["l0b_wqd"] = din("l0b_wqd", [128, 2 * KC * 128])
        i["l0b_wkvd"] = din("l0b_wkvd", [128, 1 * KC * 256])
        i["l0b_wqu"] = din("l0b_wqu", [128, 8 * 2 * 128])
        i["l0b_wku"] = din("l0b_wku", [128, 8 * 2 * 128])
        i["l0b_wvu"] = din("l0b_wvu", [128, 16 * 2 * 128])
        i["l0b_wo"] = din("l0b_wo", [128, 16 * KC * 128])
        i["tpos"] = din("tpos", [1, 1])
    if n_sub >= 5:
        i["a1_wq"] = din("a1_wq", [128, 4 * KC * 128])
        i["a1_wk"] = din("a1_wk", [128, 4 * KC * 128])
        i["a1_wv"] = din("a1_wv", [128, 4 * KC * 128])
        i["a1_wilr"] = din("a1_wilr", [128, KC * 8])
        i["a1_bilr8"] = din("a1_bilr8", [8, 1])
        i["a1_lnw"] = din("a1_lnw", [8, 64])
        i["a1_lnb"] = din("a1_lnb", [8, 64])
        i["a1_w1init"] = din("a1_w1init", [128, 4 * 64])
        i["a1_b1init"] = din("a1_b1init", [8, 64])
        i["a1_wo"] = din("a1_wo", [128, 16 * KC * 128])
        i["idxcol"] = din("idxcol", [128, 1])
        i["pcol"] = din("pcol", [128, 1])
        i["phcol"] = din("phcol", [128, 1])
        i["pm16col"] = din("pm16col", [128, 1])
        i["qselcol"] = din("qselcol", [128, 4])
    if n_sub >= 6:
        i["f1_w1t"] = din("f1_w1t", [128, 64 * KC * 128])
        i["f1_w2t"] = din("f1_w2t", [128, 16 * 64 * 128])
        i["f1_b1c"] = din("f1_b1c", [128, 64])
        i["f1_b2c"] = din("f1_b2c", [128, 16])
    if n_sub >= 7:
        i["b1_wq"] = din("b1_wq", [128, 4 * KC * 128])
        i["b1_wk"] = din("b1_wk", [128, 4 * KC * 128])
        i["b1_wv"] = din("b1_wv", [128, 4 * KC * 128])
        i["b1_wo"] = din("b1_wo", [128, 16 * KC * 128])

    y = nc.dram_tensor("y", [128, KC * TOK], F32, kind="ExternalOutput").ap()

    qstash = dint("qstash", [16, 128, TOK])
    kstash = dint("kstash", [16, 128, HALO + TOK])
    vstash = dint("vstash", [16, 128, HALO + TOK])
    etab = dint("etab", [32, 2048])
    etab3 = dint("etab3", [32, 128, 1408])
    outdbg = dint("outdbg", [128, KC * (HALO + TOK)])
    scdbg = dint("scdbg", [1, 64])
    dbg2 = dint("dbg2", [128, 8])
    dbg3 = dint("dbg3", [1, 512])
    sml = dint("sml", [1, 128])
    cc_lat_in = nc.dram_tensor("cc_lat_in", [TOK, 256], F32, kind="Internal").ap()
    _space = maybe_share_collective_output_space("AllGather", RG)
    cc_lat_out = nc.dram_tensor("cc_lat_out", [S, 256], F32, kind="Internal",
                                addr_space=_space).ap()
    bstash = dint("bstash", [16, 128, TOK])
    xbfm = dint("xbfm", [4, 128, S])
    xcfm = dint("xcfm", [4, 128, S])
    xafm = dint("xafm", [4, 128, S])
    ilrst = dint("ilrst", [8, S])
    cc_xn_in = nc.dram_tensor("cc_xn_in", [D, TOK], F32, kind="Internal").ap()
    cc_xn_out = nc.dram_tensor("cc_xn_out", [4 * D, TOK], F32, kind="Internal",
                               addr_space=_space).ap()
    cc_s_in = nc.dram_tensor("cc_s_in", [512, S], F32, kind="Internal").ap()
    scandump = dint("scandump", [512, S])
    sbdump = dint("sbdump", [128, 16 * 64 + 24])
    cndump = dint("cndump", [128, 512])
    cc_s_out = nc.dram_tensor("cc_s_out", [D, S], F32, kind="Internal",
                              addr_space=_space).ap()
    w1q = nc.dram_tensor("w1q", [128, 64 * KC * 128], BF16, kind="Internal").ap()
    w2q = nc.dram_tensor("w2q", [128, 16 * 64 * 128], BF16, kind="Internal").ap()

    X = nc.alloc_sbuf_tensor("X", [128, KC * TOK], F32)
    CONST = nc.alloc_sbuf_tensor("CONST", [128, 4], F32)
    IDENT = nc.alloc_sbuf_tensor("IDENT", [128, 128], F32)
    SC = nc.alloc_sbuf_tensor("SC", [1, 64], F32)
    CREP = nc.alloc_sbuf_tensor("CREP", [128, 2], F32)
    SQ = nc.alloc_sbuf_tensor("SQ", [128, 2 * 512], F32)
    EV = nc.alloc_sbuf_tensor("EV", [128, 2 * 512], F32)
    RECIP = nc.alloc_sbuf_tensor("RECIP", [1, 512], F32)

    pb = [nc.alloc_psum_tensor(f"pb{j}", [128, 512], F32) for j in range(6)]
    pst = [nc.alloc_psum_tensor(f"pst{j}", [1, 512], F32) for j in range(2)]

    ONESR = nc.alloc_sbuf_tensor("ONESR", [1, 128], F32)
    ones = CONST[:, 0:1]
    eps1 = CONST[0:1, 1:2]

    nc.vector.memset(CONST[:, 0:1], 1.0)
    nc.vector.memset(CONST[:, 1:2], LN_EPS)
    nc.vector.memset(ONESR[:, :], 1.0)
    nc.vector.drain()
    with nc.sbuf_tensor("i_rid", [128, 128], F32) as rid, \
         nc.sbuf_tensor("i_cid", [128, 128], F32) as cid:
        nc.gpsimd.iota(rid[:], pattern=[[0, 128]], channel_multiplier=1,
                       allow_small_or_imprecise_dtypes=True)
        nc.gpsimd.iota(cid[:], pattern=[[1, 128]], channel_multiplier=0,
                       allow_small_or_imprecise_dtypes=True)
        gp0 = g.sem("gp0")
        g.inc(nc.gpsimd.drain(), gp0)
        nc.vector.wait_ge(gp0.h, gp0.n)
        nc.vector.tensor_tensor(IDENT[:], rid[:], cid[:], op=ALU.is_equal)
        nc.vector.drain()
    g.bar()


    def ln_stats(src_fn, tcols, stat):
        ntb = tcols // 512
        mm = g.sem("lnmm")
        sq = g.sem("lnsq")
        rd = g.sem("lnrd")
        for tb in range(ntb):
            if tb >= 1:
                nc.tensor.wait_ge(rd.h, rd.n)
            for c in range(KC):
                g.inc(MM(pst[0][0:1, :], ones, src_fn(c, tb),
                                       start=(c == 0), stop=(c == KC - 1)), mm)
            for c in range(KC):
                buf = SQ[:, (c % 2) * 512:(c % 2) * 512 + 512]
                if c >= 2:
                    nc.scalar.wait_ge(mm.h, mm.n - 1)
                g.inc(nc.scalar.activation(buf, src_fn(c, tb), AF.Square), sq)
                nc.tensor.wait_ge(sq.h, sq.n)
                g.inc(MM(pst[1][0:1, :], ones, buf,
                                       start=(c == 0), stop=(c == KC - 1)), mm)
            nc.vector.wait_ge(mm.h, mm.n)
            mcol = stat[0:1, tb * 512:tb * 512 + 512]
            rcol = stat[0:1, tcols + tb * 512:tcols + tb * 512 + 512]
            nc.vector.tensor_scalar_mul(mcol, pst[0][0:1, :], 1.0 / D)
            nc.vector.drain()
            nc.vector.tensor_scalar_mul(rcol, pst[1][0:1, :], 1.0 / D)
            g.inc(nc.vector.drain(), rd)
            nc.vector.tensor_tensor(SQ[0:1, 0:512], mcol, mcol, op=ALU.mult)
            nc.vector.drain()
            nc.vector.tensor_tensor(rcol, rcol, SQ[0:1, 0:512], op=ALU.subtract)
            g.inc(nc.vector.drain(), sq)
            nc.scalar.wait_ge(sq.h, sq.n)
            g.inc(nc.scalar.activation(rcol, rcol, AF.Sqrt, bias=eps1), sq)
            nc.vector.wait_ge(sq.h, sq.n)
            nc.vector.reciprocal(rcol, rcol)
            nc.vector.drain()
        g.bar()

    def ln_apply(src_fn, dst_fn, tcols, stat, aff=None):
        ntb = tcols // 512
        mm = g.sem("lamm")
        for tb in range(ntb):
            mcol = stat[0:1, tb * 512:tb * 512 + 512]
            rcol = stat[0:1, tcols + tb * 512:tcols + tb * 512 + 512]
            MM(pb[4][:, :], ONESR[0:1, :], mcol,
                             start=True, stop=True)
            g.inc(MM(pb[5][:, :], ONESR[0:1, :], rcol,
                                   start=True, stop=True), mm)
            nc.vector.wait_ge(mm.h, mm.n)
            for c in range(KC):
                s_ = src_fn(c, tb)
                d_ = dst_fn(c, tb)
                nc.vector.tensor_tensor(d_, s_, pb[4][:, :], op=ALU.subtract)
                nc.vector.drain()
                nc.vector.tensor_tensor(d_, d_, pb[5][:, :], op=ALU.mult)
                nc.vector.drain()
                if aff is not None:
                    lnw_col, lnb_col = aff
                    nc.vector.tensor_scalar(d_, d_, lnw_col[:, c:c + 1], lnb_col[:, c:c + 1],
                                            op0=ALU.mult, op1=ALU.add)
                    nc.vector.drain()
            g.inc(nc.vector.drain(), mm)
            nc.tensor.wait_ge(mm.h, mm.n)
        g.bar()

    def proj(w_hbm, nob, src_fn, ntb, evac, kc=KC, obs=128):
        mm = g.sem("pjmm")
        ev = g.sem("pjev")
        wl = g.sem("pjwl")
        with nc.sbuf_tensor(f"WB{g.uid()}", [128, 2 * kc * obs], F32) as WB:
            j = 0
            for ob in range(nob):
                buf = WB[:, (ob % 2) * kc * obs:(ob % 2) * kc * obs + kc * obs]
                if ob >= 2:
                    nc.sync.wait_ge(mm.h, mm.off + (ob - 1) * ntb)
                g.inc(nc.sync.dma_start(
                    out=buf, in_=w_hbm[:, ob * kc * obs:(ob + 1) * kc * obs]), wl, 16)
                nc.tensor.wait_ge(wl.h, wl.off + 16 * (ob + 1))
                for tb in range(ntb):
                    psb = pb[j % 2]
                    if j >= 2:
                        nc.tensor.wait_ge(ev.h, ev.off + j - 1)
                    out_ap = psb[:, :] if obs == 128 else psb[0:obs, :]
                    for c in range(kc):
                        MM(out_ap,
                                         buf[:, c * obs:(c + 1) * obs],
                                         src_fn(c, tb),
                                         start=(c == 0), stop=(c == kc - 1))
                    g.inc(g.lastT, mm)
                    evac(ob, tb, out_ap, mm, ev, j)
                    j += 1
            g.bar()

    def scalar_bcast(src_11, dst_col, nrows=128):
        mm = g.sem("bc")
        g.inc(nc.vector.drain(), mm)
        g.inc(nc.scalar.drain(), mm)
        nc.tensor.wait_ge(mm.h, mm.n)
        g.inc(MM(pb[4][0:nrows, 0:1], ONESR[0:1, 0:nrows],
                               src_11, start=True, stop=True), mm)
        nc.vector.wait_ge(mm.h, mm.n)
        nc.vector.tensor_copy(dst_col, pb[4][0:nrows, 0:1])
        g.inc(nc.vector.drain(), mm)
        nc.tensor.wait_ge(mm.h, mm.n)

    def cross_maxP(src_col, dst_11, nrows):
        st = g.sem("cmp")
        g.inc(nc.sync.dma_start(out=AP(sml.tensor, 0, [[1, nrows], [1, 1]]), in_=src_col), st, 16)
        nc.sync.wait_ge(st.h, st.n)
        with nc.sbuf_tensor(f"cmp{g.uid()}", [1, nrows], F32) as t:
            g.inc(nc.sync.dma_start(out=t[:], in_=sml[0:1, 0:nrows]), st, 16)
            nc.vector.wait_ge(st.h, st.n)
            nc.vector.tensor_reduce(dst_11, t[:], axis=AXL.X, op=ALU.max)
            nc.vector.drain()
        g.bar()

    def cross_max32(src_col32, dst_11):
        st = g.sem("cm")
        g.inc(nc.sync.dma_start(out=AP(sml.tensor, 0, [[1, 32], [1, 1]]), in_=src_col32), st, 16)
        nc.sync.wait_ge(st.h, st.n)
        with nc.sbuf_tensor(f"cm{g.uid()}", [1, 32], F32) as t:
            g.inc(nc.sync.dma_start(out=t[:], in_=sml[0:1, 0:32]), st, 16)
            nc.vector.wait_ge(st.h, st.n)
            nc.vector.tensor_reduce(dst_11, t[:], axis=AXL.X, op=ALU.max)
            nc.vector.drain()
        g.bar()

    def s1_l0_attn_a():
        TW = HALO + TOK
        ld = g.sem("ld1")
        with nc.sbuf_tensor("XN_a", [128, KC * TW], F32) as XN:
            with nc.sbuf_tensor("LST_a", [1, 2 * TOK], F32) as LST:
                g.inc(nc.sync.dma_start(out=X[:], in_=xin[:, :]), ld, 16)
                g.inc(nc.sync.dma_start(
                    out=AP(XN.ap().tensor, 0, [list(XN.ap().ap[0]), [TW, KC], [1, HALO]]),
                    in_=xhalo[:, :].rearrange("p (c t) -> p c t", c=KC)), ld, 16)
                nc.sync.wait_ge(ld.h, ld.n)
                g.bar()
                ln_stats(lambda c, tb: XN[:, c * TW + tb * 512: c * TW + tb * 512 + 512],
                         HALO, LST[0:1, 0:1024])
                ln_apply(lambda c, tb: XN[:, c * TW + tb * 512: c * TW + tb * 512 + 512],
                         lambda c, tb: XN[:, c * TW + tb * 512: c * TW + tb * 512 + 512],
                         HALO, LST[0:1, 0:1024])
                ln_stats(lambda c, tb: X[:, c * TOK + tb * 512: c * TOK + tb * 512 + 512],
                         TOK, LST)
                ln_apply(lambda c, tb: X[:, c * TOK + tb * 512: c * TOK + tb * 512 + 512],
                         lambda c, tb: XN[:, c * TW + HALO + tb * 512: c * TW + HALO + tb * 512 + 512],
                         TOK, LST)

            with nc.sbuf_tensor("DOG", [32, 8], F32) as DOG, \
                 nc.sbuf_tensor("PP", [32, 8], F32) as PP, \
                 nc.sbuf_tensor("ET", [32, 1536], F32) as ET, \
                 nc.sbuf_tensor("SCR_e", [32, 2 * 1536], F32) as SCR:
                sq = g.sem("esq")
                ev = g.sem("eev")
                st = g.sem("est")
                g.inc(nc.sync.dma_start(out=DOG[:, 0:7], in_=i["l0a_dog"][:, 0:7]), ld, 16)
                g.inc(nc.sync.dma_start(out=SC[0:1, 32:64], in_=i["l0a_sinks"][:, :]), ld, 16)
                g.inc(nc.sync.dma_start(out=SC[0:1, 30:31], in_=hk[:, :]), ld, 16)
                nc.gpsimd.iota(SCR[:, 0:1536], pattern=[[1, 1536]], base=-512,
                               channel_multiplier=0, allow_small_or_imprecise_dtypes=True)
                g.inc(nc.gpsimd.drain(), ev)
                nc.scalar.wait_ge(ld.h, ld.n)
                nc.vector.wait_ge(ld.h, ld.n)
                nc.vector.wait_ge(ev.h, ev.n)
                nc.scalar.wait_ge(ev.h, ev.n)
                for j, scol in ((0, 1), (1, 4)):
                    nc.scalar.activation(PP[:, j:j + 1], DOG[:, scol:scol + 1], AF.Exp)
                    nc.scalar.drain()
                    nc.scalar.activation(PP[:, j:j + 1], PP[:, j:j + 1], AF.Ln,
                                         bias=CONST[0:32, 0:1])
                    nc.scalar.drain()
                    nc.scalar.activation(PP[:, j:j + 1], PP[:, j:j + 1], AF.Square)
                    nc.scalar.drain()
                    g.inc(nc.scalar.mul(PP[:, j:j + 1], PP[:, j:j + 1], 2.0), sq)
                nc.vector.wait_ge(sq.h, sq.n)
                for j in (0, 1):
                    nc.vector.reciprocal(PP[:, j:j + 1], PP[:, j:j + 1])
                    nc.vector.drain()
                    nc.vector.tensor_scalar_mul(PP[:, j:j + 1], PP[:, j:j + 1], -1.0)
                    nc.vector.drain()
                for j, acol, mcol in ((0, 0, 2), (1, 3, 5)):
                    wk = SCR[:, 1536:3072]
                    nc.vector.tensor_scalar(wk, SCR[:, 0:1536], DOG[:, mcol:mcol + 1], None,
                                            op0=ALU.subtract)
                    nc.vector.drain()
                    nc.vector.tensor_tensor(wk, wk, wk, op=ALU.mult)
                    nc.vector.drain()
                    nc.vector.tensor_scalar(wk, wk, PP[:, j:j + 1], None, op0=ALU.mult)
                    g.inc(nc.vector.drain(), ev)
                    nc.scalar.wait_ge(ev.h, ev.n)
                    g.inc(nc.scalar.activation(wk, wk, AF.Exp), sq)
                    nc.vector.wait_ge(sq.h, sq.n)
                    if j == 0:
                        nc.vector.tensor_scalar(ET[:, :], wk, DOG[:, acol:acol + 1], None,
                                                op0=ALU.mult)
                    else:
                        nc.vector.tensor_scalar(wk, wk, DOG[:, acol:acol + 1], None,
                                                op0=ALU.mult)
                        nc.vector.drain()
                        nc.vector.tensor_tensor(ET[:, :], ET[:, :], wk, op=ALU.subtract)
                    nc.vector.drain()
                nc.vector.tensor_scalar(ET[:, :], ET[:, :], DOG[:, 6:7], None, op0=ALU.add)
                nc.vector.drain()
                band = SCR[:, 1536:3072]
                nc.vector.tensor_scalar(band, SCR[:, 0:1536], -0.5, None, op0=ALU.is_ge)
                nc.vector.drain()
                nc.vector.tensor_scalar(SCR[:, 0:1536], SCR[:, 0:1536], 511.5, None,
                                        op0=ALU.is_le)
                nc.vector.drain()
                nc.vector.tensor_tensor(band, band, SCR[:, 0:1536], op=ALU.mult)
                nc.vector.drain()
                fm_ = SCR[:, 0:1536]
                nc.vector.scalar_tensor_tensor(fm_, ET[:, :], 1e30, band,
                                               op0=ALU.add, op1=ALU.mult)
                nc.vector.drain()
                nc.vector.tensor_scalar(fm_, fm_, -1e30, None, op0=ALU.add)
                nc.vector.drain()
                nc.vector.tensor_reduce(PP[:, 4:5], fm_, axis=AXL.X, op=ALU.max)
                g.inc(nc.vector.drain(), ev)
                nc.scalar.wait_ge(ev.h, ev.n)
                g.inc(nc.scalar.activation(ET[:, :], ET[:, :], AF.Exp), sq)
                nc.vector.wait_ge(sq.h, sq.n)
                nc.vector.tensor_tensor(ET[:, :], ET[:, :], band, op=ALU.mult)
                g.inc(nc.vector.drain(), ev)
                nc.sync.wait_ge(ev.h, ev.n)
                g.inc(nc.sync.dma_start(out=etab[:, 0:1536], in_=ET[:, :]), st, 16)
                nc.sync.wait_ge(st.h, st.n)
                g.inc(nc.sync.dma_start(
                    out=AP(etab3.tensor, 0, [[128 * 1408, 32], [1408, 128], [1, 1408]]),
                    in_=AP(etab.tensor, 128, [[2048, 32], [-1, 128], [1, 1408]])), st, 16)
                nc.sync.wait_ge(st.h, st.n)
                cross_max32(PP[:, 4:5], SC[0:1, 0:1])

            with nc.sbuf_tensor("QKST", [1, 2560], F32) as QKST:
                QN2 = QKST[0:1, 0:1024]
                KN2 = QKST[0:1, 1024:2560]
                nc.vector.memset(QN2, 0.0)
                nc.vector.memset(KN2, 0.0)
                nc.vector.drain()
                g.bar()

                def mk_evac(stash, statrow, scale):
                    def evac(ob, tb, ps, mm, ev, j):
                        ebuf = EV[:, (j % 2) * 512:(j % 2) * 512 + 512]
                        nc.vector.wait_ge(mm.h, mm.n)
                        if scale != 1.0:
                            nc.vector.tensor_scalar_mul(ebuf, ps, scale)
                        else:
                            nc.vector.tensor_copy(ebuf, ps)
                        g.inc(nc.vector.drain(), ev)
                        nc.sync.wait_ge(ev.h, ev.n)
                        st = evac.st
                        g.inc(nc.sync.dma_start(
                            out=stash[ob, :, tb * 512:(tb + 1) * 512], in_=ebuf), st, 16)
                        sqb = SQ[:, (j % 2) * 512:(j % 2) * 512 + 512]
                        nc.scalar.wait_ge(ev.h, ev.n)
                        if j >= 2:
                            nc.scalar.wait_ge(evac.mm2.h, evac.mm2.n - 1)
                        g.inc(nc.scalar.activation(sqb, ebuf, AF.Square), evac.sq)
                        nc.tensor.wait_ge(evac.sq.h, evac.sq.n)
                        nc.tensor.wait_ge(evac.s2.h, evac.s2.n)
                        g.inc(MM(pst[0][0:1, :], ones, sqb,
                                               start=True, stop=True), evac.mm2)
                        nc.vector.wait_ge(evac.mm2.h, evac.mm2.n)
                        col = statrow[0:1, tb * 512:tb * 512 + 512]
                        nc.vector.tensor_tensor(col, col, pst[0][0:1, :], op=ALU.max)
                        g.inc(nc.vector.drain(), evac.s2)
                    evac.st = g.sem("qkst")
                    evac.sq = g.sem("qksq")
                    evac.mm2 = g.sem("qkmm2")
                    evac.s2 = g.sem("qks2")
                    return evac

                proj(i["l0a_wq"], 16,
                     lambda c, tb: XN[:, c * TW + HALO + tb * 512: c * TW + HALO + tb * 512 + 512],
                     2, mk_evac(qstash, QN2, 0.125))
                proj(i["l0a_wk"], 16,
                     lambda c, tb: XN[:, c * TW + tb * 512: c * TW + tb * 512 + 512],
                     3, mk_evac(kstash, KN2, 1.0))
                proj(i["l0a_wv"], 16,
                     lambda c, tb: XN[:, c * TW + tb * 512: c * TW + tb * 512 + 512],
                     3, mk_evac(vstash, KN2, 1.0))

                sq = g.sem("csq")
                ev = g.sem("cev")
                nc.vector.tensor_reduce(SC[0:1, 1:2], QN2, axis=AXL.X, op=ALU.max)
                nc.vector.drain()
                nc.vector.tensor_reduce(SC[0:1, 2:3], KN2, axis=AXL.X, op=ALU.max)
                g.inc(nc.vector.drain(), ev)
                nc.scalar.wait_ge(ev.h, ev.n)
                nc.scalar.activation(SC[0:1, 1:2], SC[0:1, 1:2], AF.Sqrt)
                nc.scalar.drain()
                g.inc(nc.scalar.activation(SC[0:1, 2:3], SC[0:1, 2:3], AF.Sqrt), sq)
                nc.vector.wait_ge(sq.h, sq.n)
                nc.vector.tensor_tensor(SC[0:1, 3:4], SC[0:1, 1:2], SC[0:1, 2:3], op=ALU.mult)
                nc.vector.drain()
                nc.vector.tensor_tensor(SC[0:1, 3:4], SC[0:1, 3:4], SC[0:1, 0:1], op=ALU.add)
                nc.vector.drain()
                nc.vector.tensor_scalar_mul(SC[0:1, 4:5], SC[0:1, 3:4], -1.0)
                nc.vector.drain()
                nc.vector.tensor_scalar(SC[0:1, 32:64], SC[0:1, 32:64], SC[0:1, 3:4], None,
                                        op0=ALU.subtract)
                g.inc(nc.vector.drain(), ev)
                nc.scalar.wait_ge(ev.h, ev.n)
                g.inc(nc.scalar.activation(SC[0:1, 32:64], SC[0:1, 32:64], AF.Exp), sq)
                nc.vector.wait_ge(sq.h, sq.n)
                scalar_bcast(SC[0:1, 4:5], CREP[:, 0:1], 128)
                scalar_bcast(SC[0:1, 30:31], CREP[:, 1:2], 128)
                g.bar()

            OUT = XN
            with nc.sbuf_tensor("QH_a", [64, TOK], F32) as QH, \
                 nc.sbuf_tensor("KH_a", [64, TW], F32) as KH, \
                 nc.sbuf_tensor("VH_a", [64, TW], F32) as VH, \
                 nc.sbuf_tensor("VTM_a", [128, 12 * 72], F32) as VTM, \
                 nc.sbuf_tensor("ETW_a", [128, 1408], F32) as ETW, \
                 nc.sbuf_tensor("E64_a", [1, 72], F32) as E64:
                PT = EV
                for kt in range(12):
                    nc.vector.memset(VTM[:, kt * 72 + 64:kt * 72 + 65], 1.0)
                nc.vector.drain()
                g.bar()

                hl = g.sem("hl")
                gl = g.sem("gl")
                mm = g.sem("amm")
                mo = g.sem("amo")
                sq = g.sem("asq")
                ev = g.sem("aev")
                for h in range(32):
                    cch = h // 2
                    rbase = 64 * (h % 2)
                    g.inc(nc.sync.dma_start(out=QH[:, :], in_=qstash[cch, rbase:rbase + 64, :]), hl, 16)
                    g.inc(nc.sync.dma_start(out=KH[:, :], in_=kstash[cch, rbase:rbase + 64, :]), hl, 16)
                    g.inc(nc.sync.dma_start(out=ETW[:, :], in_=etab3[h, :, :]), hl, 16)
                    g.inc(nc.gpsimd.dma_start(out=VH[:, :], in_=vstash[cch, rbase:rbase + 64, :]), gl, 16)
                    nc.tensor.wait_ge(gl.h, gl.n)
                    for kt in range(12):
                        TR(pb[2][:, 0:64], VH[:, kt * 128:(kt + 1) * 128],
                                            IDENT[0:64, 0:64])
                        g.inc(g.lastT, mm)
                        nc.vector.wait_ge(mm.h, mm.n)
                        nc.vector.tensor_copy(VTM[:, kt * 72:kt * 72 + 64], pb[2][:, 0:64])
                        g.inc(nc.vector.drain(), ev)
                        nc.tensor.wait_ge(ev.h, ev.n)
                    nc.sync.wait_ge(hl.h, hl.n)
                    g.bar()

                    for qb in range(2):
                        for kt in range(8):
                            wkt = qb * 4 + kt
                            psb = pb[kt % 2]
                            if kt >= 2:
                                nc.tensor.wait_ge(sq.h, sq.n - 1)
                            MM(psb[:, :], KH[:, wkt * 128:(wkt + 1) * 128],
                                             QH[:, qb * 512:(qb + 1) * 512],
                                             start=True, stop=True)
                            g.inc(g.lastT, mm)
                            ptile = PT[:, (kt % 2) * 512:(kt % 2) * 512 + 512]
                            nc.scalar.wait_ge(mm.h, mm.n)
                            if kt >= 2:
                                nc.scalar.wait_ge(mo.h, mo.n - 1)
                            g.inc(nc.scalar.activation(ptile, psb[:, :], AF.Exp,
                                                       bias=CREP[:, 0:1]), sq)
                            nc.vector.wait_ge(sq.h, sq.n)
                            nc.vector.tensor_tensor(ptile, ptile,
                                                    ETW[:, 896 - 128 * kt: 1408 - 128 * kt],
                                                    op=ALU.mult)
                            if qb == 0 and kt < 4:
                                nc.vector.drain()
                                nc.vector.tensor_scalar(ptile, ptile, CREP[:, 1:2], None,
                                                        op0=ALU.mult)
                            g.inc(nc.vector.drain(), ev)
                            nc.tensor.wait_ge(ev.h, ev.n)
                            g.inc(MM(pb[3][0:65, :],
                                                   VTM[:, wkt * 72:wkt * 72 + 65], ptile,
                                                   start=(kt == 0), stop=(kt == 7),
                                                   skip_group_check=True), mo)
                        nc.vector.wait_ge(mo.h, mo.n)
                        nc.vector.tensor_scalar(RECIP[0:1, :], pb[3][64:65, :],
                                                SC[0:1, 32 + h:33 + h], None, op0=ALU.add)
                        nc.vector.drain()
                        nc.vector.reciprocal(RECIP[0:1, :], RECIP[0:1, :])
                        g.inc(nc.vector.drain(), ev)
                        nc.tensor.wait_ge(ev.h, ev.n)
                        g.inc(MM(pb[4][0:64, :], ONESR[0:1, 0:64],
                                               RECIP[0:1, :], start=True, stop=True), mm)
                        nc.vector.wait_ge(mm.h, mm.n)
                        nc.vector.tensor_copy(SQ[64:128, 0:512], pb[4][0:64, :])
                        nc.vector.drain()
                        dsl = slice(cch * TW + qb * 512, cch * TW + qb * 512 + 512)
                        nc.vector.tensor_tensor(OUT[rbase:rbase + 64, dsl],
                                                pb[3][0:64, :], SQ[64:128, 0:512], op=ALU.mult)
                        g.inc(nc.vector.drain(), ev)
                        nc.tensor.wait_ge(ev.h, ev.n)
                    g.bar()

            if DEBUG:
                dst_ = g.sem("dbg")
                g.inc(nc.sync.dma_start(out=outdbg[:, :], in_=OUT[:, :]), dst_, 16)
                g.inc(nc.sync.dma_start(out=scdbg[:, :], in_=SC[:, :]), dst_, 16)
                g.inc(nc.sync.dma_start(out=dbg2[:, 0:2], in_=CREP[:, 0:2]), dst_, 16)
                g.inc(nc.sync.dma_start(out=dbg3[:, :], in_=RECIP[0:1, :]), dst_, 16)
                nc.sync.wait_ge(dst_.h, dst_.n)
                g.bar()

            def evac_res(ob, tb, ps, mm, ev, j):
                nc.vector.wait_ge(mm.h, mm.n)
                sl = slice(ob * TOK + tb * 512, ob * TOK + tb * 512 + 512)
                nc.vector.tensor_tensor(X[:, sl], X[:, sl], ps, op=ALU.add)
                g.inc(nc.vector.drain(), ev)

            proj(i["l0a_wo"], 16,
                 lambda c, tb: OUT[:, c * TW + tb * 512: c * TW + tb * 512 + 512],
                 2, evac_res)


    def s_l0_ffn(first_call):
        with nc.sbuf_tensor(f"XN_f{g.uid()}", [128, KC * TOK], F32) as XN2, \
             nc.sbuf_tensor(f"LST_f{g.uid()}", [1, 2 * TOK], F32) as LST, \
             nc.sbuf_tensor(f"COLS_f{g.uid()}", [128, 5 * KC], F32) as COLS, \
             nc.sbuf_tensor(f"WMX_f{g.uid()}", [128, 2], F32) as WMX:
            ld = g.sem("ldf")
            g.inc(nc.sync.dma_start(out=COLS[:, 0:4 * KC], in_=i["l0f_cols"][:, :]), ld, 16)
            ln_stats(lambda c, tb: X[:, c * TOK + tb * 512: c * TOK + tb * 512 + 512],
                     TOK, LST)
            ln_apply(lambda c, tb: X[:, c * TOK + tb * 512: c * TOK + tb * 512 + 512],
                     lambda c, tb: XN2[:, c * TOK + tb * 512: c * TOK + tb * 512 + 512],
                     TOK, LST)
            ln_stats(lambda c, tb: XN2[:, c * TOK + tb * 512: c * TOK + tb * 512 + 512],
                     TOK, LST)
            nc.vector.wait_ge(ld.h, ld.n)
            g.bar()
            ln_apply(lambda c, tb: XN2[:, c * TOK + tb * 512: c * TOK + tb * 512 + 512],
                     lambda c, tb: XN2[:, c * TOK + tb * 512: c * TOK + tb * 512 + 512],
                     TOK, LST,
                     aff=(COLS[:, 2 * KC:3 * KC], COLS[:, 3 * KC:4 * KC]))

            if first_call:
                mx = g.sem("fmx")
                wl = g.sem("fwl")
                nc.vector.memset(WMX[:, 0:1], 0.0)
                g.inc(nc.vector.drain(), mx)
                with nc.sbuf_tensor(f"WBS{g.uid()}", [128, 2 * KC * 128], F32) as WB:
                    for ob in range(16):
                        buf = WB[:, (ob % 2) * KC * 128:(ob % 2) * KC * 128 + KC * 128]
                        if ob >= 2:
                            nc.sync.wait_ge(mx.h, mx.n - 2)
                        g.inc(nc.sync.dma_start(
                            out=buf, in_=i["l0f_wT"][:, ob * KC * 128:(ob + 1) * KC * 128]), wl, 16)
                        nc.vector.wait_ge(wl.h, wl.off + 16 * (ob + 1))
                        nc.vector.tensor_reduce(WMX[:, 1:2], buf, axis=AXL.X, op=ALU.max,
                                                apply_absolute_value=True)
                        nc.vector.drain()
                        nc.vector.tensor_tensor(WMX[:, 0:1], WMX[:, 0:1], WMX[:, 1:2], op=ALU.max)
                        g.inc(nc.vector.drain(), mx)
                g.bar()
                cross_maxP(WMX[:, 0:1], SC[0:1, 8:9], 128)
            sb = g.sem("fsb")
            scalar_bcast(SC[0:1, 8:9], WMX[:, 0:1], 128)
            nc.vector.tensor_scalar(COLS[:, 4 * KC:5 * KC], COLS[:, 0:KC], WMX[:, 0:1], None,
                                    op0=ALU.mult)
            g.inc(nc.vector.drain(), sb)
            g.bar()

            mm = g.sem("fmm")
            ev = g.sem("fev")
            wl = g.sem("fwl2")
            sg = g.sem("fsg")
            with nc.sbuf_tensor(f"WBF{g.uid()}", [128, 2 * KC * 128], F32) as WB:
                j = 0
                for ob in range(16):
                    buf = WB[:, (ob % 2) * KC * 128:(ob % 2) * KC * 128 + KC * 128]
                    if ob >= 2:
                        nc.sync.wait_ge(mm.h, mm.off + (ob - 1) * 2)
                    g.inc(nc.sync.dma_start(
                        out=buf, in_=i["l0f_wT"][:, ob * KC * 128:(ob + 1) * KC * 128]), wl, 16)
                    nc.scalar.wait_ge(wl.h, wl.off + 16 * (ob + 1))
                    g.inc(nc.scalar.activation(buf, buf, AF.Sign), sg)
                    nc.tensor.wait_ge(sg.h, sg.n)
                    for tb in range(2):
                        psb = pb[j % 2]
                        if j >= 2:
                            nc.tensor.wait_ge(ev.h, ev.off + j - 1)
                        for c in range(KC):
                            MM(psb[:, :], buf[:, c * 128:(c + 1) * 128],
                                             XN2[:, c * TOK + tb * 512: c * TOK + tb * 512 + 512],
                                             start=(c == 0), stop=(c == KC - 1))
                        g.inc(g.lastT, mm)
                        nc.vector.wait_ge(mm.h, mm.n)
                        ebuf = EV[:, (j % 2) * 512:(j % 2) * 512 + 512]
                        nc.vector.tensor_scalar(ebuf, psb[:, :],
                                                COLS[:, 4 * KC + ob:4 * KC + ob + 1],
                                                COLS[:, KC + ob:KC + ob + 1],
                                                op0=ALU.mult, op1=ALU.add)
                        nc.vector.drain()
                        sl = slice(ob * TOK + tb * 512, ob * TOK + tb * 512 + 512)
                        nc.vector.tensor_tensor(X[:, sl], X[:, sl], ebuf, op=ALU.add)
                        g.inc(nc.vector.drain(), ev)
                        j += 1
            g.bar()

    def s3_l0_attn_b():
        from contextlib import ExitStack
        ld = g.sem("ldb")
        ccs = g.sem("bccs")
        esq2 = ExitStack()
        QU = esq2.enter_context(nc.sbuf_tensor("QU_b", [128, 8 * TOK], F32))
        with nc.sbuf_tensor("XN_b", [128, KC * TOK], F32) as XN3, \
             nc.sbuf_tensor("LST_b", [1, 2 * TOK], F32) as LST:
            ln_stats(lambda c, tb: X[:, c * TOK + tb * 512: c * TOK + tb * 512 + 512],
                     TOK, LST)
            ln_apply(lambda c, tb: X[:, c * TOK + tb * 512: c * TOK + tb * 512 + 512],
                     lambda c, tb: XN3[:, c * TOK + tb * 512: c * TOK + tb * 512 + 512],
                     TOK, LST)

            mm = g.sem("bmm")
            ev = g.sem("bev")
            st = g.sem("bst")
            with nc.sbuf_tensor("WKVD", [128, KC * 256], F32) as WKVD:
                g.inc(nc.sync.dma_start(out=WKVD[:, :], in_=i["l0b_wkvd"][:, :]), ld, 16)
                nc.tensor.wait_ge(ld.h, ld.n)
                for tt in range(8):
                    psb = pb[tt % 2]
                    if tt >= 2:
                        nc.tensor.wait_ge(ev.h, ev.n - 1)
                    for c in range(KC):
                        MM(psb[:, 0:256],
                                         XN3[:, c * TOK + tt * 128: c * TOK + tt * 128 + 128],
                                         WKVD[:, c * 256:(c + 1) * 256],
                                         start=(c == 0), stop=(c == KC - 1))
                    g.inc(g.lastT, mm)
                    nc.vector.wait_ge(mm.h, mm.n)
                    ebuf = EV[:, (tt % 2) * 512:(tt % 2) * 512 + 256]
                    nc.vector.tensor_copy(ebuf, psb[:, 0:256])
                    g.inc(nc.vector.drain(), ev)
                    nc.sync.wait_ge(ev.h, ev.n)
                    g.inc(nc.sync.dma_start(out=cc_lat_in[tt * 128:(tt + 1) * 128, :],
                                            in_=ebuf), st, 16)
            nc.gpsimd.wait_ge(st.h, st.n)
            g.inc(nc.gpsimd.collective_compute(
                "AllGather", ALU.bypass, replica_groups=RG,
                ins=[cc_lat_in[:, :]], outs=[cc_lat_out[:, :]]), ccs)

            with nc.sbuf_tensor("QL_b", [128, 2 * TOK], F32) as QL:
                def evac_ql(ob, tb, ps, mm_, ev_, j):
                    nc.vector.wait_ge(mm_.h, mm_.n)
                    nc.vector.tensor_copy(QL[:, ob * TOK + tb * 512: ob * TOK + tb * 512 + 512], ps)
                    g.inc(nc.vector.drain(), ev_)
                proj(i["l0b_wqd"], 2,
                     lambda c, tb: XN3[:, c * TOK + tb * 512: c * TOK + tb * 512 + 512],
                     2, evac_ql)

                def evac_qu(ob, tb, ps, mm_, ev_, j):
                    nc.vector.wait_ge(mm_.h, mm_.n)
                    nc.vector.tensor_copy(QU[:, ob * TOK + tb * 512: ob * TOK + tb * 512 + 512], ps)
                    g.inc(nc.vector.drain(), ev_)
                proj(i["l0b_wqu"], 8,
                     lambda c, tb: QL[:, c * TOK + tb * 512: c * TOK + tb * 512 + 512],
                     2, evac_qu, kc=2)

        stq = g.sem("stq")
        for ch in range(8):
            g.inc(nc.sync.dma_start(out=qstash[ch, :, :], in_=QU[:, ch * TOK:(ch + 1) * TOK]),
                  stq, 16)
        nc.sync.wait_ge(stq.h, stq.n)
        g.bar()
        esq2.close()

        es = ExitStack()
        COSQ = es.enter_context(nc.sbuf_tensor("COSQ_b", [64, TOK], F32))
        SINQ = es.enter_context(nc.sbuf_tensor("SINQ_b", [64, TOK], F32))
        COSK = es.enter_context(nc.sbuf_tensor("COSK_b", [64, S], F32))
        SINK = es.enter_context(nc.sbuf_tensor("SINK_b", [64, S], F32))
        LATG = es.enter_context(nc.sbuf_tensor("LATG_b", [128, 2 * S], F32))

        mm = g.sem("rmm")
        ev = g.sem("rev")
        sq = g.sem("rsq")
        with nc.sbuf_tensor("RF", [1, 64], F32) as RF, \
             nc.sbuf_tensor("TRW", [1, S], F32) as TRW:
            g.inc(nc.sync.dma_start(out=SC[0:1, 9:10], in_=i["tpos"][:, :]), ld, 16)
            nc.gpsimd.iota(RF[0:1, 0:32], pattern=[[1, 32]], base=0,
                           channel_multiplier=0, allow_small_or_imprecise_dtypes=True)
            nc.gpsimd.iota(TRW[0:1, :], pattern=[[1, S]], base=0,
                           channel_multiplier=0, allow_small_or_imprecise_dtypes=True)
            gp = g.sem("rgp")
            g.inc(nc.gpsimd.drain(), gp)
            nc.vector.wait_ge(gp.h, gp.n)
            nc.vector.tensor_scalar_mul(RF[0:1, 0:32], RF[0:1, 0:32],
                                        -2.0 / 64.0 * 9.210340371976184)
            g.inc(nc.vector.drain(), ev)
            nc.scalar.wait_ge(ev.h, ev.n)
            g.inc(nc.scalar.activation(RF[0:1, 0:32], RF[0:1, 0:32], AF.Exp), sq)
            nc.vector.wait_ge(sq.h, sq.n)

            def build_cs(cos_dst, sin_dst, pos_row, ncols, pref):
                mm2 = g.sem(pref + "mm")
                ev2 = g.sem(pref + "ev")
                sq2 = g.sem(pref + "sq")
                I32 = mybir.dt.int32
                with nc.sbuf_tensor(f"ITMP{g.uid()}", [32, 512], I32) as IT:
                    for tb in range(ncols // 512):
                        g.inc(nc.vector.drain(), mm2)
                        g.inc(nc.scalar.drain(), mm2)
                        nc.tensor.wait_ge(mm2.h, mm2.n)
                        g.inc(MM(pb[4][0:32, :], RF[0:1, 0:32],
                                               pos_row[0:1, tb * 512:(tb + 1) * 512],
                                               start=True, stop=True), mm2)
                        nc.vector.wait_ge(mm2.h, mm2.n)
                        for dst, shiftv in ((sin_dst, 0.0), (cos_dst, 0.25)):
                            wk = SQ[0:32, 0:512]
                            wk2 = SQ[0:32, 512:1024]
                            nc.vector.tensor_scalar(wk, pb[4][0:32, :],
                                                    1.0 / 6.283185307179586, shiftv,
                                                    op0=ALU.mult, op1=ALU.add)
                            nc.vector.drain()
                            nc.vector.tensor_copy(IT[:, :], wk)
                            nc.vector.drain()
                            nc.vector.tensor_copy(wk2, IT[:, :])
                            nc.vector.drain()
                            nc.vector.tensor_tensor(wk, wk, wk2, op=ALU.subtract)
                            nc.vector.drain()
                            nc.vector.tensor_scalar(wk2, wk, 0.5, None, op0=ALU.is_ge)
                            nc.vector.drain()
                            nc.vector.scalar_tensor_tensor(wk, wk2, -1.0, wk,
                                                           op0=ALU.mult, op1=ALU.add)
                            nc.vector.drain()
                            nc.vector.tensor_scalar(wk2, wk, -0.5, None, op0=ALU.is_lt)
                            nc.vector.drain()
                            nc.vector.scalar_tensor_tensor(wk, wk2, 1.0, wk,
                                                           op0=ALU.mult, op1=ALU.add)
                            nc.vector.drain()
                            nc.vector.tensor_scalar_mul(wk, wk, 6.283185307179586)
                            g.inc(nc.vector.drain(), ev2)
                            nc.scalar.wait_ge(ev2.h, ev2.n)
                            g.inc(nc.scalar.activation(dst[0:32, tb * 512:tb * 512 + 512],
                                                       wk, AF.Sin), sq2)
                            nc.vector.wait_ge(sq2.h, sq2.n)
                        g.inc(nc.vector.drain(), mm2)
                        nc.tensor.wait_ge(mm2.h, mm2.n)
                nc.vector.tensor_copy(cos_dst[32:64, 0:ncols], cos_dst[0:32, 0:ncols])
                nc.vector.drain()
                nc.vector.tensor_copy(sin_dst[32:64, 0:ncols], sin_dst[0:32, 0:ncols])
                nc.vector.drain()
                nc.vector.tensor_scalar_mul(sin_dst[0:32, 0:ncols], sin_dst[0:32, 0:ncols],
                                            -1.0)
                nc.vector.drain()

            with nc.sbuf_tensor("TRQ", [1, TOK], F32) as TRQ:
                nc.vector.wait_ge(ld.h, ld.n)
                nc.vector.tensor_scalar(TRQ[0:1, :], TRW[0:1, 0:TOK], SC[0:1, 9:10], None,
                                        op0=ALU.add)
                nc.vector.drain()
                build_cs(COSQ, SINQ, TRQ, TOK, "q")
            build_cs(COSK, SINK, TRW, S, "k")
        nc.vector.tensor_scalar_mul(COSQ[0:64, :], COSQ[0:64, :], 0.125)
        nc.vector.drain()
        nc.vector.tensor_scalar_mul(SINQ[0:64, :], SINQ[0:64, :], 0.125)
        nc.vector.drain()
        g.bar()

        mm = g.sem("tmm")
        ev = g.sem("tev")
        ld2 = g.sem("tld")
        nc.sync.wait_ge(ccs.h, ccs.n)
        with nc.sbuf_tensor("LTT", [128, 2 * 256], F32) as LTT:
            for tt in range(32):
                buf = LTT[:, (tt % 2) * 256:(tt % 2) * 256 + 256]
                if tt >= 2:
                    nc.sync.wait_ge(mm.h, mm.n - 3)
                g.inc(nc.sync.dma_start(out=buf, in_=cc_lat_out[tt * 128:(tt + 1) * 128, :]),
                      ld2, 16)
                nc.tensor.wait_ge(ld2.h, ld2.off + 16 * (tt + 1))
                for c2 in range(2):
                    TR(pb[2][:, 0:128], buf[:, c2 * 128:(c2 + 1) * 128],
                                        IDENT[:, :])
                    g.inc(g.lastT, mm)
                    nc.vector.wait_ge(mm.h, mm.n)
                    nc.vector.tensor_copy(LATG[:, c2 * S + tt * 128: c2 * S + tt * 128 + 128],
                                          pb[2][:, 0:128])
                    g.inc(nc.vector.drain(), ev)
                    nc.tensor.wait_ge(ev.h, ev.n)
        g.bar()

        with nc.sbuf_tensor("KH_b", [64, S], F32) as KH3, \
             nc.sbuf_tensor("VH_b", [128, 32 * 130], F32) as VH3, \
             nc.sbuf_tensor("WKU", [128, 2 * 128], F32) as WKU, \
             nc.sbuf_tensor("WVU", [128, 2 * 128], F32) as WVU, \
             nc.sbuf_tensor("RSK", [64, 512], F32) as RSK, \
             nc.sbuf_tensor("QHH_b", [64, TOK], F32) as QHH, \
             nc.sbuf_tensor("QSH_b", [64, TOK], F32) as QSH:
            PT = EV
            for kt in range(32):
                nc.vector.memset(VH3[:, kt * 130 + 64:kt * 130 + 65], 1.0)
                nc.vector.memset(VH3[:, kt * 130 + 129:kt * 130 + 130], 1.0)
            nc.vector.drain()
            g.bar()
            hl = g.sem("fhl")
            mm = g.sem("fmm3")
            mo = g.sem("fmo3")
            sq = g.sem("fsq3")
            ev = g.sem("fev3")
            st = g.sem("fst3")
            for h in range(16):
                rbase = 64 * (h % 2)
                cch = h // 2
                g.inc(nc.sync.dma_start(out=WKU[:, :],
                                        in_=i["l0b_wku"][:, (h // 2) * 256:(h // 2 + 1) * 256]),
                      hl, 16)
                g.inc(nc.sync.dma_start(out=WVU[:, :],
                                        in_=i["l0b_wvu"][:, h * 256:(h + 1) * 256]), hl, 16)
                g.inc(nc.sync.dma_start(out=QHH[:, :], in_=qstash[cch, rbase:rbase + 64, :]),
                      hl, 16)
                nc.tensor.wait_ge(hl.h, hl.n)
                for tb in range(8):
                    psb = pb[tb % 2]
                    if tb >= 2:
                        nc.tensor.wait_ge(ev.h, ev.n - 1)
                    for c2 in range(2):
                        MM(psb[0:64, :],
                                         WKU[:, c2 * 128 + 64 * (h % 2):
                                             c2 * 128 + 64 * (h % 2) + 64],
                                         LATG[:, c2 * S + tb * 512: c2 * S + tb * 512 + 512],
                                         start=(c2 == 0), stop=(c2 == 1))
                    g.inc(g.lastT, mm)
                    nc.vector.wait_ge(mm.h, mm.n)
                    nc.vector.tensor_copy(KH3[:, tb * 512:(tb + 1) * 512], psb[0:64, :])
                    g.inc(nc.vector.drain(), ev)
                for tt in range(32):
                    psb = pb[2 + (tt % 2) * 3]
                    if tt >= 2:
                        nc.tensor.wait_ge(ev.h, ev.n - 1)
                    for c2 in range(2):
                        MM(psb[:, 0:128],
                                         LATG[:, c2 * S + tt * 128: c2 * S + tt * 128 + 128],
                                         WVU[:, c2 * 128:(c2 + 1) * 128],
                                         start=(c2 == 0), stop=(c2 == 1))
                    g.inc(g.lastT, mm)
                    nc.vector.wait_ge(mm.h, mm.n)
                    nc.vector.tensor_copy(VH3[:, tt * 130:tt * 130 + 64], psb[:, 0:64])
                    nc.vector.drain()
                    nc.vector.tensor_copy(VH3[:, tt * 130 + 65:tt * 130 + 129], psb[:, 64:128])
                    g.inc(nc.vector.drain(), ev)
                g.bar()
                nc.vector.tensor_copy(QSH[0:32, :], QHH[32:64, :])
                nc.vector.drain()
                nc.vector.tensor_copy(QSH[32:64, :], QHH[0:32, :])
                nc.vector.drain()
                nc.vector.tensor_tensor(QHH[:, :], QHH[:, :], COSQ[0:64, :], op=ALU.mult)
                nc.vector.drain()
                nc.vector.tensor_tensor(QSH[:, :], QSH[:, :], SINQ[0:64, :], op=ALU.mult)
                nc.vector.drain()
                nc.vector.tensor_tensor(QHH[:, :], QHH[:, :], QSH[:, :], op=ALU.add)
                nc.vector.drain()
                for tb in range(8):
                    ksl = KH3[:, tb * 512:(tb + 1) * 512]
                    csl = COSK[0:64, tb * 512:(tb + 1) * 512]
                    ssl = SINK[0:64, tb * 512:(tb + 1) * 512]
                    nc.vector.tensor_copy(RSK[0:32, :], ksl[32:64, :])
                    nc.vector.drain()
                    nc.vector.tensor_copy(RSK[32:64, :], ksl[0:32, :])
                    nc.vector.drain()
                    nc.vector.tensor_tensor(ksl[0:64, :], ksl[0:64, :], csl, op=ALU.mult)
                    nc.vector.drain()
                    nc.vector.tensor_tensor(RSK[0:64, :], RSK[0:64, :], ssl, op=ALU.mult)
                    nc.vector.drain()
                    nc.vector.tensor_tensor(ksl[0:64, :], ksl[0:64, :], RSK[0:64, :],
                                            op=ALU.add)
                    nc.vector.drain()
                g.inc(nc.vector.drain(), ev)
                kn = g.sem(f"kn{h}")
                nc.vector.memset(SC[0:1, 10:11], 0.0)
                nc.vector.memset(SC[0:1, 13:14], 0.0)
                g.inc(nc.vector.drain(), kn)
                nc.scalar.wait_ge(ev.h, ev.n)
                for src_, ncols_, slot in ((KH3, 8, 10), (QHH, 2, 13)):
                    for tb in range(ncols_):
                        buf = SQ[0:64, (tb % 2) * 512:(tb % 2) * 512 + 512]
                        if tb >= 2 or slot == 13:
                            nc.scalar.wait_ge(kn.h, kn.n - 1)
                        g.inc(nc.scalar.activation(buf, src_[0:64, tb * 512:(tb + 1) * 512],
                                                   AF.Square), sq)
                        nc.tensor.wait_ge(sq.h, sq.n)
                        nc.tensor.wait_ge(kn.h, kn.n)
                        g.inc(MM(pst[0][0:1, :], ones[0:64, :], buf,
                                               start=True, stop=True), mm)
                        nc.vector.wait_ge(mm.h, mm.n)
                        nc.vector.tensor_reduce(SC[0:1, 11:12], pst[0][0:1, :], axis=AXL.X,
                                                op=ALU.max)
                        nc.vector.drain()
                        nc.vector.tensor_tensor(SC[0:1, slot:slot + 1], SC[0:1, slot:slot + 1],
                                                SC[0:1, 11:12], op=ALU.max)
                        g.inc(nc.vector.drain(), kn)
                nc.vector.tensor_tensor(SC[0:1, 12:13], SC[0:1, 10:11], SC[0:1, 13:14],
                                        op=ALU.mult)
                g.inc(nc.vector.drain(), ev)
                nc.scalar.wait_ge(ev.h, ev.n)
                g.inc(nc.scalar.activation(SC[0:1, 12:13], SC[0:1, 12:13], AF.Sqrt), sq)
                nc.vector.wait_ge(sq.h, sq.n)
                nc.vector.tensor_scalar_mul(SC[0:1, 12:13], SC[0:1, 12:13], -1.0)
                nc.vector.drain()
                scalar_bcast(SC[0:1, 12:13], CREP[:, 0:1], 128)
                g.bar()

                for qb in range(2):
                    for kt in range(32):
                        psb = pb[kt % 2]
                        if kt >= 2:
                            nc.tensor.wait_ge(sq.h, sq.n - 1)
                        MM(psb[:, :], KH3[:, kt * 128:(kt + 1) * 128],
                                         QHH[:, qb * 512:(qb + 1) * 512],
                                         start=True, stop=True)
                        g.inc(g.lastT, mm)
                        ptile = PT[:, (kt % 2) * 512:(kt % 2) * 512 + 512]
                        nc.scalar.wait_ge(mm.h, mm.n)
                        if kt >= 2:
                            nc.scalar.wait_ge(mo.h, mo.n - 2)
                        g.inc(nc.scalar.activation(ptile, psb[:, :], AF.Exp,
                                                   bias=CREP[:, 0:1]), sq)
                        nc.tensor.wait_ge(sq.h, sq.n)
                        g.inc(MM(pb[3][0:65, :],
                                               VH3[:, kt * 130:kt * 130 + 65],
                                               ptile, start=(kt == 0), stop=(kt == 31),
                                               skip_group_check=True), mo)
                        g.inc(MM(pb[4][0:65, :],
                                               VH3[:, kt * 130 + 65:kt * 130 + 130],
                                               ptile, start=(kt == 0), stop=(kt == 31),
                                               skip_group_check=True), mo)
                    nc.vector.wait_ge(mo.h, mo.n)
                    nc.vector.reciprocal(RECIP[0:1, :], pb[3][64:65, :])
                    g.inc(nc.vector.drain(), ev)
                    nc.tensor.wait_ge(ev.h, ev.n)
                    g.inc(MM(pb[5][0:64, :], ONESR[0:1, 0:64],
                                           RECIP[0:1, :], start=True, stop=True), mm)
                    nc.vector.wait_ge(mm.h, mm.n)
                    nc.vector.tensor_copy(SQ[64:128, 0:512], pb[5][0:64, :])
                    nc.vector.drain()
                    obuf = EV[:, 0:512]
                    nc.vector.tensor_tensor(obuf[0:64, :], pb[3][0:64, :], SQ[64:128, 0:512],
                                            op=ALU.mult)
                    nc.vector.drain()
                    nc.vector.tensor_tensor(obuf[64:128, :], pb[4][0:64, :], SQ[64:128, 0:512],
                                            op=ALU.mult)
                    g.inc(nc.vector.drain(), ev)
                    nc.tensor.wait_ge(ev.h, ev.n)
                    nc.sync.wait_ge(ev.h, ev.n)
                    g.inc(nc.sync.dma_start(out=bstash[h, :, qb * 512:(qb + 1) * 512],
                                            in_=obuf), st, 16)
                    nc.scalar.wait_ge(st.h, st.n)
                g.bar()
        es.close()
        g.bar()

        with nc.sbuf_tensor("OST_b", [128, KC * TOK], F32) as OST:
            ld3 = g.sem("wold")
            g.inc(nc.sync.dma_start(
                out=OST[:].rearrange("p (c t) -> p c t", c=KC),
                in_=AP(bstash.tensor, 0, [[TOK, 128], [128 * TOK, 16], [1, TOK]])), ld3, 16)
            nc.sync.wait_ge(ld3.h, ld3.n)
            g.bar()

            def evac_res3(ob, tb, ps, mm_, ev_, j):
                nc.vector.wait_ge(mm_.h, mm_.n)
                sl = slice(ob * TOK + tb * 512, ob * TOK + tb * 512 + 512)
                nc.vector.tensor_tensor(X[:, sl], X[:, sl], ps, op=ALU.add)
                g.inc(nc.vector.drain(), ev_)

            proj(i["l0b_wo"], 16,
                 lambda c, tb: OST[:, c * TOK + tb * 512: c * TOK + tb * 512 + 512],
                 2, evac_res3)


    def s5_l1_attn_a():
        from contextlib import ExitStack
        ccs = g.sem("a1cc")
        with nc.sbuf_tensor("XN_5", [128, KC * TOK], F32) as XN5, \
             nc.sbuf_tensor("LST_5", [1, 2 * TOK], F32) as LST:
            ln_stats(lambda c, tb: X[:, c * TOK + tb * 512: c * TOK + tb * 512 + 512],
                     TOK, LST)
            ln_apply(lambda c, tb: X[:, c * TOK + tb * 512: c * TOK + tb * 512 + 512],
                     lambda c, tb: XN5[:, c * TOK + tb * 512: c * TOK + tb * 512 + 512],
                     TOK, LST)
            st = g.sem("a1st")
            g.inc(nc.sync.dma_start(
                out=AP(cc_xn_in.tensor, 0, [[TOK, 128], [128 * TOK, KC], [1, TOK]]),
                in_=XN5[:].rearrange("p (c t) -> p c t", c=KC)), st, 16)
            nc.gpsimd.wait_ge(st.h, st.n)
            g.inc(nc.gpsimd.collective_compute(
                "AllGather", ALU.bypass, replica_groups=RG,
                ins=[cc_xn_in[:, :]], outs=[cc_xn_out[:, :]]), ccs)
        g.bar()

        with nc.sbuf_tensor("WQ5", [128, 4 * KC * 128], F32) as WQ5, \
             nc.sbuf_tensor("WK5", [128, 4 * KC * 128], F32) as WK5, \
             nc.sbuf_tensor("WV5", [128, 4 * KC * 128], F32) as WV5, \
             nc.sbuf_tensor("WIL5", [128, KC * 8], F32) as WIL5, \
             nc.sbuf_tensor("XG5", [128, KC * 512], F32) as XG5:
            ld = g.sem("a1ld")
            mm = g.sem("a1mm")
            ev = g.sem("a1ev")
            st = g.sem("a1st2")
            xl = g.sem("a1xl")
            for wt, nm in ((WQ5, "a1_wq"), (WK5, "a1_wk"), (WV5, "a1_wv"),
                           (WIL5, "a1_wilr")):
                g.inc(nc.sync.dma_start(out=wt[:], in_=i[nm][:, :]), ld, 16)
            nc.sync.wait_ge(ccs.h, ccs.n)
            nc.tensor.wait_ge(ld.h, ld.n)
            for gt in range(8):
                Q_, tb = divmod(gt, 2)
                buf = XG5[:, 0:KC * 512]
                if gt >= 1:
                    nc.sync.wait_ge(mm.h, mm.off + gt * 13)
                g.inc(nc.sync.dma_start(
                    out=buf.rearrange("p (c t) -> p c t", c=KC),
                    in_=AP(cc_xn_out.tensor, Q_ * D * TOK + tb * 512,
                           [[TOK, 128], [128 * TOK, KC], [1, 512]])), xl, 16)
                nc.tensor.wait_ge(xl.h, xl.off + 16 * (gt + 1))
                for wt, dst in ((WQ5, xcfm), (WK5, xbfm), (WV5, xafm)):
                    for ob in range(4):
                        psb = pb[ob % 2]
                        nc.tensor.wait_ge(ev.h, max(ev.off, ev.n - 1))
                        for c in range(KC):
                            MM(
                                psb[:, :],
                                wt[:, (ob * KC + c) * 128:(ob * KC + c) * 128 + 128],
                                buf[:, c * 512:(c + 1) * 512],
                                start=(c == 0), stop=(c == KC - 1))
                        g.inc(g.lastT, mm)
                        nc.vector.wait_ge(mm.h, mm.n)
                        ebuf = EV[:, (ob % 2) * 512:(ob % 2) * 512 + 512]
                        nc.vector.tensor_copy(ebuf, psb[:, :])
                        g.inc(nc.vector.drain(), ev)
                        nc.sync.wait_ge(ev.h, ev.n)
                        g.inc(nc.sync.dma_start(
                            out=dst[ob, :, Q_ * TOK + tb * 512: Q_ * TOK + tb * 512 + 512],
                            in_=ebuf), st, 16)
                nc.tensor.wait_ge(ev.h, max(ev.off, ev.n - 1))
                for c in range(KC):
                    MM(pb[2][0:8, :], WIL5[:, c * 8:(c + 1) * 8],
                                     buf[:, c * 512:(c + 1) * 512],
                                     start=(c == 0), stop=(c == KC - 1))
                g.inc(g.lastT, mm)
                nc.vector.wait_ge(mm.h, mm.n)
                nc.vector.tensor_copy(EV[0:8, 0:512], pb[2][0:8, :])
                g.inc(nc.vector.drain(), ev)
                nc.sync.wait_ge(ev.h, ev.n)
                g.inc(nc.sync.dma_start(
                    out=ilrst[:, Q_ * TOK + tb * 512: Q_ * TOK + tb * 512 + 512],
                    in_=EV[0:8, 0:512]), st, 16)
            nc.sync.wait_ge(st.h, st.n)
        g.bar()

        with nc.sbuf_tensor("CN5", [128, 21 * 128], F32) as CN, \
             nc.sbuf_tensor("CL5", [128, 24], F32) as CL, \
             nc.sbuf_tensor("SB5", [128, 16 * 64], F32) as SB, \
             nc.sbuf_tensor("XGB5", [128, 2 * 4 * 1024], F32) as XGB, \
             nc.sbuf_tensor("XGC5", [128, 2 * 4 * 1024], F32) as XGC, \
             nc.sbuf_tensor("XGA5", [128, 2 * 4 * 1024], F32) as XGA, \
             nc.sbuf_tensor("XCW5", [128, 4 * 1024], F32) as XCW:
            ld = g.sem("a1ld2")
            mm = g.sem("a1mm2")
            ev = g.sem("a1ev2")
            sq = g.sem("a1sq2")
            st = g.sem("a1st3")
            cld = g.sem("a1cld")
            gp = g.sem("a1gp")
            W1ST = SB[:, 0:256]
            B1ST = SB[0:8, 256:320]
            LNW = SB[:, 5 * 64:6 * 64]
            LNB = SB[:, 6 * 64:7 * 64]
            XBT = SB[:, 7 * 64:8 * 64]
            XCT = SB[:, 8 * 64:9 * 64]
            TGT = SB[:, 9 * 64:10 * 64]
            ZC = SB[:, 10 * 64:11 * 64]
            XH = SB[:, 11 * 64:12 * 64]
            GG = SB[:, 12 * 64:13 * 64]
            GRD = SB[:, 13 * 64:14 * 64]
            GRDC = SB[:, 14 * 64:15 * 64]
            OUTT = SB[:, 15 * 64:16 * 64]
            DGB = CN[:, 12 * 128:16 * 128]
            TRILM = CN[:, 8 * 128:9 * 128]
            I2 = CN[:, 9 * 128:9 * 128 + 64]
            IOTC = CN[:, 10 * 128:11 * 128]
            TSEL = CN[:, 11 * 128:11 * 128 + 16]
            HSELT = CN[:, 11 * 128 + 16:11 * 128 + 24]
            SEL15 = CN[:, 11 * 128 + 24:11 * 128 + 32]
            HSEL2 = CN[0:8, 11 * 128 + 32:11 * 128 + 32 + 96][0:8, 0:0 + 0] if False else None
            for nm, col in (("idxcol", 0), ("pcol", 1), ("phcol", 2), ("pm16col", 3)):
                g.inc(nc.sync.dma_start(out=CL[:, col:col + 1], in_=i[nm][:, :]), ld, 16)
            g.inc(nc.sync.dma_start(out=CL[0:8, 4:5], in_=i["a1_bilr8"][:, :]), ld, 16)
            g.inc(nc.sync.dma_start(out=W1ST, in_=i["a1_w1init"][:, :]), ld, 16)
            g.inc(nc.sync.dma_start(out=B1ST, in_=i["a1_b1init"][:, :]), ld, 16)
            g.inc(nc.sync.dma_start(out=SB[0:8, 5 * 64:6 * 64], in_=i["a1_lnw"][:, :]), ld, 16)
            g.inc(nc.sync.dma_start(out=SB[0:8, 6 * 64:7 * 64], in_=i["a1_lnb"][:, :]), ld, 16)
            nc.vector.memset(CL[:, 5:6], 1e-6)
            nc.vector.wait_ge(ld.h, ld.n)
            for c in range(4):
                m3 = CN[:, c * 128:(c + 1) * 128]
                nc.vector.memset(m3, 0.0)
                nc.vector.drain()
                nc.vector.memset(m3[0:64, 16 * (2 * c):16 * (2 * c) + 16], 1.0)
                nc.vector.drain()
                nc.vector.memset(m3[64:128, 16 * (2 * c + 1):16 * (2 * c + 1) + 16], 1.0)
                nc.vector.drain()
            nc.gpsimd.iota(IOTC, pattern=[[1, 128]], base=0, channel_multiplier=0,
                           allow_small_or_imprecise_dtypes=True)
            g.inc(nc.gpsimd.drain(), gp)
            nc.vector.wait_ge(gp.h, gp.n)
            nc.vector.tensor_scalar(CL[:, 21:22], CL[:, 2:3], 16.0, 16.0,
                                    op0=ALU.mult, op1=ALU.add)
            nc.vector.drain()
            nc.vector.tensor_scalar(TRILM, IOTC, CL[:, 1:2], None, op0=ALU.is_ge)
            nc.vector.drain()
            nc.vector.tensor_scalar(CN[:, 9 * 128:10 * 128], IOTC, CL[:, 21:22], None,
                                    op0=ALU.is_lt)
            nc.vector.drain()
            nc.vector.tensor_tensor(TRILM, TRILM, CN[:, 9 * 128:10 * 128], op=ALU.mult)
            nc.vector.drain()
            HI64 = CN[:, 9 * 128:10 * 128]
            nc.vector.tensor_scalar(HI64, IOTC, 63.5, None, op0=ALU.is_ge)
            nc.vector.drain()
            for c in range(4):
                mo = CN[:, (4 + c) * 128:(5 + c) * 128]
                nc.vector.tensor_scalar(mo, HI64, float(2 * c), None, op0=ALU.add)
                nc.vector.drain()
                nc.vector.tensor_scalar(mo, mo, CL[:, 2:3], None, op0=ALU.is_equal)
                nc.vector.drain()
            nc.vector.tensor_copy(I2[0:64, :], IDENT[0:64, 0:64])
            nc.vector.drain()
            nc.vector.tensor_copy(I2[64:128, :], IDENT[0:64, 0:64])
            nc.vector.drain()
            nc.vector.tensor_scalar(TSEL, IOTC[:, 0:16], CL[:, 3:4], None, op0=ALU.is_equal)
            nc.vector.drain()
            nc.vector.tensor_scalar(HSELT, IOTC[:, 0:8], CL[:, 2:3], None, op0=ALU.is_equal)
            nc.vector.drain()
            nc.vector.tensor_scalar(SEL15, IOTC[:, 0:8], 16.0, 15.0, op0=ALU.mult,
                                    op1=ALU.add)
            nc.vector.drain()
            nc.vector.tensor_scalar(SEL15, SEL15, CL[:, 1:2], None, op0=ALU.is_equal)
            nc.vector.drain()
            BM16 = CN[:, 16 * 128:17 * 128]
            PHJ = CN[:, 17 * 128:17 * 128 + 8]
            RAW8 = CN[0:8, 18 * 128:19 * 128]
            SIG8 = CN[0:8, 17 * 128 + 16:17 * 128 + 16 + 128]
            TOKT = CN[:, 18 * 128 + 64:18 * 128 + 72]
            COTMP = CN[:, 19 * 128:20 * 128]
            nc.gpsimd.iota(BM16, pattern=[[0, 8], [1, 16]], base=0, channel_multiplier=0,
                           allow_small_or_imprecise_dtypes=True)
            g.inc(nc.gpsimd.drain(), gp)
            nc.vector.wait_ge(gp.h, gp.n)
            nc.vector.tensor_scalar(BM16, BM16, CL[:, 3:4], None, op0=ALU.is_equal)
            nc.vector.drain()
            nc.vector.tensor_scalar(PHJ, IOTC[:, 0:8], CL[:, 2:3], None, op0=ALU.is_equal)
            nc.vector.drain()
            HSEL2 = CN[0:8, 9 * 128 + 64:9 * 128 + 64 + 128][0:8, :] if False else None
            HSEL2 = CN[0:8, 20 * 128:21 * 128]
            nc.gpsimd.iota(HSEL2, pattern=[[1, 128]], base=0, channel_multiplier=-16,
                           allow_small_or_imprecise_dtypes=True)
            g.inc(nc.gpsimd.drain(), gp)
            nc.vector.wait_ge(gp.h, gp.n)
            nc.vector.tensor_scalar(CN[0:8, 12 * 128:13 * 128], HSEL2, -0.5, None,
                                    op0=ALU.is_ge)
            nc.vector.drain()
            nc.vector.tensor_scalar(HSEL2, HSEL2, 15.5, None, op0=ALU.is_le)
            nc.vector.drain()
            nc.vector.tensor_tensor(HSEL2, HSEL2, CN[0:8, 12 * 128:13 * 128], op=ALU.mult)
            g.inc(nc.vector.drain(), ev)
            nc.tensor.wait_ge(ev.h, ev.n)
            g.inc(MM(pb[0][:, 0:64], HSEL2, SB[0:8, 5 * 64:6 * 64],
                                   start=True, stop=True), mm)
            g.inc(MM(pb[1][:, 0:64], HSEL2, SB[0:8, 6 * 64:7 * 64],
                                   start=True, stop=True), mm)
            nc.vector.wait_ge(mm.h, mm.n)
            nc.vector.tensor_copy(LNW, pb[0][:, 0:64])
            nc.vector.drain()
            nc.vector.tensor_copy(LNB, pb[1][:, 0:64])
            nc.vector.drain()
            g.bar()

            for grp in range(4):
                gbuf = grp % 2
                for dstb, srct in ((XGB, xbfm), (XGC, xcfm), (XGA, xafm)):
                    g.inc(nc.sync.dma_start(
                        out=dstb[:, gbuf * 4096:(gbuf + 1) * 4096].rearrange(
                            "p (c t) -> p c t", c=4),
                        in_=AP(srct.tensor, grp * 1024,
                               [[S, 128], [128 * S, 4], [1, 1024]])), cld, 16)
                nc.sync.wait_ge(cld.h, cld.n)
                g.bar()
                for nn in range(64):
                    n = grp * 64 + nn
                    cof = nn % 8
                    if cof == 0:
                        nc.sync.wait_ge(sq.h, sq.n)
                        g.inc(nc.sync.dma_start(out=RAW8,
                                                in_=ilrst[:, n * 16:n * 16 + 128]), cld, 16)
                        nc.scalar.wait_ge(cld.h, cld.n)
                        nc.scalar.wait_ge(mm.h, mm.n)
                        g.inc(nc.scalar.activation(SIG8, RAW8, AF.Sigmoid,
                                                   bias=CL[0:8, 4:5]), sq)
                        nc.tensor.wait_ge(sq.h, sq.n)
                        TR(pb[3][:, 2:10], SIG8, IDENT[0:8, 0:8])
                        g.inc(g.lastT, mm)
                        nc.vector.wait_ge(mm.h, mm.n)
                        nc.vector.tensor_copy(TOKT, pb[3][:, 2:10])
                        g.inc(nc.vector.drain(), ev)
                        nc.tensor.wait_ge(ev.h, ev.n)
                    nc.vector.tensor_tensor(
                        COTMP,
                        AP(TOKT.tensor, TOKT.offset, [list(TOKT.ap[0]), [1, 8], [0, 16]]),
                        BM16, op=ALU.mult)
                    nc.vector.drain()
                    nc.vector.tensor_scalar(COTMP, COTMP, PHJ[:, cof:cof + 1], None,
                                            op0=ALU.mult)
                    g.inc(nc.vector.drain(), ev)
                    nc.tensor.wait_ge(ev.h, ev.n)
                    g.inc(MM(pb[3][:, 10:11], COTMP, ones[0:128, 0:1],
                                           start=True, stop=True,
                                           skip_group_check=True), mm)
                    nc.vector.wait_ge(mm.h, mm.n)
                    nc.vector.tensor_scalar(CL[:, 10:11] if False else CL[:, 12:13],
                                            pb[3][:, 10:11], CL[:, 0:1], None, op0=ALU.mult)
                    g.inc(nc.vector.drain(), ev)
                    co_col = CL[:, 12:13]

                    def xsl(bufT, c):
                        base = gbuf * 4096 + c * 1024 + nn * 16
                        return bufT[:, base:base + 16]

                    def xbc(bufT, c):
                        s_ = xsl(bufT, c)
                        return AP(s_.tensor, s_.offset, [list(s_.ap[0]), [0, 8], [1, 16]])

                    for c in range(4):
                        nc.vector.tensor_tensor(DGB[:, c * 128:(c + 1) * 128],
                                                xbc(XGB, c), CN[:, c * 128:(c + 1) * 128],
                                                op=ALU.mult)
                        nc.vector.drain()
                    g.inc(nc.vector.drain(), ev)
                    nc.tensor.wait_ge(ev.h, ev.n)
                    for c in range(4):
                        dg = DGB[:, c * 128:(c + 1) * 128]
                        MM(pb[0][:, 0:64], dg, W1ST[:, c * 64:(c + 1) * 64],
                                         start=(c == 0), stop=False, skip_group_check=True)
                        MM(pb[4][:, 0:16], dg, xsl(XGC, c),
                                         start=(c == 0), stop=(c == 3),
                                         skip_group_check=True)
                        MM(pb[1][:, 0:64], dg, I2,
                                         start=(c == 0), stop=(c == 3),
                                         skip_group_check=True)
                    MM(pb[0][:, 0:64], HSEL2, B1ST, start=False, stop=True,
                                     skip_group_check=True)
                    g.inc(g.lastT, mm)
                    nc.vector.wait_ge(mm.h, mm.n)
                    for c in range(4):
                        nc.vector.tensor_tensor(DGB[:, c * 128:(c + 1) * 128],
                                                xbc(XGC, c), CN[:, c * 128:(c + 1) * 128],
                                                op=ALU.mult)
                        nc.vector.drain()
                    g.inc(nc.vector.drain(), ev)
                    nc.tensor.wait_ge(ev.h, ev.n)
                    for c in range(4):
                        dg = DGB[:, c * 128:(c + 1) * 128]
                        MM(pb[1][:, 64:128], dg, I2,
                                         start=(c == 0), stop=(c == 3),
                                         skip_group_check=True)
                        MM(pb[0][:, 128:192], dg,
                                         W1ST[:, c * 64:(c + 1) * 64],
                                         start=(c == 0), stop=False,
                                         skip_group_check=True)
                    MM(pb[0][:, 128:192], HSEL2, B1ST, start=False,
                                     stop=True, skip_group_check=True)
                    g.inc(g.lastT, mm)
                    nc.vector.wait_ge(mm.h, mm.n)
                    for c in range(4):
                        nc.vector.tensor_tensor(DGB[:, c * 128:(c + 1) * 128],
                                                xbc(XGA, c), CN[:, c * 128:(c + 1) * 128],
                                                op=ALU.mult)
                        nc.vector.drain()
                    g.inc(nc.vector.drain(), ev)
                    nc.tensor.wait_ge(ev.h, ev.n)
                    for c in range(4):
                        MM(pb[1][:, 128:192], DGB[:, c * 128:(c + 1) * 128],
                                         I2, start=(c == 0), stop=(c == 3),
                                         skip_group_check=True)
                    g.inc(g.lastT, mm)
                    nc.vector.wait_ge(mm.h, mm.n)
                    nc.vector.tensor_copy(XBT, pb[1][:, 0:64])
                    nc.vector.drain()
                    nc.vector.tensor_copy(XCT, pb[1][:, 64:128])
                    nc.vector.drain()
                    nc.vector.tensor_tensor(TGT, pb[1][:, 128:192], XBT, op=ALU.subtract)
                    nc.vector.drain()
                    nc.vector.tensor_reduce(CL[:, 6:7], pb[0][:, 0:64], axis=AXL.X,
                                            op=ALU.add)
                    nc.vector.drain()
                    nc.vector.tensor_scalar_mul(CL[:, 6:7], CL[:, 6:7], 1.0 / 64.0)
                    nc.vector.drain()
                    nc.vector.tensor_scalar(ZC, pb[0][:, 0:64], CL[:, 6:7], None,
                                            op0=ALU.subtract)
                    g.inc(nc.vector.drain(), ev)
                    nc.scalar.wait_ge(ev.h, ev.n)
                    g.inc(nc.scalar.activation(XH, ZC, AF.Square, accum_out=CL[:, 11:12]),
                          sq)
                    nc.vector.wait_ge(sq.h, sq.n)
                    nc.vector.tensor_scalar_mul(CL[:, 11:12], CL[:, 11:12], 1.0 / 64.0)
                    g.inc(nc.vector.drain(), ev)
                    nc.scalar.wait_ge(ev.h, ev.n)
                    g.inc(nc.scalar.activation(CL[:, 7:8], CL[:, 11:12], AF.Sqrt,
                                               bias=CL[:, 5:6]), sq)
                    nc.vector.wait_ge(sq.h, sq.n)
                    nc.vector.reciprocal(CL[:, 7:8], CL[:, 7:8])
                    nc.vector.drain()
                    nc.vector.tensor_scalar(XH, ZC, CL[:, 7:8], None, op0=ALU.mult)
                    nc.vector.drain()
                    nc.vector.tensor_tensor(GG, XH, LNW, op=ALU.mult)
                    nc.vector.drain()
                    nc.vector.tensor_tensor(GG, GG, LNB, op=ALU.add)
                    nc.vector.drain()
                    nc.vector.tensor_tensor(GG, GG, TGT, op=ALU.subtract)
                    nc.vector.drain()
                    nc.vector.tensor_tensor(GG, GG, LNW, op=ALU.mult)
                    nc.vector.drain()
                    nc.vector.tensor_scalar(GRD, GG, 1.0, 0.0, op0=ALU.mult,
                                            op1=ALU.add, accum_out=CL[:, 8:9])
                    nc.vector.drain()
                    nc.vector.tensor_tensor(GRDC, GG, XH, op=ALU.mult)
                    nc.vector.drain()
                    nc.vector.tensor_scalar(GRDC, GRDC, 1.0, 0.0, op0=ALU.mult,
                                            op1=ALU.add, accum_out=CL[:, 9:10])
                    nc.vector.drain()
                    nc.vector.tensor_tensor(CL[:, 11:12], CL[:, 8:9], CL[:, 7:8],
                                            op=ALU.mult)
                    nc.vector.drain()
                    nc.vector.tensor_scalar_mul(CL[:, 8:9], CL[:, 11:12], 1.0 / 64.0)
                    nc.vector.drain()
                    nc.vector.tensor_tensor(CL[:, 11:12], CL[:, 9:10], CL[:, 7:8],
                                            op=ALU.mult)
                    nc.vector.drain()
                    nc.vector.tensor_scalar_mul(CL[:, 9:10], CL[:, 11:12], 1.0 / 64.0)
                    nc.vector.drain()
                    nc.vector.tensor_scalar(GRD, GG, CL[:, 7:8], None, op0=ALU.mult)
                    nc.vector.drain()
                    nc.vector.tensor_scalar(GRDC, XH, CL[:, 9:10], None, op0=ALU.mult)
                    nc.vector.drain()
                    nc.vector.tensor_tensor(GRD, GRD, GRDC, op=ALU.subtract)
                    nc.vector.drain()
                    nc.vector.tensor_scalar(GRD, GRD, CL[:, 8:9], None, op0=ALU.subtract)
                    nc.vector.drain()
                    nc.vector.tensor_tensor(
                        DGB[:, 0:128],
                        AP(pb[4].ap().tensor, pb[4].ap().offset,
                           [list(pb[4].ap().ap[0]), [0, 8], [1, 16]]),
                        TRILM, op=ALU.mult)
                    g.inc(nc.vector.drain(), ev)
                    nc.tensor.wait_ge(ev.h, ev.n)
                    MM(pb[0][:, 192:256], DGB[:, 0:128], GRD,
                                     start=True, stop=False, skip_group_check=True)
                    MM(pb[0][:, 192:256], TRILM, GRD,
                                     start=False, stop=True, skip_group_check=True)
                    g.inc(g.lastT, mm)
                    nc.vector.wait_ge(mm.h, mm.n)
                    nc.vector.tensor_scalar(XH, pb[0][:, 192:256], co_col, None,
                                            op0=ALU.mult)
                    nc.vector.drain()
                    nc.vector.tensor_tensor(ZC, pb[0][:, 128:192], XH, op=ALU.subtract)
                    nc.vector.drain()
                    nc.vector.tensor_reduce(CL[:, 6:7], ZC, axis=AXL.X, op=ALU.add)
                    nc.vector.drain()
                    nc.vector.tensor_scalar_mul(CL[:, 6:7], CL[:, 6:7], 1.0 / 64.0)
                    nc.vector.drain()
                    nc.vector.tensor_scalar(ZC, ZC, CL[:, 6:7], None, op0=ALU.subtract)
                    g.inc(nc.vector.drain(), ev)
                    nc.scalar.wait_ge(ev.h, ev.n)
                    g.inc(nc.scalar.activation(XH, ZC, AF.Square, accum_out=CL[:, 11:12]),
                          sq)
                    nc.vector.wait_ge(sq.h, sq.n)
                    nc.vector.tensor_scalar_mul(CL[:, 11:12], CL[:, 11:12], 1.0 / 64.0)
                    g.inc(nc.vector.drain(), ev)
                    nc.scalar.wait_ge(ev.h, ev.n)
                    g.inc(nc.scalar.activation(CL[:, 7:8], CL[:, 11:12], AF.Sqrt,
                                               bias=CL[:, 5:6]), sq)
                    nc.vector.wait_ge(sq.h, sq.n)
                    nc.vector.reciprocal(CL[:, 7:8], CL[:, 7:8])
                    nc.vector.drain()
                    nc.vector.tensor_scalar(XH, ZC, CL[:, 7:8], None, op0=ALU.mult)
                    nc.vector.drain()
                    nc.vector.tensor_tensor(OUTT, XH, LNW, op=ALU.mult)
                    nc.vector.drain()
                    nc.vector.tensor_tensor(OUTT, OUTT, LNB, op=ALU.add)
                    nc.vector.drain()
                    nc.vector.tensor_tensor(OUTT, OUTT, XCT, op=ALU.add)
                    nc.vector.drain()
                    for c in range(4):
                        nc.vector.tensor_tensor(
                            DGB[:, c * 128:(c + 1) * 128],
                            AP(OUTT.tensor, OUTT.offset,
                               [list(OUTT.ap[0]), [0, 2], [1, 64]]),
                            CN[:, (4 + c) * 128:(5 + c) * 128], op=ALU.mult)
                        nc.vector.drain()
                    g.inc(nc.vector.drain(), ev)
                    nc.tensor.wait_ge(ev.h, ev.n)
                    for c in range(4):
                        MM(pb[1][:, 192 + 16 * c:208 + 16 * c],
                                         DGB[:, c * 128:(c + 1) * 128], TSEL,
                                         start=True, stop=True, skip_group_check=True)
                    MM(pb[3][0:8, 0:1], SEL15, co_col, start=True,
                                     stop=True, skip_group_check=True)
                    g.inc(g.lastT, mm)
                    nc.vector.wait_ge(mm.h, mm.n)
                    for c in range(4):
                        nc.vector.tensor_copy(
                            XCW[:, c * 1024 + nn * 16: c * 1024 + nn * 16 + 16],
                            pb[1][:, 192 + 16 * c:208 + 16 * c])
                        nc.vector.drain()
                    nc.vector.tensor_copy(CL[0:8, 20:21], pb[3][0:8, 0:1])
                    g.inc(nc.vector.drain(), ev)
                    nc.tensor.wait_ge(ev.h, ev.n)
                    MM(pb[3][:, 1:2], HSEL2, CL[0:8, 20:21],
                                     start=True, stop=True, skip_group_check=True)
                    g.inc(g.lastT, mm)
                    nc.vector.wait_ge(mm.h, mm.n)
                    nc.vector.tensor_copy(CL[:, 10:11], pb[3][:, 1:2])
                    nc.vector.drain()
                    nc.vector.tensor_scalar(GRDC, GRD, CL[:, 10:11], None, op0=ALU.mult)
                    nc.vector.drain()
                    for c in range(4):
                        nc.vector.tensor_tensor(
                            DGB[:, c * 128:(c + 1) * 128],
                            AP(XBT.tensor, XBT.offset,
                               [list(XBT.ap[0]), [0, 2], [1, 64]]),
                            CN[:, (4 + c) * 128:(5 + c) * 128], op=ALU.mult)
                        nc.vector.drain()
                    g.inc(nc.vector.drain(), ev)
                    nc.tensor.wait_ge(ev.h, ev.n)
                    for c in range(4):
                        MM(pb[2][:, c * 64:(c + 1) * 64],
                                         DGB[:, c * 128:(c + 1) * 128], GRDC,
                                         start=True, stop=True, skip_group_check=True)
                    MM(pb[3][0:8, 72:136], HSELT, GRDC,
                                     start=True, stop=True, skip_group_check=True)
                    g.inc(g.lastT, mm)
                    nc.vector.wait_ge(mm.h, mm.n)
                    for c in range(4):
                        nc.vector.tensor_tensor(W1ST[:, c * 64:(c + 1) * 64],
                                                W1ST[:, c * 64:(c + 1) * 64],
                                                pb[2][:, c * 64:(c + 1) * 64],
                                                op=ALU.subtract)
                        nc.vector.drain()
                    nc.vector.tensor_tensor(B1ST, B1ST, pb[3][0:8, 72:136], op=ALU.subtract)
                    g.inc(nc.vector.drain(), ev)
                    nc.tensor.wait_ge(ev.h, ev.n)
                    if DEBUG and n == 8:
                        g.bar()
                        dsb = g.sem("a1sbd")
                        g.inc(nc.sync.dma_start(out=sbdump[:, 0:16 * 64], in_=SB[:]), dsb, 16)
                        g.inc(nc.sync.dma_start(out=sbdump[:, 16 * 64:], in_=CL[:]), dsb, 16)
                        g.inc(nc.sync.dma_start(out=cndump[:, :],
                                                in_=CN[:, 16 * 128:20 * 128]), dsb, 16)
                        nc.sync.wait_ge(dsb.h, dsb.n)
                        g.bar()
                nc.sync.wait_ge(ev.h, ev.n)
                g.inc(nc.sync.dma_start(
                    out=AP(cc_s_in.tensor, grp * 1024,
                           [[S, 128], [128 * S, 4], [1, 1024]]),
                    in_=XCW[:].rearrange("p (c t) -> p c t", c=4)), st, 16)
                nc.sync.wait_ge(st.h, st.n)
            g.bar()

        if DEBUG:
            dsem = g.sem("a1dbg")
            g.inc(nc.sync.dma_start(out=scandump[:, :], in_=cc_s_in[:, :]), dsem, 16)
            nc.sync.wait_ge(dsem.h, dsem.n)
            g.bar()
        ccs2 = g.sem("a1cc2")
        g.inc(nc.gpsimd.collective_compute(
            "AllGather", ALU.bypass, replica_groups=RG,
            ins=[cc_s_in[:, :]], outs=[cc_s_out[:, :]]), ccs2)
        with nc.sbuf_tensor("OST_5", [128, KC * TOK], F32) as OST, \
             nc.sbuf_tensor("QSL_5", [128, 4], F32) as QSL, \
             nc.sbuf_tensor("OC4_5", [128, 4 * 1024], F32) as OC4:
            ld4 = g.sem("a1ld4")
            ev4 = g.sem("a1ev4")
            g.inc(nc.sync.dma_start(out=QSL[:, :], in_=i["qselcol"][:, :]), ld4, 16)
            nc.sync.wait_ge(ccs2.h, ccs2.n)
            for c in range(KC):
                if c >= 1:
                    nc.sync.wait_ge(ev4.h, ev4.n)
                for k in range(4):
                    g.inc(nc.sync.dma_start(
                        out=OC4[:, k * 1024:(k + 1) * 1024],
                        in_=cc_s_out[c * 128:(c + 1) * 128, k * 1024:(k + 1) * 1024]),
                        ld4, 16)
                nc.vector.wait_ge(ld4.h, ld4.n)
                nc.vector.tensor_scalar(OST[:, c * TOK:(c + 1) * TOK],
                                        OC4[:, 0:1024], QSL[:, 0:1], None, op0=ALU.mult)
                nc.vector.drain()
                for k in range(1, 4):
                    nc.vector.scalar_tensor_tensor(OST[:, c * TOK:(c + 1) * TOK],
                                                   OC4[:, k * 1024:(k + 1) * 1024],
                                                   QSL[:, k:k + 1],
                                                   OST[:, c * TOK:(c + 1) * TOK],
                                                   op0=ALU.mult, op1=ALU.add)
                    nc.vector.drain()
                g.inc(nc.vector.drain(), ev4)
            g.bar()

            def evac_res5(ob, tb, ps, mm_, ev_, j):
                nc.vector.wait_ge(mm_.h, mm_.n)
                sl = slice(ob * TOK + tb * 512, ob * TOK + tb * 512 + 512)
                nc.vector.tensor_tensor(X[:, sl], X[:, sl], ps, op=ALU.add)
                g.inc(nc.vector.drain(), ev_)

            proj(i["a1_wo"], 16,
                 lambda c, tb: OST[:, c * TOK + tb * 512: c * TOK + tb * 512 + 512],
                 2, evac_res5)


    def s_l1_ffn(first_call):
        FF = 8192
        FOB = FF // 128
        if first_call:
            for (wsrc, wdst, nel, nblk) in ((i["f1_w1t"], w1q, 2048 * FF, 64),
                                            (i["f1_w2t"], w2q, FF * 2048, 64)):
                ld = g.sem("qld")
                ev = g.sem("qev")
                st = g.sem("qst")
                mmq = g.sem("qmm")
                blkcols = wsrc.shape[1] // nblk
                with nc.sbuf_tensor(f"QW{g.uid()}", [128, 2 * blkcols], F32) as QW, \
                     nc.sbuf_tensor(f"QA{g.uid()}", [128, 4], F32) as QA, \
                     nc.sbuf_tensor(f"QT{g.uid()}", [128, blkcols], F32) as QT, \
                     nc.sbuf_tensor(f"QB{g.uid()}", [128, 2 * blkcols], BF16) as QB:
                    nc.vector.memset(QA[:, 0:1], 0.0)
                    g.inc(nc.vector.drain(), ev)
                    for bk in range(nblk):
                        buf = QW[:, (bk % 2) * blkcols:(bk % 2) * blkcols + blkcols]
                        if bk >= 2:
                            nc.sync.wait_ge(ev.h, ev.off + bk)
                        g.inc(nc.sync.dma_start(
                            out=buf, in_=wsrc[:, bk * blkcols:(bk + 1) * blkcols]), ld, 16)
                        nc.vector.wait_ge(ld.h, ld.off + 16 * (bk + 1))
                        nc.vector.tensor_reduce(QA[:, 1:2], buf, axis=AXL.X, op=ALU.add,
                                                apply_absolute_value=True)
                        nc.vector.drain()
                        nc.vector.tensor_tensor(QA[:, 0:1], QA[:, 0:1], QA[:, 1:2],
                                                op=ALU.add)
                        g.inc(nc.vector.drain(), ev)
                    nc.tensor.wait_ge(ev.h, ev.n)
                    g.inc(MM(pst[0][0:1, 0:1], ones, QA[:, 0:1],
                                           start=True, stop=True), mmq)
                    nc.vector.wait_ge(mmq.h, mmq.n)
                    nc.vector.tensor_scalar_mul(SC[0:1, 16:17], pst[0][0:1, 0:1],
                                                0.7 / float(nel))
                    nc.vector.drain()
                    nc.vector.tensor_scalar_mul(SC[0:1, 17:18], SC[0:1, 16:17], -1.0)
                    nc.vector.drain()
                    scalar_bcast(SC[0:1, 16:17], QA[:, 2:3], 128)
                    scalar_bcast(SC[0:1, 17:18], QA[:, 3:4], 128)
                    g.bar()
                    for bk in range(nblk):
                        buf = QW[:, (bk % 2) * blkcols:(bk % 2) * blkcols + blkcols]
                        obuf = QB[:, (bk % 2) * blkcols:(bk % 2) * blkcols + blkcols]
                        if bk >= 2:
                            nc.sync.wait_ge(ev.h, ev.n - 1)
                        g.inc(nc.sync.dma_start(
                            out=buf, in_=wsrc[:, bk * blkcols:(bk + 1) * blkcols]), ld, 16)
                        nc.vector.wait_ge(ld.h, ld.off + 16 * (nblk + bk + 1))
                        if bk >= 2:
                            nc.vector.wait_ge(st.h, st.n - 16)
                        nc.vector.tensor_scalar(QT[:, :], buf, QA[:, 2:3], None,
                                                op0=ALU.is_ge)
                        nc.vector.drain()
                        nc.vector.tensor_scalar(buf, buf, QA[:, 3:4], None, op0=ALU.is_le)
                        nc.vector.drain()
                        nc.vector.tensor_tensor(obuf, QT[:, :], buf, op=ALU.subtract)
                        g.inc(nc.vector.drain(), ev)
                        nc.sync.wait_ge(ev.h, ev.n)
                        g.inc(nc.sync.dma_start(
                            out=wdst[:, bk * blkcols:(bk + 1) * blkcols], in_=obuf), st, 16)
                    nc.sync.wait_ge(st.h, st.n)
                g.bar()

        with nc.sbuf_tensor(f"LSTf{g.uid()}", [1, 2 * TOK], F32) as LST, \
             nc.sbuf_tensor(f"B12{g.uid()}", [128, 160], F32) as BB, \
             nc.sbuf_tensor(f"XBH{g.uid()}", [128, KC * 512], BF16) as XBH, \
             nc.sbuf_tensor(f"XBL{g.uid()}", [128, KC * 512], BF16) as XBL, \
             nc.sbuf_tensor(f"HQH{g.uid()}", [128, 16 * 512], BF16) as HQH, \
             nc.sbuf_tensor(f"HQL{g.uid()}", [128, 16 * 512], BF16) as HQL, \
             nc.sbuf_tensor(f"YAC{g.uid()}", [128, KC * 512], F32) as YAC, \
             nc.sbuf_tensor(f"GEL{g.uid()}", [128, 3 * 512], F32) as GEL, \
             nc.sbuf_tensor(f"WBF{g.uid()}", [128, 2 * KC * 128], BF16) as WB:
            ldc = g.sem("fldc")
            g.inc(nc.sync.dma_start(out=BB[:, 0:64], in_=i["f1_b1c"][:, :]), ldc, 16)
            g.inc(nc.sync.dma_start(out=BB[:, 64:80], in_=i["f1_b2c"][:, :]), ldc, 16)
            ln_stats(lambda c, tb: X[:, c * TOK + tb * 512: c * TOK + tb * 512 + 512],
                     TOK, LST)
            nc.vector.wait_ge(ldc.h, ldc.n)
            nc.vector.tensor_scalar_mul(BB[:, 80:144], BB[:, 0:64], 0.7071067811865476)
            nc.vector.drain()
            g.bar()
            mm = g.sem("fm6")
            ev = g.sem("fe6")
            sq = g.sem("fs6")
            wl = g.sem("fw6")
            for tb in range(2):
                lam = g.sem("flam")
                mcol = LST[0:1, tb * 512:tb * 512 + 512]
                rcol = LST[0:1, TOK + tb * 512:TOK + tb * 512 + 512]
                nc.tensor.wait_ge(ev.h, ev.n)
                MM(pb[4][:, :], ONESR[0:1, :], mcol, start=True, stop=True)
                g.inc(MM(pb[5][:, :], ONESR[0:1, :], rcol,
                                       start=True, stop=True), lam)
                nc.vector.wait_ge(lam.h, lam.n)
                for c in range(KC):
                    ssl = slice(c * TOK + tb * 512, c * TOK + tb * 512 + 512)
                    dsl = slice(c * 512, c * 512 + 512)
                    xnt = GEL[:, 0:512]
                    nc.vector.tensor_tensor(xnt, X[:, ssl], pb[4][:, :], op=ALU.subtract)
                    nc.vector.drain()
                    nc.vector.tensor_tensor(xnt, xnt, pb[5][:, :], op=ALU.mult)
                    nc.vector.drain()
                    nc.vector.tensor_copy(XBH[:, dsl], xnt)
                    nc.vector.drain()
                    nc.vector.tensor_tensor(XBL[:, dsl], xnt, XBH[:, dsl],
                                            op=ALU.subtract)
                    nc.vector.drain()
                nc.vector.memset(YAC[:, :], 0.0)
                g.inc(nc.vector.drain(), lam)
                nc.tensor.wait_ge(lam.h, lam.n)
                g.bar()
                for fq in range(4):
                    j = 0
                    for fb in range(16):
                        gfb = fq * 16 + fb
                        buf = WB[:, (fb % 2) * KC * 128:(fb % 2) * KC * 128 + KC * 128]
                        if fb >= 2:
                            nc.sync.wait_ge(mm.h, mm.n - 1)
                        g.inc(nc.sync.dma_start(
                            out=buf, in_=w1q[:, gfb * KC * 128:(gfb + 1) * KC * 128]),
                            wl, 16)
                        nc.tensor.wait_ge(wl.h, wl.n)
                        psb = pb[j % 2]
                        if j >= 2:
                            nc.tensor.wait_ge(sq.h, sq.n - 2)
                        for c in range(KC):
                            wslc = buf[:, c * 128:(c + 1) * 128]
                            MM(psb[:, :], wslc,
                                             XBH[:, c * 512:(c + 1) * 512],
                                             start=(c == 0), stop=False,
                                             skip_group_check=True)
                            MM(psb[:, :], wslc,
                                             XBL[:, c * 512:(c + 1) * 512],
                                             start=False, stop=(c == KC - 1),
                                             skip_group_check=True)
                        g.inc(g.lastT, mm)
                        nc.scalar.wait_ge(mm.h, mm.n)
                        g.inc(nc.scalar.activation(GEL[:, 0:512], psb[:, :], AF.Erf,
                                                   bias=BB[:, 80 + gfb:81 + gfb],
                                                   scale=0.7071067811865476), sq)
                        g.inc(nc.scalar.activation(GEL[:, 512:1024], psb[:, :], AF.Identity,
                                                   bias=BB[:, gfb:gfb + 1]), sq)
                        nc.vector.wait_ge(sq.h, sq.n)
                        nc.vector.tensor_tensor(GEL[:, 1024:1536],
                                                GEL[:, 0:512], GEL[:, 512:1024],
                                                op=ALU.mult)
                        nc.vector.drain()
                        nc.vector.tensor_tensor(GEL[:, 1024:1536],
                                                GEL[:, 1024:1536],
                                                GEL[:, 512:1024], op=ALU.add)
                        nc.vector.drain()
                        nc.vector.tensor_copy(HQH[:, fb * 512:(fb + 1) * 512],
                                              GEL[:, 1024:1536])
                        nc.vector.drain()
                        nc.vector.tensor_tensor(HQL[:, fb * 512:(fb + 1) * 512],
                                                GEL[:, 1024:1536],
                                                HQH[:, fb * 512:(fb + 1) * 512],
                                                op=ALU.subtract)
                        g.inc(nc.vector.drain(), ev)
                        j += 1
                    g.bar()
                    j = 0
                    for ob in range(16):
                        buf = WB[:, (ob % 2) * KC * 128:(ob % 2) * KC * 128 + KC * 128]
                        if ob >= 2:
                            nc.sync.wait_ge(mm.h, mm.n - 1)
                        g.inc(nc.sync.dma_start(
                            out=buf,
                            in_=w2q[:, (ob * 64 + fq * 16) * 128:
                                    (ob * 64 + fq * 16 + 16) * 128]), wl, 16)
                        nc.tensor.wait_ge(wl.h, wl.n)
                        psb = pb[2 + j % 2]
                        if j >= 2:
                            nc.tensor.wait_ge(ev.h, ev.n - 1)
                        for c in range(16):
                            wslc = buf[:, c * 128:(c + 1) * 128]
                            MM(psb[:, :], wslc,
                                             HQH[:, c * 512:(c + 1) * 512],
                                             start=(c == 0), stop=False,
                                             skip_group_check=True)
                            MM(psb[:, :], wslc,
                                             HQL[:, c * 512:(c + 1) * 512],
                                             start=False, stop=(c == 15),
                                             skip_group_check=True)
                        g.inc(g.lastT, mm)
                        nc.vector.wait_ge(mm.h, mm.n)
                        nc.vector.tensor_tensor(YAC[:, ob * 512:(ob + 1) * 512],
                                                YAC[:, ob * 512:(ob + 1) * 512],
                                                psb[:, :], op=ALU.add)
                        g.inc(nc.vector.drain(), ev)
                        j += 1
                    g.bar()
                for c in range(KC):
                    sl = slice(c * TOK + tb * 512, c * TOK + tb * 512 + 512)
                    nc.vector.scalar_tensor_tensor(GEL[:, 0:512], YAC[:, c * 512:(c + 1) * 512],
                                                   0.5, X[:, sl], op0=ALU.mult, op1=ALU.add)
                    nc.vector.drain()
                    nc.vector.tensor_scalar(X[:, sl], GEL[:, 0:512], BB[:, 64 + c:65 + c],
                                            None, op0=ALU.add)
                    nc.vector.drain()
                g.bar()

    def s7_l1_attn_b():
        ccs = g.sem("b7cc")
        with nc.sbuf_tensor("XN_7", [128, KC * TOK], F32) as XN7, \
             nc.sbuf_tensor("LST_7", [1, 2 * TOK], F32) as LST:
            ln_stats(lambda c, tb: X[:, c * TOK + tb * 512: c * TOK + tb * 512 + 512],
                     TOK, LST)
            ln_apply(lambda c, tb: X[:, c * TOK + tb * 512: c * TOK + tb * 512 + 512],
                     lambda c, tb: XN7[:, c * TOK + tb * 512: c * TOK + tb * 512 + 512],
                     TOK, LST)
            st = g.sem("b7st")
            g.inc(nc.sync.dma_start(
                out=AP(cc_xn_in.tensor, 0, [[TOK, 128], [128 * TOK, KC], [1, TOK]]),
                in_=XN7[:].rearrange("p (c t) -> p c t", c=KC)), st, 16)
            nc.gpsimd.wait_ge(st.h, st.n)
            g.inc(nc.gpsimd.collective_compute(
                "AllGather", ALU.bypass, replica_groups=RG,
                ins=[cc_xn_in[:, :]], outs=[cc_xn_out[:, :]]), ccs)
        g.bar()

        with nc.sbuf_tensor("QH_7", [128, S], F32) as QH, \
             nc.sbuf_tensor("KH_7", [128, S], F32) as KH, \
             nc.sbuf_tensor("VH_7", [128, 32 * 130], F32) as VH, \
             nc.sbuf_tensor("WQKV7", [128, 3 * KC * 128], F32) as WQKV, \
             nc.sbuf_tensor("XG7", [128, KC * 512], F32) as XG:
            PT = EV
            for kt in range(32):
                nc.vector.memset(VH[:, kt * 130 + 64:kt * 130 + 65], 1.0)
                nc.vector.memset(VH[:, kt * 130 + 129:kt * 130 + 130], 1.0)
            nc.vector.drain()
            hl = g.sem("b7hl")
            mm = g.sem("b7mm")
            mo = g.sem("b7mo")
            sq = g.sem("b7sq")
            ev = g.sem("b7ev")
            st = g.sem("b7st2")
            xl = g.sem("b7xl")
            nc.sync.wait_ge(ccs.h, ccs.n)
            g.bar()
            for h in range(4):
                for wi, nm in ((0, "b1_wq"), (1, "b1_wk"), (2, "b1_wv")):
                    g.inc(nc.sync.dma_start(
                        out=WQKV[:, wi * KC * 128:(wi + 1) * KC * 128],
                        in_=i[nm][:, h * KC * 128:(h + 1) * KC * 128]), hl, 16)
                nc.tensor.wait_ge(hl.h, hl.n)
                for gt in range(8):
                    if True:
                        g.inc(nc.sync.dma_start(
                            out=XG[:].rearrange("p (c t) -> p c t", c=KC),
                            in_=AP(cc_xn_out.tensor, (gt // 2) * D * TOK + (gt % 2) * 512,
                                   [[TOK, 128], [128 * TOK, KC], [1, 512]])), xl, 16)
                        nc.tensor.wait_ge(xl.h, xl.n)
                    for wi, dstb in ((0, QH), (1, KH)):
                        psb = pb[wi]
                        for c in range(KC):
                            MM(
                                psb[:, :],
                                WQKV[:, wi * KC * 128 + c * 128:wi * KC * 128 + c * 128 + 128],
                                XG[:, c * 512:(c + 1) * 512],
                                start=(c == 0), stop=(c == KC - 1))
                        g.inc(g.lastT, mm)
                        nc.vector.wait_ge(mm.h, mm.n)
                        if wi == 0:
                            nc.vector.tensor_scalar_mul(dstb[:, gt * 512:(gt + 1) * 512],
                                                        psb[:, :], 0.08838834764831845)
                        else:
                            nc.vector.tensor_copy(dstb[:, gt * 512:(gt + 1) * 512], psb[:, :])
                        g.inc(nc.vector.drain(), ev)
                    for tt in range(4):
                        gtt = gt * 4 + tt
                        psb = pb[2 + tt % 2]
                        for c in range(KC):
                            MM(
                                psb[:, 0:128],
                                XG[:, c * 512 + tt * 128: c * 512 + tt * 128 + 128],
                                WQKV[:, 2 * KC * 128 + c * 128:2 * KC * 128 + c * 128 + 128],
                                start=(c == 0), stop=(c == KC - 1))
                        g.inc(g.lastT, mm)
                        nc.vector.wait_ge(mm.h, mm.n)
                        nc.vector.tensor_copy(VH[:, gtt * 130:gtt * 130 + 64], psb[:, 0:64])
                        nc.vector.drain()
                        nc.vector.tensor_copy(VH[:, gtt * 130 + 65:gtt * 130 + 129],
                                              psb[:, 64:128])
                        g.inc(nc.vector.drain(), ev)
                    nc.tensor.wait_ge(ev.h, ev.n)
                g.bar()
                kn = g.sem(f"b7kn{h}")
                nc.vector.memset(SC[0:1, 10:11], 0.0)
                nc.vector.memset(SC[0:1, 13:14], 0.0)
                g.inc(nc.vector.drain(), kn)
                for src_, slot in ((KH, 10), (QH, 13)):
                    for tbb in range(8):
                        buf = SQ[:, (tbb % 2) * 512:(tbb % 2) * 512 + 512]
                        if tbb >= 2 or slot == 13:
                            nc.scalar.wait_ge(kn.h, kn.n - 1)
                        g.inc(nc.scalar.activation(buf, src_[:, tbb * 512:(tbb + 1) * 512],
                                                   AF.Square), sq)
                        nc.tensor.wait_ge(sq.h, sq.n)
                        nc.tensor.wait_ge(kn.h, kn.n)
                        g.inc(MM(pst[0][0:1, :], ones, buf,
                                               start=True, stop=True), mm)
                        nc.vector.wait_ge(mm.h, mm.n)
                        nc.vector.tensor_reduce(SC[0:1, 11:12], pst[0][0:1, :], axis=AXL.X,
                                                op=ALU.max)
                        nc.vector.drain()
                        nc.vector.tensor_tensor(SC[0:1, slot:slot + 1],
                                                SC[0:1, slot:slot + 1], SC[0:1, 11:12],
                                                op=ALU.max)
                        g.inc(nc.vector.drain(), kn)
                nc.vector.tensor_tensor(SC[0:1, 12:13], SC[0:1, 10:11], SC[0:1, 13:14],
                                        op=ALU.mult)
                g.inc(nc.vector.drain(), ev)
                nc.scalar.wait_ge(ev.h, ev.n)
                g.inc(nc.scalar.activation(SC[0:1, 12:13], SC[0:1, 12:13], AF.Sqrt), sq)
                nc.vector.wait_ge(sq.h, sq.n)
                nc.vector.tensor_scalar_mul(SC[0:1, 12:13], SC[0:1, 12:13], -1.0)
                nc.vector.drain()
                scalar_bcast(SC[0:1, 12:13], CREP[:, 0:1], 128)
                g.bar()
                for qb in range(8):
                    for kt in range(32):
                        psb = pb[kt % 2]
                        if kt >= 2:
                            nc.tensor.wait_ge(sq.h, sq.n - 1)
                        MM(psb[:, :], KH[:, kt * 128:(kt + 1) * 128],
                                         QH[:, qb * 512:(qb + 1) * 512],
                                         start=True, stop=True)
                        g.inc(g.lastT, mm)
                        ptile = PT[:, (kt % 2) * 512:(kt % 2) * 512 + 512]
                        nc.scalar.wait_ge(mm.h, mm.n)
                        if kt >= 2:
                            nc.scalar.wait_ge(mo.h, mo.n - 2)
                        g.inc(nc.scalar.activation(ptile, psb[:, :], AF.Exp,
                                                   bias=CREP[:, 0:1]), sq)
                        nc.tensor.wait_ge(sq.h, sq.n)
                        g.inc(MM(pb[3][0:65, :],
                                               VH[:, kt * 130:kt * 130 + 65],
                                               ptile, start=(kt == 0), stop=(kt == 31),
                                               skip_group_check=True), mo)
                        g.inc(MM(pb[4][0:65, :],
                                               VH[:, kt * 130 + 65:kt * 130 + 130],
                                               ptile, start=(kt == 0), stop=(kt == 31),
                                               skip_group_check=True), mo)
                    nc.vector.wait_ge(mo.h, mo.n)
                    nc.vector.reciprocal(RECIP[0:1, :], pb[3][64:65, :])
                    g.inc(nc.vector.drain(), ev)
                    nc.tensor.wait_ge(ev.h, ev.n)
                    g.inc(MM(pb[5][0:64, :], ONESR[0:1, 0:64],
                                           RECIP[0:1, :], start=True, stop=True), mm)
                    nc.vector.wait_ge(mm.h, mm.n)
                    nc.vector.tensor_copy(SQ[64:128, 0:512], pb[5][0:64, :])
                    nc.vector.drain()
                    obuf = EV[:, 0:512]
                    nc.vector.tensor_tensor(obuf[0:64, :], pb[3][0:64, :],
                                            SQ[64:128, 0:512], op=ALU.mult)
                    nc.vector.drain()
                    nc.vector.tensor_tensor(obuf[64:128, :], pb[4][0:64, :],
                                            SQ[64:128, 0:512], op=ALU.mult)
                    g.inc(nc.vector.drain(), ev)
                    nc.tensor.wait_ge(ev.h, ev.n)
                    nc.sync.wait_ge(ev.h, ev.n)
                    g.inc(nc.sync.dma_start(
                        out=cc_s_in[h * 128:(h + 1) * 128, qb * 512:(qb + 1) * 512],
                        in_=obuf), st, 16)
                    nc.scalar.wait_ge(st.h, st.n)
                g.bar()

        ccs2 = g.sem("b7cc2")
        g.inc(nc.gpsimd.collective_compute(
            "AllGather", ALU.bypass, replica_groups=RG,
            ins=[cc_s_in[:, :]], outs=[cc_s_out[:, :]]), ccs2)
        with nc.sbuf_tensor("OST_7", [128, KC * TOK], F32) as OST, \
             nc.sbuf_tensor("QSL_7", [128, 4], F32) as QSL, \
             nc.sbuf_tensor("OC4_7", [128, 4 * 1024], F32) as OC4:
            ld4 = g.sem("b7ld4")
            ev4 = g.sem("b7ev4")
            g.inc(nc.sync.dma_start(out=QSL[:, :], in_=i["qselcol"][:, :]), ld4, 16)
            nc.sync.wait_ge(ccs2.h, ccs2.n)
            for c in range(KC):
                if c >= 1:
                    nc.sync.wait_ge(ev4.h, ev4.n)
                for k in range(4):
                    g.inc(nc.sync.dma_start(
                        out=OC4[:, k * 1024:(k + 1) * 1024],
                        in_=cc_s_out[c * 128:(c + 1) * 128, k * 1024:(k + 1) * 1024]),
                        ld4, 16)
                nc.vector.wait_ge(ld4.h, ld4.n)
                nc.vector.tensor_scalar(OST[:, c * TOK:(c + 1) * TOK],
                                        OC4[:, 0:1024], QSL[:, 0:1], None, op0=ALU.mult)
                nc.vector.drain()
                for k in range(1, 4):
                    nc.vector.scalar_tensor_tensor(OST[:, c * TOK:(c + 1) * TOK],
                                                   OC4[:, k * 1024:(k + 1) * 1024],
                                                   QSL[:, k:k + 1],
                                                   OST[:, c * TOK:(c + 1) * TOK],
                                                   op0=ALU.mult, op1=ALU.add)
                    nc.vector.drain()
                g.inc(nc.vector.drain(), ev4)
            g.bar()

            def evac_res7(ob, tb, ps, mm_, ev_, j):
                nc.vector.wait_ge(mm_.h, mm_.n)
                sl = slice(ob * TOK + tb * 512, ob * TOK + tb * 512 + 512)
                nc.vector.tensor_tensor(X[:, sl], X[:, sl], ps, op=ALU.add)
                g.inc(nc.vector.drain(), ev_)

            proj(i["b1_wo"], 16,
                 lambda c, tb: OST[:, c * TOK + tb * 512: c * TOK + tb * 512 + 512],
                 2, evac_res7)

    if n_sub >= 1:
        s1_l0_attn_a()
    if n_sub >= 2:
        s_l0_ffn(True)
    if n_sub >= 3:
        s3_l0_attn_b()
    if n_sub >= 4:
        s_l0_ffn(False)
    if n_sub >= 5:
        s5_l1_attn_a()
    if n_sub >= 6:
        s_l1_ffn(True)
    if n_sub >= 7:
        s7_l1_attn_b()
    if n_sub >= 8:
        s_l1_ffn(False)
    if n_sub < 1:
        ld0 = g.sem("ld0")
        g.inc(nc.sync.dma_start(out=X[:], in_=xin[:, :]), ld0, 16)
        nc.sync.wait_ge(ld0.h, ld0.n)
        g.bar()

    fin = g.sem("fin")
    g.inc(nc.sync.dma_start(out=y[:, :], in_=X[:]), fin, 16)
    nc.sync.wait_ge(fin.h, fin.n)
    return nc



def kernel(**inputs):
    ins = {k: np.ascontiguousarray(np.asarray(v, np.float32)) for k, v in inputs.items()}
    x = ins["L0_attn_a_x"]

    dogp = np.zeros((32, 8), np.float32)
    dogp[:, 0:7] = ins["L0_attn_a_dog_params"]
    shared = {
        "l0a_wq": wprep(ins["L0_attn_a_Wq"]),
        "l0a_wk": wprep(ins["L0_attn_a_Wk"]),
        "l0a_wv": wprep(ins["L0_attn_a_Wv"]),
        "l0a_wo": wprep(ins["L0_attn_a_Wo"]),
        "l0a_dog": dogp,
        "l0a_sinks": ins["L0_attn_a_sinks"].reshape(1, 32),
    }
    if N_SUB >= 2:
        shared["l0f_wT"] = wprep(np.ascontiguousarray(ins["L0_ffn_weight"].T))
        shared["l0f_cols"] = np.concatenate(
            [colprep(ins["L0_ffn_gamma"]), colprep(ins["L0_ffn_beta"]),
             colprep(ins["L0_ffn_ln_weight"]), colprep(ins["L0_ffn_ln_bias"])], axis=1)
    if N_SUB >= 3:
        shared["l0b_wqd"] = wprep(ins["L0_attn_b_wq_d"])
        shared["l0b_wkvd"] = wprep(ins["L0_attn_b_wkv_d"], obs=256)
        shared["l0b_wqu"] = wprep(ins["L0_attn_b_wq_u"])
        shared["l0b_wku"] = wprep(ins["L0_attn_b_wk_u"])
        shared["l0b_wvu"] = wprep(ins["L0_attn_b_wv_u"])
        shared["l0b_wo"] = wprep(ins["L0_attn_b_wo"])
    if N_SUB >= 5:
        p_ = np.arange(128)
        shared["idxcol"] = (1.0 / ((p_ % 16) + 1) / 64.0).reshape(128, 1).astype(np.float32)
        shared["pcol"] = p_.reshape(128, 1).astype(np.float32)
        shared["phcol"] = (p_ // 16).reshape(128, 1).astype(np.float32)
        shared["pm16col"] = (p_ % 16).reshape(128, 1).astype(np.float32)
        shared["a1_wo"] = wprep(ins["L1_attn_a_Wo"])
    if N_SUB >= 6:
        shared["f1_w1t"] = wprep(np.ascontiguousarray(ins["L1_ffn_w1"].T))
        shared["f1_w2t"] = wprep(np.ascontiguousarray(ins["L1_ffn_w2"].T))
        shared["f1_b1c"] = colprep(ins["L1_ffn_b1"])
        shared["f1_b2c"] = colprep(ins["L1_ffn_b2"])
    if N_SUB >= 7:
        shared["b1_wo"] = wprep(ins["L1_attn_b_wo"])
    if N_SUB < 1:
        shared = {}

    in_maps = []
    for c in range(NCORES):
        b, q = divmod(c, 4)
        t0 = q * TOK
        m = dict(shared)
        m["xin"] = fmprep(x[b, t0:t0 + TOK])
        if q == 0:
            m["xhalo"] = np.zeros((128, KC * HALO), np.float32)
            m["hk"] = np.zeros((1, 1), np.float32)
        else:
            m["xhalo"] = fmprep(x[b, t0 - HALO:t0])
            m["hk"] = np.ones((1, 1), np.float32)
        if N_SUB >= 3:
            m["tpos"] = np.full((1, 1), float(t0), np.float32)
        if N_SUB >= 5:
            hs = slice(q * 512, q * 512 + 512)
            m["a1_wq"] = wprep(np.ascontiguousarray(ins["L1_attn_a_Wq"][:, hs]))
            m["a1_wk"] = wprep(np.ascontiguousarray(ins["L1_attn_a_Wk"][:, hs]))
            m["a1_wv"] = wprep(np.ascontiguousarray(ins["L1_attn_a_Wv"][:, hs]))
            m["a1_wilr"] = wprep(np.ascontiguousarray(
                ins["L1_attn_a_W_ilr"][:, 8 * q:8 * q + 8]), obs=8)
            m["a1_bilr8"] = np.ascontiguousarray(
                ins["L1_attn_a_b_ilr"][8 * q:8 * q + 8].reshape(8, 1))
            m["a1_lnw"] = np.ascontiguousarray(ins["L1_attn_a_ln_weight"][8 * q:8 * q + 8])
            m["a1_lnb"] = np.ascontiguousarray(ins["L1_attn_a_ln_bias"][8 * q:8 * q + 8])
            m["a1_w1init"] = np.ascontiguousarray(
                ins["L1_attn_a_W1_init"][8 * q:8 * q + 8].reshape(4, 2, 64, 64)
                .transpose(1, 2, 0, 3).reshape(128, 256))
            m["a1_b1init"] = np.ascontiguousarray(
                ins["L1_attn_a_b1_init"][8 * q:8 * q + 8].reshape(8, 64))
            qsel = np.zeros((128, 4), np.float32)
            qsel[:, q] = 1.0
            m["qselcol"] = qsel
        if N_SUB >= 7:
            hs2 = slice(q * 512, q * 512 + 512)
            m["b1_wq"] = wprep(np.ascontiguousarray(ins["L1_attn_b_wq"][:, hs2]))
            m["b1_wk"] = wprep(np.ascontiguousarray(ins["L1_attn_b_wk"][:, hs2]))
            m["b1_wv"] = wprep(np.ascontiguousarray(ins["L1_attn_b_wv"][:, hs2]))
        in_maps.append(m)

    nc = build_program(N_SUB)
    res = run_bass_kernel_spmd(nc, in_maps, core_ids=list(range(NCORES)))
    if DEBUG:
        globals()["dbg"] = res.results
    out = np.empty((B, S, D), np.float32)
    for c in range(NCORES):
        b, q = divmod(c, 4)
        yc = res.results[c]["y"]
        out[b, q * TOK:(q + 1) * TOK] = yc.reshape(128, KC, TOK).transpose(1, 0, 2).reshape(D, TOK).T
    return out


# version 07 (final)
# speedup vs baseline: 1.0524x; 1.0354x over previous
import numpy as np
import concourse.bass as bass
import concourse.mybir as mybir
from concourse.ap import AP
from concourse.bass_utils import run_bass_kernel_spmd
from concourse.replica_groups import maybe_share_collective_output_space

F32 = mybir.dt.float32
BF16 = mybir.dt.bfloat16
AF = mybir.ActivationFunctionType
ALU = mybir.AluOpType
AXL = mybir.AxisListType

B, S, D = 2, 4096, 2048
NCORES = 8
TOK = 1024
HALO = 512
KC = D // 128
RG = [[0, 1, 2, 3], [4, 5, 6, 7]]
LN_EPS = 1e-5

N_SUB = 8
DEBUG = False



def wprep(W, obs=128):
    Din, M = W.shape
    kcw = Din // 128
    nob = M // obs
    return np.ascontiguousarray(
        W.reshape(kcw, 128, nob, obs).transpose(1, 2, 0, 3).reshape(128, nob * kcw * obs))


def fmprep(Xtok):
    T = Xtok.shape[0]
    return np.ascontiguousarray(
        Xtok.T.reshape(KC, 128, T).transpose(1, 0, 2).reshape(128, KC * T))


def colprep(v):
    return np.ascontiguousarray(v.reshape(-1, 128).T)



class Sem:
    __slots__ = ("h", "n", "off", "g")

    def __init__(self, h, off, g):
        self.h = h
        self.n = off
        self.off = off
        self.g = g


class G:
    NPOOL = 40

    def __init__(self, nc):
        self.nc = nc
        self._uid = 0
        self._pool = []
        self._counts = {}
        self._rr = 0

    def sem(self, prefix="s"):
        if len(self._pool) < self.NPOOL:
            h = self.nc.alloc_semaphore(name=f"pool{len(self._pool)}")
            self._pool.append(h)
            self._counts[id(h)] = 0
            hh = h
        else:
            hh = self._pool[self._rr % self.NPOOL]
            self._rr += 1
        return Sem(hh, self._counts[id(hh)], self)

    def uid(self):
        self._uid += 1
        return self._uid

    def inc(self, ins, sem, k=1):
        ins.then_inc(sem.h, k)
        sem.n += k
        self._counts[id(sem.h)] = sem.n

    def bar(self):
        self.nc.all_engine_barrier()


def build_program(n_sub):
    nc = bass.Bass("TRN2", target_bir_lowering=False)
    g = G(nc)

    def MM(*a, **k):
        ins = nc.tensor.matmul(*a, **k)
        g.lastT = ins
        return ins

    def TR(*a, **k):
        ins = nc.tensor.transpose(*a, **k)
        g.lastT = ins
        return ins

    def din(name, shape):
        return nc.dram_tensor(name, shape, F32, kind="ExternalInput").ap()

    def dint(name, shape):
        kind = "ExternalOutput" if DEBUG else "Internal"
        return nc.dram_tensor(name, shape, F32, kind=kind).ap()

    xin = din("xin", [128, KC * TOK])
    xhalo = din("xhalo", [128, KC * HALO])
    hk = din("hk", [1, 1])
    i = {}
    i["l0a_wq"] = din("l0a_wq", [128, 16 * KC * 128])
    i["l0a_wk"] = din("l0a_wk", [128, 16 * KC * 128])
    i["l0a_wv"] = din("l0a_wv", [128, 16 * KC * 128])
    i["l0a_wo"] = din("l0a_wo", [128, 16 * KC * 128])
    i["l0a_dog"] = din("l0a_dog", [32, 8])
    i["l0a_sinks"] = din("l0a_sinks", [1, 32])
    if n_sub >= 2:
        i["l0f_wT"] = din("l0f_wT", [128, 16 * KC * 128])
        i["l0f_cols"] = din("l0f_cols", [128, 4 * KC])
    if n_sub >= 3:
        i["l0b_wqd"] = din("l0b_wqd", [128, 2 * KC * 128])
        i["l0b_wkvd"] = din("l0b_wkvd", [128, 1 * KC * 256])
        i["l0b_wqu"] = din("l0b_wqu", [128, 8 * 2 * 128])
        i["l0b_wku"] = din("l0b_wku", [128, 8 * 2 * 128])
        i["l0b_wvu"] = din("l0b_wvu", [128, 16 * 2 * 128])
        i["l0b_wo"] = din("l0b_wo", [128, 16 * KC * 128])
        i["tpos"] = din("tpos", [1, 1])
    if n_sub >= 5:
        i["a1_wq"] = din("a1_wq", [128, 4 * KC * 128])
        i["a1_wk"] = din("a1_wk", [128, 4 * KC * 128])
        i["a1_wv"] = din("a1_wv", [128, 4 * KC * 128])
        i["a1_wilr"] = din("a1_wilr", [128, KC * 8])
        i["a1_bilr8"] = din("a1_bilr8", [8, 1])
        i["a1_lnw"] = din("a1_lnw", [8, 64])
        i["a1_lnb"] = din("a1_lnb", [8, 64])
        i["a1_w1init"] = din("a1_w1init", [128, 4 * 64])
        i["a1_b1init"] = din("a1_b1init", [8, 64])
        i["a1_wo"] = din("a1_wo", [128, 16 * KC * 128])
        i["idxcol"] = din("idxcol", [128, 1])
        i["pcol"] = din("pcol", [128, 1])
        i["phcol"] = din("phcol", [128, 1])
        i["pm16col"] = din("pm16col", [128, 1])
        i["qselcol"] = din("qselcol", [128, 4])
    if n_sub >= 6:
        i["f1_w1t"] = din("f1_w1t", [128, 64 * KC * 128])
        i["f1_w2t"] = din("f1_w2t", [128, 16 * 64 * 128])
        i["f1_b1c"] = din("f1_b1c", [128, 64])
        i["f1_b2c"] = din("f1_b2c", [128, 16])
    if n_sub >= 7:
        i["b1_wq"] = din("b1_wq", [128, 4 * KC * 128])
        i["b1_wk"] = din("b1_wk", [128, 4 * KC * 128])
        i["b1_wv"] = din("b1_wv", [128, 4 * KC * 128])
        i["b1_wo"] = din("b1_wo", [128, 16 * KC * 128])

    y = nc.dram_tensor("y", [128, KC * TOK], F32, kind="ExternalOutput").ap()

    qstash = dint("qstash", [16, 128, TOK])
    kstash = dint("kstash", [16, 128, HALO + TOK])
    vstash = dint("vstash", [16, 128, HALO + TOK])
    etab = dint("etab", [32, 2048])
    etab3 = dint("etab3", [32, 128, 1408])
    outdbg = dint("outdbg", [128, KC * (HALO + TOK)])
    scdbg = dint("scdbg", [1, 64])
    dbg2 = dint("dbg2", [128, 8])
    dbg3 = dint("dbg3", [1, 512])
    sml = dint("sml", [1, 128])
    cc_lat_in = nc.dram_tensor("cc_lat_in", [TOK, 256], F32, kind="Internal").ap()
    _space = maybe_share_collective_output_space("AllGather", RG)
    cc_lat_out = nc.dram_tensor("cc_lat_out", [S, 256], F32, kind="Internal",
                                addr_space=_space).ap()
    bstash = dint("bstash", [16, 128, TOK])
    xbfm = dint("xbfm", [4, 128, S])
    xcfm = dint("xcfm", [4, 128, S])
    xafm = dint("xafm", [4, 128, S])
    ilrst = dint("ilrst", [8, S])
    cc_xn_in = nc.dram_tensor("cc_xn_in", [D, TOK], F32, kind="Internal").ap()
    cc_xn_out = nc.dram_tensor("cc_xn_out", [4 * D, TOK], F32, kind="Internal",
                               addr_space=_space).ap()
    cc_s_in = nc.dram_tensor("cc_s_in", [512, S], F32, kind="Internal").ap()
    scandump = dint("scandump", [512, S])
    sbdump = dint("sbdump", [128, 16 * 64 + 24])
    cndump = dint("cndump", [128, 512])
    cc_s_out = nc.dram_tensor("cc_s_out", [D, S], F32, kind="Internal",
                              addr_space=_space).ap()
    w1q = nc.dram_tensor("w1q", [128, 64 * KC * 128], BF16, kind="Internal").ap()
    w2q = nc.dram_tensor("w2q", [128, 16 * 64 * 128], BF16, kind="Internal").ap()

    X = nc.alloc_sbuf_tensor("X", [128, KC * TOK], F32)
    CONST = nc.alloc_sbuf_tensor("CONST", [128, 4], F32)
    IDENT = nc.alloc_sbuf_tensor("IDENT", [128, 128], F32)
    SC = nc.alloc_sbuf_tensor("SC", [1, 64], F32)
    CREP = nc.alloc_sbuf_tensor("CREP", [128, 2], F32)
    SQ = nc.alloc_sbuf_tensor("SQ", [128, 2 * 512], F32)
    EV = nc.alloc_sbuf_tensor("EV", [128, 2 * 512], F32)
    RECIP = nc.alloc_sbuf_tensor("RECIP", [1, 512], F32)

    pb = [nc.alloc_psum_tensor(f"pb{j}", [128, 512], F32) for j in range(6)]
    pst = [nc.alloc_psum_tensor(f"pst{j}", [1, 512], F32) for j in range(2)]

    ONESR = nc.alloc_sbuf_tensor("ONESR", [1, 128], F32)
    ones = CONST[:, 0:1]
    eps1 = CONST[0:1, 1:2]

    nc.vector.memset(CONST[:, 0:1], 1.0)
    nc.vector.memset(CONST[:, 1:2], LN_EPS)
    nc.vector.memset(ONESR[:, :], 1.0)
    nc.vector.drain()
    with nc.sbuf_tensor("i_rid", [128, 128], F32) as rid, \
         nc.sbuf_tensor("i_cid", [128, 128], F32) as cid:
        nc.gpsimd.iota(rid[:], pattern=[[0, 128]], channel_multiplier=1,
                       allow_small_or_imprecise_dtypes=True)
        nc.gpsimd.iota(cid[:], pattern=[[1, 128]], channel_multiplier=0,
                       allow_small_or_imprecise_dtypes=True)
        gp0 = g.sem("gp0")
        g.inc(nc.gpsimd.drain(), gp0)
        nc.vector.wait_ge(gp0.h, gp0.n)
        nc.vector.tensor_tensor(IDENT[:], rid[:], cid[:], op=ALU.is_equal)
        nc.vector.drain()
    g.bar()


    def ln_stats(src_fn, tcols, stat):
        ntb = tcols // 512
        mm = g.sem("lnmm")
        sq = g.sem("lnsq")
        rd = g.sem("lnrd")
        for tb in range(ntb):
            if tb >= 1:
                nc.tensor.wait_ge(rd.h, rd.n)
            for c in range(KC):
                g.inc(MM(pst[0][0:1, :], ones, src_fn(c, tb),
                                       start=(c == 0), stop=(c == KC - 1)), mm)
            for c in range(KC):
                buf = SQ[:, (c % 2) * 512:(c % 2) * 512 + 512]
                if c >= 2:
                    nc.scalar.wait_ge(mm.h, mm.n - 1)
                g.inc(nc.scalar.activation(buf, src_fn(c, tb), AF.Square), sq)
                nc.tensor.wait_ge(sq.h, sq.n)
                g.inc(MM(pst[1][0:1, :], ones, buf,
                                       start=(c == 0), stop=(c == KC - 1)), mm)
            nc.vector.wait_ge(mm.h, mm.n)
            mcol = stat[0:1, tb * 512:tb * 512 + 512]
            rcol = stat[0:1, tcols + tb * 512:tcols + tb * 512 + 512]
            nc.vector.tensor_scalar_mul(mcol, pst[0][0:1, :], 1.0 / D)
            nc.vector.drain()
            nc.vector.tensor_scalar_mul(rcol, pst[1][0:1, :], 1.0 / D)
            g.inc(nc.vector.drain(), rd)
            nc.vector.tensor_tensor(SQ[0:1, 0:512], mcol, mcol, op=ALU.mult)
            nc.vector.drain()
            nc.vector.tensor_tensor(rcol, rcol, SQ[0:1, 0:512], op=ALU.subtract)
            g.inc(nc.vector.drain(), sq)
            nc.scalar.wait_ge(sq.h, sq.n)
            g.inc(nc.scalar.activation(rcol, rcol, AF.Sqrt, bias=eps1), sq)
            nc.vector.wait_ge(sq.h, sq.n)
            nc.vector.reciprocal(rcol, rcol)
            nc.vector.drain()
        g.bar()

    def ln_apply(src_fn, dst_fn, tcols, stat, aff=None):
        ntb = tcols // 512
        mm = g.sem("lamm")
        for tb in range(ntb):
            mcol = stat[0:1, tb * 512:tb * 512 + 512]
            rcol = stat[0:1, tcols + tb * 512:tcols + tb * 512 + 512]
            MM(pb[4][:, :], ONESR[0:1, :], mcol,
                             start=True, stop=True)
            g.inc(MM(pb[5][:, :], ONESR[0:1, :], rcol,
                                   start=True, stop=True), mm)
            nc.vector.wait_ge(mm.h, mm.n)
            for c in range(KC):
                s_ = src_fn(c, tb)
                d_ = dst_fn(c, tb)
                nc.vector.tensor_tensor(d_, s_, pb[4][:, :], op=ALU.subtract)
                nc.vector.drain()
                nc.vector.tensor_tensor(d_, d_, pb[5][:, :], op=ALU.mult)
                nc.vector.drain()
                if aff is not None:
                    lnw_col, lnb_col = aff
                    nc.vector.tensor_scalar(d_, d_, lnw_col[:, c:c + 1], lnb_col[:, c:c + 1],
                                            op0=ALU.mult, op1=ALU.add)
                    nc.vector.drain()
            g.inc(nc.vector.drain(), mm)
            nc.tensor.wait_ge(mm.h, mm.n)
        g.bar()

    def proj(w_hbm, nob, src_fn, ntb, evac, kc=KC, obs=128):
        mm = g.sem("pjmm")
        ev = g.sem("pjev")
        wl = g.sem("pjwl")
        with nc.sbuf_tensor(f"WB{g.uid()}", [128, 2 * kc * obs], F32) as WB:
            j = 0
            for ob in range(nob):
                buf = WB[:, (ob % 2) * kc * obs:(ob % 2) * kc * obs + kc * obs]
                if ob >= 2:
                    nc.sync.wait_ge(mm.h, mm.off + (ob - 1) * ntb)
                g.inc(nc.sync.dma_start(
                    out=buf, in_=w_hbm[:, ob * kc * obs:(ob + 1) * kc * obs]), wl, 16)
                nc.tensor.wait_ge(wl.h, wl.off + 16 * (ob + 1))
                for tb in range(ntb):
                    psb = pb[j % 2]
                    if j >= 2:
                        nc.tensor.wait_ge(ev.h, ev.off + j - 1)
                    out_ap = psb[:, :] if obs == 128 else psb[0:obs, :]
                    for c in range(kc):
                        MM(out_ap,
                                         buf[:, c * obs:(c + 1) * obs],
                                         src_fn(c, tb),
                                         start=(c == 0), stop=(c == kc - 1))
                    g.inc(g.lastT, mm)
                    evac(ob, tb, out_ap, mm, ev, j)
                    j += 1
            g.bar()

    def scalar_bcast(src_11, dst_col, nrows=128):
        mm = g.sem("bc")
        g.inc(nc.vector.drain(), mm)
        g.inc(nc.scalar.drain(), mm)
        nc.tensor.wait_ge(mm.h, mm.n)
        g.inc(MM(pb[4][0:nrows, 0:1], ONESR[0:1, 0:nrows],
                               src_11, start=True, stop=True), mm)
        nc.vector.wait_ge(mm.h, mm.n)
        nc.vector.tensor_copy(dst_col, pb[4][0:nrows, 0:1])
        g.inc(nc.vector.drain(), mm)
        nc.tensor.wait_ge(mm.h, mm.n)

    def cross_maxP(src_col, dst_11, nrows):
        st = g.sem("cmp")
        g.inc(nc.sync.dma_start(out=AP(sml.tensor, 0, [[1, nrows], [1, 1]]), in_=src_col), st, 16)
        nc.sync.wait_ge(st.h, st.n)
        with nc.sbuf_tensor(f"cmp{g.uid()}", [1, nrows], F32) as t:
            g.inc(nc.sync.dma_start(out=t[:], in_=sml[0:1, 0:nrows]), st, 16)
            nc.vector.wait_ge(st.h, st.n)
            nc.vector.tensor_reduce(dst_11, t[:], axis=AXL.X, op=ALU.max)
            nc.vector.drain()
        g.bar()

    def cross_max32(src_col32, dst_11):
        st = g.sem("cm")
        g.inc(nc.sync.dma_start(out=AP(sml.tensor, 0, [[1, 32], [1, 1]]), in_=src_col32), st, 16)
        nc.sync.wait_ge(st.h, st.n)
        with nc.sbuf_tensor(f"cm{g.uid()}", [1, 32], F32) as t:
            g.inc(nc.sync.dma_start(out=t[:], in_=sml[0:1, 0:32]), st, 16)
            nc.vector.wait_ge(st.h, st.n)
            nc.vector.tensor_reduce(dst_11, t[:], axis=AXL.X, op=ALU.max)
            nc.vector.drain()
        g.bar()

    def s1_l0_attn_a():
        TW = HALO + TOK
        ld = g.sem("ld1")
        with nc.sbuf_tensor("XN_a", [128, KC * TW], F32) as XN:
            with nc.sbuf_tensor("LST_a", [1, 2 * TOK], F32) as LST:
                g.inc(nc.sync.dma_start(out=X[:], in_=xin[:, :]), ld, 16)
                g.inc(nc.sync.dma_start(
                    out=AP(XN.ap().tensor, 0, [list(XN.ap().ap[0]), [TW, KC], [1, HALO]]),
                    in_=xhalo[:, :].rearrange("p (c t) -> p c t", c=KC)), ld, 16)
                nc.sync.wait_ge(ld.h, ld.n)
                g.bar()
                ln_stats(lambda c, tb: XN[:, c * TW + tb * 512: c * TW + tb * 512 + 512],
                         HALO, LST[0:1, 0:1024])
                ln_apply(lambda c, tb: XN[:, c * TW + tb * 512: c * TW + tb * 512 + 512],
                         lambda c, tb: XN[:, c * TW + tb * 512: c * TW + tb * 512 + 512],
                         HALO, LST[0:1, 0:1024])
                ln_stats(lambda c, tb: X[:, c * TOK + tb * 512: c * TOK + tb * 512 + 512],
                         TOK, LST)
                ln_apply(lambda c, tb: X[:, c * TOK + tb * 512: c * TOK + tb * 512 + 512],
                         lambda c, tb: XN[:, c * TW + HALO + tb * 512: c * TW + HALO + tb * 512 + 512],
                         TOK, LST)

            with nc.sbuf_tensor("DOG", [32, 8], F32) as DOG, \
                 nc.sbuf_tensor("PP", [32, 8], F32) as PP, \
                 nc.sbuf_tensor("ET", [32, 1536], F32) as ET, \
                 nc.sbuf_tensor("SCR_e", [32, 2 * 1536], F32) as SCR:
                sq = g.sem("esq")
                ev = g.sem("eev")
                st = g.sem("est")
                g.inc(nc.sync.dma_start(out=DOG[:, 0:7], in_=i["l0a_dog"][:, 0:7]), ld, 16)
                g.inc(nc.sync.dma_start(out=SC[0:1, 32:64], in_=i["l0a_sinks"][:, :]), ld, 16)
                g.inc(nc.sync.dma_start(out=SC[0:1, 30:31], in_=hk[:, :]), ld, 16)
                nc.gpsimd.iota(SCR[:, 0:1536], pattern=[[1, 1536]], base=-512,
                               channel_multiplier=0, allow_small_or_imprecise_dtypes=True)
                g.inc(nc.gpsimd.drain(), ev)
                nc.scalar.wait_ge(ld.h, ld.n)
                nc.vector.wait_ge(ld.h, ld.n)
                nc.vector.wait_ge(ev.h, ev.n)
                nc.scalar.wait_ge(ev.h, ev.n)
                for j, scol in ((0, 1), (1, 4)):
                    nc.scalar.activation(PP[:, j:j + 1], DOG[:, scol:scol + 1], AF.Exp)
                    nc.scalar.drain()
                    nc.scalar.activation(PP[:, j:j + 1], PP[:, j:j + 1], AF.Ln,
                                         bias=CONST[0:32, 0:1])
                    nc.scalar.drain()
                    nc.scalar.activation(PP[:, j:j + 1], PP[:, j:j + 1], AF.Square)
                    nc.scalar.drain()
                    g.inc(nc.scalar.mul(PP[:, j:j + 1], PP[:, j:j + 1], 2.0), sq)
                nc.vector.wait_ge(sq.h, sq.n)
                for j in (0, 1):
                    nc.vector.reciprocal(PP[:, j:j + 1], PP[:, j:j + 1])
                    nc.vector.drain()
                    nc.vector.tensor_scalar_mul(PP[:, j:j + 1], PP[:, j:j + 1], -1.0)
                    nc.vector.drain()
                for j, acol, mcol in ((0, 0, 2), (1, 3, 5)):
                    wk = SCR[:, 1536:3072]
                    nc.vector.tensor_scalar(wk, SCR[:, 0:1536], DOG[:, mcol:mcol + 1], None,
                                            op0=ALU.subtract)
                    nc.vector.drain()
                    nc.vector.tensor_tensor(wk, wk, wk, op=ALU.mult)
                    nc.vector.drain()
                    nc.vector.tensor_scalar(wk, wk, PP[:, j:j + 1], None, op0=ALU.mult)
                    g.inc(nc.vector.drain(), ev)
                    nc.scalar.wait_ge(ev.h, ev.n)
                    g.inc(nc.scalar.activation(wk, wk, AF.Exp), sq)
                    nc.vector.wait_ge(sq.h, sq.n)
                    if j == 0:
                        nc.vector.tensor_scalar(ET[:, :], wk, DOG[:, acol:acol + 1], None,
                                                op0=ALU.mult)
                    else:
                        nc.vector.tensor_scalar(wk, wk, DOG[:, acol:acol + 1], None,
                                                op0=ALU.mult)
                        nc.vector.drain()
                        nc.vector.tensor_tensor(ET[:, :], ET[:, :], wk, op=ALU.subtract)
                    nc.vector.drain()
                nc.vector.tensor_scalar(ET[:, :], ET[:, :], DOG[:, 6:7], None, op0=ALU.add)
                nc.vector.drain()
                band = SCR[:, 1536:3072]
                nc.vector.tensor_scalar(band, SCR[:, 0:1536], -0.5, None, op0=ALU.is_ge)
                nc.vector.drain()
                nc.vector.tensor_scalar(SCR[:, 0:1536], SCR[:, 0:1536], 511.5, None,
                                        op0=ALU.is_le)
                nc.vector.drain()
                nc.vector.tensor_tensor(band, band, SCR[:, 0:1536], op=ALU.mult)
                nc.vector.drain()
                fm_ = SCR[:, 0:1536]
                nc.vector.scalar_tensor_tensor(fm_, ET[:, :], 1e30, band,
                                               op0=ALU.add, op1=ALU.mult)
                nc.vector.drain()
                nc.vector.tensor_scalar(fm_, fm_, -1e30, None, op0=ALU.add)
                nc.vector.drain()
                nc.vector.tensor_reduce(PP[:, 4:5], fm_, axis=AXL.X, op=ALU.max)
                g.inc(nc.vector.drain(), ev)
                nc.scalar.wait_ge(ev.h, ev.n)
                g.inc(nc.scalar.activation(ET[:, :], ET[:, :], AF.Exp), sq)
                nc.vector.wait_ge(sq.h, sq.n)
                nc.vector.tensor_tensor(ET[:, :], ET[:, :], band, op=ALU.mult)
                g.inc(nc.vector.drain(), ev)
                nc.sync.wait_ge(ev.h, ev.n)
                g.inc(nc.sync.dma_start(out=etab[:, 0:1536], in_=ET[:, :]), st, 16)
                nc.sync.wait_ge(st.h, st.n)
                g.inc(nc.sync.dma_start(
                    out=AP(etab3.tensor, 0, [[128 * 1408, 32], [1408, 128], [1, 1408]]),
                    in_=AP(etab.tensor, 128, [[2048, 32], [-1, 128], [1, 1408]])), st, 16)
                nc.sync.wait_ge(st.h, st.n)
                cross_max32(PP[:, 4:5], SC[0:1, 0:1])

            with nc.sbuf_tensor("QKST", [1, 2560], F32) as QKST:
                QN2 = QKST[0:1, 0:1024]
                KN2 = QKST[0:1, 1024:2560]
                nc.vector.memset(QN2, 0.0)
                nc.vector.memset(KN2, 0.0)
                nc.vector.drain()
                g.bar()

                def mk_evac(stash, statrow, scale):
                    def evac(ob, tb, ps, mm, ev, j):
                        ebuf = EV[:, (j % 2) * 512:(j % 2) * 512 + 512]
                        nc.vector.wait_ge(mm.h, mm.n)
                        if scale != 1.0:
                            nc.vector.tensor_scalar_mul(ebuf, ps, scale)
                        else:
                            nc.vector.tensor_copy(ebuf, ps)
                        g.inc(nc.vector.drain(), ev)
                        nc.sync.wait_ge(ev.h, ev.n)
                        st = evac.st
                        g.inc(nc.sync.dma_start(
                            out=stash[ob, :, tb * 512:(tb + 1) * 512], in_=ebuf), st, 16)
                        sqb = SQ[:, (j % 2) * 512:(j % 2) * 512 + 512]
                        nc.scalar.wait_ge(ev.h, ev.n)
                        if j >= 2:
                            nc.scalar.wait_ge(evac.mm2.h, evac.mm2.n - 1)
                        g.inc(nc.scalar.activation(sqb, ebuf, AF.Square), evac.sq)
                        nc.tensor.wait_ge(evac.sq.h, evac.sq.n)
                        nc.tensor.wait_ge(evac.s2.h, evac.s2.n)
                        g.inc(MM(pst[0][0:1, :], ones, sqb,
                                               start=True, stop=True), evac.mm2)
                        nc.vector.wait_ge(evac.mm2.h, evac.mm2.n)
                        col = statrow[0:1, tb * 512:tb * 512 + 512]
                        nc.vector.tensor_tensor(col, col, pst[0][0:1, :], op=ALU.max)
                        g.inc(nc.vector.drain(), evac.s2)
                    evac.st = g.sem("qkst")
                    evac.sq = g.sem("qksq")
                    evac.mm2 = g.sem("qkmm2")
                    evac.s2 = g.sem("qks2")
                    return evac

                proj(i["l0a_wq"], 16,
                     lambda c, tb: XN[:, c * TW + HALO + tb * 512: c * TW + HALO + tb * 512 + 512],
                     2, mk_evac(qstash, QN2, 0.125))
                proj(i["l0a_wk"], 16,
                     lambda c, tb: XN[:, c * TW + tb * 512: c * TW + tb * 512 + 512],
                     3, mk_evac(kstash, KN2, 1.0))
                proj(i["l0a_wv"], 16,
                     lambda c, tb: XN[:, c * TW + tb * 512: c * TW + tb * 512 + 512],
                     3, mk_evac(vstash, KN2, 1.0))

                sq = g.sem("csq")
                ev = g.sem("cev")
                nc.vector.tensor_reduce(SC[0:1, 1:2], QN2, axis=AXL.X, op=ALU.max)
                nc.vector.drain()
                nc.vector.tensor_reduce(SC[0:1, 2:3], KN2, axis=AXL.X, op=ALU.max)
                g.inc(nc.vector.drain(), ev)
                nc.scalar.wait_ge(ev.h, ev.n)
                nc.scalar.activation(SC[0:1, 1:2], SC[0:1, 1:2], AF.Sqrt)
                nc.scalar.drain()
                g.inc(nc.scalar.activation(SC[0:1, 2:3], SC[0:1, 2:3], AF.Sqrt), sq)
                nc.vector.wait_ge(sq.h, sq.n)
                nc.vector.tensor_tensor(SC[0:1, 3:4], SC[0:1, 1:2], SC[0:1, 2:3], op=ALU.mult)
                nc.vector.drain()
                nc.vector.tensor_tensor(SC[0:1, 3:4], SC[0:1, 3:4], SC[0:1, 0:1], op=ALU.add)
                nc.vector.drain()
                nc.vector.tensor_scalar_mul(SC[0:1, 4:5], SC[0:1, 3:4], -1.0)
                nc.vector.drain()
                nc.vector.tensor_scalar(SC[0:1, 32:64], SC[0:1, 32:64], SC[0:1, 3:4], None,
                                        op0=ALU.subtract)
                g.inc(nc.vector.drain(), ev)
                nc.scalar.wait_ge(ev.h, ev.n)
                g.inc(nc.scalar.activation(SC[0:1, 32:64], SC[0:1, 32:64], AF.Exp), sq)
                nc.vector.wait_ge(sq.h, sq.n)
                scalar_bcast(SC[0:1, 4:5], CREP[:, 0:1], 128)
                scalar_bcast(SC[0:1, 30:31], CREP[:, 1:2], 128)
                g.bar()

            OUT = XN
            with nc.sbuf_tensor("QH_a", [64, TOK], F32) as QH, \
                 nc.sbuf_tensor("KH_a", [64, TW], F32) as KH, \
                 nc.sbuf_tensor("VH_a", [64, TW], F32) as VH, \
                 nc.sbuf_tensor("VTM_a", [128, 12 * 72], F32) as VTM, \
                 nc.sbuf_tensor("ETW_a", [128, 1408], F32) as ETW, \
                 nc.sbuf_tensor("E64_a", [1, 72], F32) as E64:
                PT = EV
                for kt in range(12):
                    nc.vector.memset(VTM[:, kt * 72 + 64:kt * 72 + 65], 1.0)
                nc.vector.drain()
                g.bar()

                hl = g.sem("hl")
                gl = g.sem("gl")
                mm = g.sem("amm")
                mo = g.sem("amo")
                sq = g.sem("asq")
                ev = g.sem("aev")
                for h in range(32):
                    cch = h // 2
                    rbase = 64 * (h % 2)
                    g.inc(nc.sync.dma_start(out=QH[:, :], in_=qstash[cch, rbase:rbase + 64, :]), hl, 16)
                    g.inc(nc.sync.dma_start(out=KH[:, :], in_=kstash[cch, rbase:rbase + 64, :]), hl, 16)
                    g.inc(nc.sync.dma_start(out=ETW[:, :], in_=etab3[h, :, :]), hl, 16)
                    g.inc(nc.gpsimd.dma_start(out=VH[:, :], in_=vstash[cch, rbase:rbase + 64, :]), gl, 16)
                    nc.tensor.wait_ge(gl.h, gl.n)
                    for kt in range(12):
                        TR(pb[2][:, 0:64], VH[:, kt * 128:(kt + 1) * 128],
                                            IDENT[0:64, 0:64])
                        g.inc(g.lastT, mm)
                        nc.vector.wait_ge(mm.h, mm.n)
                        nc.vector.tensor_copy(VTM[:, kt * 72:kt * 72 + 64], pb[2][:, 0:64])
                        g.inc(nc.vector.drain(), ev)
                        nc.tensor.wait_ge(ev.h, ev.n)
                    nc.sync.wait_ge(hl.h, hl.n)
                    g.bar()

                    for qb in range(2):
                        for kt in range(8):
                            wkt = qb * 4 + kt
                            psb = pb[kt % 2]
                            if kt >= 2:
                                nc.tensor.wait_ge(sq.h, sq.n - 1)
                            MM(psb[:, :], KH[:, wkt * 128:(wkt + 1) * 128],
                                             QH[:, qb * 512:(qb + 1) * 512],
                                             start=True, stop=True)
                            g.inc(g.lastT, mm)
                            ptile = PT[:, (kt % 2) * 512:(kt % 2) * 512 + 512]
                            nc.scalar.wait_ge(mm.h, mm.n)
                            if kt >= 2:
                                nc.scalar.wait_ge(mo.h, mo.n - 1)
                            g.inc(nc.scalar.activation(ptile, psb[:, :], AF.Exp,
                                                       bias=CREP[:, 0:1]), sq)
                            nc.vector.wait_ge(sq.h, sq.n)
                            nc.vector.tensor_tensor(ptile, ptile,
                                                    ETW[:, 896 - 128 * kt: 1408 - 128 * kt],
                                                    op=ALU.mult)
                            if qb == 0 and kt < 4:
                                nc.vector.drain()
                                nc.vector.tensor_scalar(ptile, ptile, CREP[:, 1:2], None,
                                                        op0=ALU.mult)
                            g.inc(nc.vector.drain(), ev)
                            nc.tensor.wait_ge(ev.h, ev.n)
                            g.inc(MM(pb[3][0:65, :],
                                                   VTM[:, wkt * 72:wkt * 72 + 65], ptile,
                                                   start=(kt == 0), stop=(kt == 7),
                                                   skip_group_check=True), mo)
                        nc.vector.wait_ge(mo.h, mo.n)
                        nc.vector.tensor_scalar(RECIP[0:1, :], pb[3][64:65, :],
                                                SC[0:1, 32 + h:33 + h], None, op0=ALU.add)
                        nc.vector.drain()
                        nc.vector.reciprocal(RECIP[0:1, :], RECIP[0:1, :])
                        g.inc(nc.vector.drain(), ev)
                        nc.tensor.wait_ge(ev.h, ev.n)
                        g.inc(MM(pb[4][0:64, :], ONESR[0:1, 0:64],
                                               RECIP[0:1, :], start=True, stop=True), mm)
                        nc.vector.wait_ge(mm.h, mm.n)
                        nc.vector.tensor_copy(SQ[64:128, 0:512], pb[4][0:64, :])
                        nc.vector.drain()
                        dsl = slice(cch * TW + qb * 512, cch * TW + qb * 512 + 512)
                        nc.vector.tensor_tensor(OUT[rbase:rbase + 64, dsl],
                                                pb[3][0:64, :], SQ[64:128, 0:512], op=ALU.mult)
                        g.inc(nc.vector.drain(), ev)
                        nc.tensor.wait_ge(ev.h, ev.n)
                    g.bar()

            if DEBUG:
                dst_ = g.sem("dbg")
                g.inc(nc.sync.dma_start(out=outdbg[:, :], in_=OUT[:, :]), dst_, 16)
                g.inc(nc.sync.dma_start(out=scdbg[:, :], in_=SC[:, :]), dst_, 16)
                g.inc(nc.sync.dma_start(out=dbg2[:, 0:2], in_=CREP[:, 0:2]), dst_, 16)
                g.inc(nc.sync.dma_start(out=dbg3[:, :], in_=RECIP[0:1, :]), dst_, 16)
                nc.sync.wait_ge(dst_.h, dst_.n)
                g.bar()

            def evac_res(ob, tb, ps, mm, ev, j):
                nc.vector.wait_ge(mm.h, mm.n)
                sl = slice(ob * TOK + tb * 512, ob * TOK + tb * 512 + 512)
                nc.vector.tensor_tensor(X[:, sl], X[:, sl], ps, op=ALU.add)
                g.inc(nc.vector.drain(), ev)

            proj(i["l0a_wo"], 16,
                 lambda c, tb: OUT[:, c * TW + tb * 512: c * TW + tb * 512 + 512],
                 2, evac_res)


    def s_l0_ffn(first_call):
        with nc.sbuf_tensor(f"XN_f{g.uid()}", [128, KC * TOK], F32) as XN2, \
             nc.sbuf_tensor(f"LST_f{g.uid()}", [1, 2 * TOK], F32) as LST, \
             nc.sbuf_tensor(f"COLS_f{g.uid()}", [128, 5 * KC], F32) as COLS, \
             nc.sbuf_tensor(f"WMX_f{g.uid()}", [128, 2], F32) as WMX:
            ld = g.sem("ldf")
            g.inc(nc.sync.dma_start(out=COLS[:, 0:4 * KC], in_=i["l0f_cols"][:, :]), ld, 16)
            ln_stats(lambda c, tb: X[:, c * TOK + tb * 512: c * TOK + tb * 512 + 512],
                     TOK, LST)
            ln_apply(lambda c, tb: X[:, c * TOK + tb * 512: c * TOK + tb * 512 + 512],
                     lambda c, tb: XN2[:, c * TOK + tb * 512: c * TOK + tb * 512 + 512],
                     TOK, LST)
            ln_stats(lambda c, tb: XN2[:, c * TOK + tb * 512: c * TOK + tb * 512 + 512],
                     TOK, LST)
            nc.vector.wait_ge(ld.h, ld.n)
            g.bar()
            ln_apply(lambda c, tb: XN2[:, c * TOK + tb * 512: c * TOK + tb * 512 + 512],
                     lambda c, tb: XN2[:, c * TOK + tb * 512: c * TOK + tb * 512 + 512],
                     TOK, LST,
                     aff=(COLS[:, 2 * KC:3 * KC], COLS[:, 3 * KC:4 * KC]))

            if first_call:
                mx = g.sem("fmx")
                wl = g.sem("fwl")
                nc.vector.memset(WMX[:, 0:1], 0.0)
                g.inc(nc.vector.drain(), mx)
                with nc.sbuf_tensor(f"WBS{g.uid()}", [128, 2 * KC * 128], F32) as WB:
                    for ob in range(16):
                        buf = WB[:, (ob % 2) * KC * 128:(ob % 2) * KC * 128 + KC * 128]
                        if ob >= 2:
                            nc.sync.wait_ge(mx.h, mx.n - 2)
                        g.inc(nc.sync.dma_start(
                            out=buf, in_=i["l0f_wT"][:, ob * KC * 128:(ob + 1) * KC * 128]), wl, 16)
                        nc.vector.wait_ge(wl.h, wl.off + 16 * (ob + 1))
                        nc.vector.tensor_reduce(WMX[:, 1:2], buf, axis=AXL.X, op=ALU.max,
                                                apply_absolute_value=True)
                        nc.vector.drain()
                        nc.vector.tensor_tensor(WMX[:, 0:1], WMX[:, 0:1], WMX[:, 1:2], op=ALU.max)
                        g.inc(nc.vector.drain(), mx)
                g.bar()
                cross_maxP(WMX[:, 0:1], SC[0:1, 8:9], 128)
            sb = g.sem("fsb")
            scalar_bcast(SC[0:1, 8:9], WMX[:, 0:1], 128)
            nc.vector.tensor_scalar(COLS[:, 4 * KC:5 * KC], COLS[:, 0:KC], WMX[:, 0:1], None,
                                    op0=ALU.mult)
            g.inc(nc.vector.drain(), sb)
            g.bar()

            mm = g.sem("fmm")
            ev = g.sem("fev")
            wl = g.sem("fwl2")
            sg = g.sem("fsg")
            with nc.sbuf_tensor(f"WBF{g.uid()}", [128, 2 * KC * 128], F32) as WB:
                j = 0
                for ob in range(16):
                    buf = WB[:, (ob % 2) * KC * 128:(ob % 2) * KC * 128 + KC * 128]
                    if ob >= 2:
                        nc.sync.wait_ge(mm.h, mm.off + (ob - 1) * 2)
                    g.inc(nc.sync.dma_start(
                        out=buf, in_=i["l0f_wT"][:, ob * KC * 128:(ob + 1) * KC * 128]), wl, 16)
                    nc.scalar.wait_ge(wl.h, wl.off + 16 * (ob + 1))
                    g.inc(nc.scalar.activation(buf, buf, AF.Sign), sg)
                    nc.tensor.wait_ge(sg.h, sg.n)
                    for tb in range(2):
                        psb = pb[j % 2]
                        if j >= 2:
                            nc.tensor.wait_ge(ev.h, ev.off + j - 1)
                        for c in range(KC):
                            MM(psb[:, :], buf[:, c * 128:(c + 1) * 128],
                                             XN2[:, c * TOK + tb * 512: c * TOK + tb * 512 + 512],
                                             start=(c == 0), stop=(c == KC - 1))
                        g.inc(g.lastT, mm)
                        nc.vector.wait_ge(mm.h, mm.n)
                        ebuf = EV[:, (j % 2) * 512:(j % 2) * 512 + 512]
                        nc.vector.tensor_scalar(ebuf, psb[:, :],
                                                COLS[:, 4 * KC + ob:4 * KC + ob + 1],
                                                COLS[:, KC + ob:KC + ob + 1],
                                                op0=ALU.mult, op1=ALU.add)
                        nc.vector.drain()
                        sl = slice(ob * TOK + tb * 512, ob * TOK + tb * 512 + 512)
                        nc.vector.tensor_tensor(X[:, sl], X[:, sl], ebuf, op=ALU.add)
                        g.inc(nc.vector.drain(), ev)
                        j += 1
            g.bar()

    def s3_l0_attn_b():
        from contextlib import ExitStack
        ld = g.sem("ldb")
        ccs = g.sem("bccs")
        esq2 = ExitStack()
        QU = esq2.enter_context(nc.sbuf_tensor("QU_b", [128, 8 * TOK], F32))
        with nc.sbuf_tensor("XN_b", [128, KC * TOK], F32) as XN3, \
             nc.sbuf_tensor("LST_b", [1, 2 * TOK], F32) as LST:
            ln_stats(lambda c, tb: X[:, c * TOK + tb * 512: c * TOK + tb * 512 + 512],
                     TOK, LST)
            ln_apply(lambda c, tb: X[:, c * TOK + tb * 512: c * TOK + tb * 512 + 512],
                     lambda c, tb: XN3[:, c * TOK + tb * 512: c * TOK + tb * 512 + 512],
                     TOK, LST)

            mm = g.sem("bmm")
            ev = g.sem("bev")
            st = g.sem("bst")
            with nc.sbuf_tensor("WKVD", [128, KC * 256], F32) as WKVD:
                g.inc(nc.sync.dma_start(out=WKVD[:, :], in_=i["l0b_wkvd"][:, :]), ld, 16)
                nc.tensor.wait_ge(ld.h, ld.n)
                for tt in range(8):
                    psb = pb[tt % 2]
                    if tt >= 2:
                        nc.tensor.wait_ge(ev.h, ev.n - 1)
                    for c in range(KC):
                        MM(psb[:, 0:256],
                                         XN3[:, c * TOK + tt * 128: c * TOK + tt * 128 + 128],
                                         WKVD[:, c * 256:(c + 1) * 256],
                                         start=(c == 0), stop=(c == KC - 1))
                    g.inc(g.lastT, mm)
                    nc.vector.wait_ge(mm.h, mm.n)
                    ebuf = EV[:, (tt % 2) * 512:(tt % 2) * 512 + 256]
                    nc.vector.tensor_copy(ebuf, psb[:, 0:256])
                    g.inc(nc.vector.drain(), ev)
                    nc.sync.wait_ge(ev.h, ev.n)
                    g.inc(nc.sync.dma_start(out=cc_lat_in[tt * 128:(tt + 1) * 128, :],
                                            in_=ebuf), st, 16)
            nc.gpsimd.wait_ge(st.h, st.n)
            g.inc(nc.gpsimd.collective_compute(
                "AllGather", ALU.bypass, replica_groups=RG,
                ins=[cc_lat_in[:, :]], outs=[cc_lat_out[:, :]]), ccs)

            with nc.sbuf_tensor("QL_b", [128, 2 * TOK], F32) as QL:
                def evac_ql(ob, tb, ps, mm_, ev_, j):
                    nc.vector.wait_ge(mm_.h, mm_.n)
                    nc.vector.tensor_copy(QL[:, ob * TOK + tb * 512: ob * TOK + tb * 512 + 512], ps)
                    g.inc(nc.vector.drain(), ev_)
                proj(i["l0b_wqd"], 2,
                     lambda c, tb: XN3[:, c * TOK + tb * 512: c * TOK + tb * 512 + 512],
                     2, evac_ql)

                def evac_qu(ob, tb, ps, mm_, ev_, j):
                    nc.vector.wait_ge(mm_.h, mm_.n)
                    nc.vector.tensor_copy(QU[:, ob * TOK + tb * 512: ob * TOK + tb * 512 + 512], ps)
                    g.inc(nc.vector.drain(), ev_)
                proj(i["l0b_wqu"], 8,
                     lambda c, tb: QL[:, c * TOK + tb * 512: c * TOK + tb * 512 + 512],
                     2, evac_qu, kc=2)

        stq = g.sem("stq")
        for ch in range(8):
            g.inc(nc.sync.dma_start(out=qstash[ch, :, :], in_=QU[:, ch * TOK:(ch + 1) * TOK]),
                  stq, 16)
        nc.sync.wait_ge(stq.h, stq.n)
        g.bar()
        esq2.close()

        es = ExitStack()
        COSQ = es.enter_context(nc.sbuf_tensor("COSQ_b", [64, TOK], F32))
        SINQ = es.enter_context(nc.sbuf_tensor("SINQ_b", [64, TOK], F32))
        COSK = es.enter_context(nc.sbuf_tensor("COSK_b", [64, S], F32))
        SINK = es.enter_context(nc.sbuf_tensor("SINK_b", [64, S], F32))
        LATG = es.enter_context(nc.sbuf_tensor("LATG_b", [128, 2 * S], F32))

        mm = g.sem("rmm")
        ev = g.sem("rev")
        sq = g.sem("rsq")
        with nc.sbuf_tensor("RF", [1, 64], F32) as RF, \
             nc.sbuf_tensor("TRW", [1, S], F32) as TRW:
            g.inc(nc.sync.dma_start(out=SC[0:1, 9:10], in_=i["tpos"][:, :]), ld, 16)
            nc.gpsimd.iota(RF[0:1, 0:32], pattern=[[1, 32]], base=0,
                           channel_multiplier=0, allow_small_or_imprecise_dtypes=True)
            nc.gpsimd.iota(TRW[0:1, :], pattern=[[1, S]], base=0,
                           channel_multiplier=0, allow_small_or_imprecise_dtypes=True)
            gp = g.sem("rgp")
            g.inc(nc.gpsimd.drain(), gp)
            nc.vector.wait_ge(gp.h, gp.n)
            nc.vector.tensor_scalar_mul(RF[0:1, 0:32], RF[0:1, 0:32],
                                        -2.0 / 64.0 * 9.210340371976184)
            g.inc(nc.vector.drain(), ev)
            nc.scalar.wait_ge(ev.h, ev.n)
            g.inc(nc.scalar.activation(RF[0:1, 0:32], RF[0:1, 0:32], AF.Exp), sq)
            nc.vector.wait_ge(sq.h, sq.n)

            def build_cs(cos_dst, sin_dst, pos_row, ncols, pref):
                mm2 = g.sem(pref + "mm")
                ev2 = g.sem(pref + "ev")
                sq2 = g.sem(pref + "sq")
                I32 = mybir.dt.int32
                with nc.sbuf_tensor(f"ITMP{g.uid()}", [32, 512], I32) as IT:
                    for tb in range(ncols // 512):
                        g.inc(nc.vector.drain(), mm2)
                        g.inc(nc.scalar.drain(), mm2)
                        nc.tensor.wait_ge(mm2.h, mm2.n)
                        g.inc(MM(pb[4][0:32, :], RF[0:1, 0:32],
                                               pos_row[0:1, tb * 512:(tb + 1) * 512],
                                               start=True, stop=True), mm2)
                        nc.vector.wait_ge(mm2.h, mm2.n)
                        for dst, shiftv in ((sin_dst, 0.0), (cos_dst, 0.25)):
                            wk = SQ[0:32, 0:512]
                            wk2 = SQ[0:32, 512:1024]
                            nc.vector.tensor_scalar(wk, pb[4][0:32, :],
                                                    1.0 / 6.283185307179586, shiftv,
                                                    op0=ALU.mult, op1=ALU.add)
                            nc.vector.drain()
                            nc.vector.tensor_copy(IT[:, :], wk)
                            nc.vector.drain()
                            nc.vector.tensor_copy(wk2, IT[:, :])
                            nc.vector.drain()
                            nc.vector.tensor_tensor(wk, wk, wk2, op=ALU.subtract)
                            nc.vector.drain()
                            nc.vector.tensor_scalar(wk2, wk, 0.5, None, op0=ALU.is_ge)
                            nc.vector.drain()
                            nc.vector.scalar_tensor_tensor(wk, wk2, -1.0, wk,
                                                           op0=ALU.mult, op1=ALU.add)
                            nc.vector.drain()
                            nc.vector.tensor_scalar(wk2, wk, -0.5, None, op0=ALU.is_lt)
                            nc.vector.drain()
                            nc.vector.scalar_tensor_tensor(wk, wk2, 1.0, wk,
                                                           op0=ALU.mult, op1=ALU.add)
                            nc.vector.drain()
                            nc.vector.tensor_scalar_mul(wk, wk, 6.283185307179586)
                            g.inc(nc.vector.drain(), ev2)
                            nc.scalar.wait_ge(ev2.h, ev2.n)
                            g.inc(nc.scalar.activation(dst[0:32, tb * 512:tb * 512 + 512],
                                                       wk, AF.Sin), sq2)
                            nc.vector.wait_ge(sq2.h, sq2.n)
                        g.inc(nc.vector.drain(), mm2)
                        nc.tensor.wait_ge(mm2.h, mm2.n)
                nc.vector.tensor_copy(cos_dst[32:64, 0:ncols], cos_dst[0:32, 0:ncols])
                nc.vector.drain()
                nc.vector.tensor_copy(sin_dst[32:64, 0:ncols], sin_dst[0:32, 0:ncols])
                nc.vector.drain()
                nc.vector.tensor_scalar_mul(sin_dst[0:32, 0:ncols], sin_dst[0:32, 0:ncols],
                                            -1.0)
                nc.vector.drain()

            with nc.sbuf_tensor("TRQ", [1, TOK], F32) as TRQ:
                nc.vector.wait_ge(ld.h, ld.n)
                nc.vector.tensor_scalar(TRQ[0:1, :], TRW[0:1, 0:TOK], SC[0:1, 9:10], None,
                                        op0=ALU.add)
                nc.vector.drain()
                build_cs(COSQ, SINQ, TRQ, TOK, "q")
            build_cs(COSK, SINK, TRW, S, "k")
        nc.vector.tensor_scalar_mul(COSQ[0:64, :], COSQ[0:64, :], 0.125)
        nc.vector.drain()
        nc.vector.tensor_scalar_mul(SINQ[0:64, :], SINQ[0:64, :], 0.125)
        nc.vector.drain()
        g.bar()

        mm = g.sem("tmm")
        ev = g.sem("tev")
        ld2 = g.sem("tld")
        nc.sync.wait_ge(ccs.h, ccs.n)
        with nc.sbuf_tensor("LTT", [128, 2 * 256], F32) as LTT:
            for tt in range(32):
                buf = LTT[:, (tt % 2) * 256:(tt % 2) * 256 + 256]
                if tt >= 2:
                    nc.sync.wait_ge(mm.h, mm.n - 3)
                g.inc(nc.sync.dma_start(out=buf, in_=cc_lat_out[tt * 128:(tt + 1) * 128, :]),
                      ld2, 16)
                nc.tensor.wait_ge(ld2.h, ld2.off + 16 * (tt + 1))
                for c2 in range(2):
                    TR(pb[2][:, 0:128], buf[:, c2 * 128:(c2 + 1) * 128],
                                        IDENT[:, :])
                    g.inc(g.lastT, mm)
                    nc.vector.wait_ge(mm.h, mm.n)
                    nc.vector.tensor_copy(LATG[:, c2 * S + tt * 128: c2 * S + tt * 128 + 128],
                                          pb[2][:, 0:128])
                    g.inc(nc.vector.drain(), ev)
                    nc.tensor.wait_ge(ev.h, ev.n)
        g.bar()

        with nc.sbuf_tensor("KH_b", [64, S], F32) as KH3, \
             nc.sbuf_tensor("VH_b", [128, 32 * 130], F32) as VH3, \
             nc.sbuf_tensor("WKU", [128, 2 * 128], F32) as WKU, \
             nc.sbuf_tensor("WVU", [128, 2 * 128], F32) as WVU, \
             nc.sbuf_tensor("RSK", [64, 512], F32) as RSK, \
             nc.sbuf_tensor("QHH_b", [64, TOK], F32) as QHH, \
             nc.sbuf_tensor("QSH_b", [64, TOK], F32) as QSH:
            PT = EV
            for kt in range(32):
                nc.vector.memset(VH3[:, kt * 130 + 64:kt * 130 + 65], 1.0)
                nc.vector.memset(VH3[:, kt * 130 + 129:kt * 130 + 130], 1.0)
            nc.vector.drain()
            g.bar()
            hl = g.sem("fhl")
            mm = g.sem("fmm3")
            mo = g.sem("fmo3")
            sq = g.sem("fsq3")
            ev = g.sem("fev3")
            st = g.sem("fst3")
            for h in range(16):
                rbase = 64 * (h % 2)
                cch = h // 2
                g.inc(nc.sync.dma_start(out=WKU[:, :],
                                        in_=i["l0b_wku"][:, (h // 2) * 256:(h // 2 + 1) * 256]),
                      hl, 16)
                g.inc(nc.sync.dma_start(out=WVU[:, :],
                                        in_=i["l0b_wvu"][:, h * 256:(h + 1) * 256]), hl, 16)
                g.inc(nc.sync.dma_start(out=QHH[:, :], in_=qstash[cch, rbase:rbase + 64, :]),
                      hl, 16)
                nc.tensor.wait_ge(hl.h, hl.n)
                for tb in range(8):
                    psb = pb[tb % 2]
                    if tb >= 2:
                        nc.tensor.wait_ge(ev.h, ev.n - 1)
                    for c2 in range(2):
                        MM(psb[0:64, :],
                                         WKU[:, c2 * 128 + 64 * (h % 2):
                                             c2 * 128 + 64 * (h % 2) + 64],
                                         LATG[:, c2 * S + tb * 512: c2 * S + tb * 512 + 512],
                                         start=(c2 == 0), stop=(c2 == 1))
                    g.inc(g.lastT, mm)
                    nc.vector.wait_ge(mm.h, mm.n)
                    nc.vector.tensor_copy(KH3[:, tb * 512:(tb + 1) * 512], psb[0:64, :])
                    g.inc(nc.vector.drain(), ev)
                for tt in range(32):
                    psb = pb[2 + (tt % 2) * 3]
                    if tt >= 2:
                        nc.tensor.wait_ge(ev.h, ev.n - 1)
                    for c2 in range(2):
                        MM(psb[:, 0:128],
                                         LATG[:, c2 * S + tt * 128: c2 * S + tt * 128 + 128],
                                         WVU[:, c2 * 128:(c2 + 1) * 128],
                                         start=(c2 == 0), stop=(c2 == 1))
                    g.inc(g.lastT, mm)
                    nc.vector.wait_ge(mm.h, mm.n)
                    nc.vector.tensor_copy(VH3[:, tt * 130:tt * 130 + 64], psb[:, 0:64])
                    nc.vector.drain()
                    nc.vector.tensor_copy(VH3[:, tt * 130 + 65:tt * 130 + 129], psb[:, 64:128])
                    g.inc(nc.vector.drain(), ev)
                g.bar()
                nc.vector.tensor_copy(QSH[0:32, :], QHH[32:64, :])
                nc.vector.drain()
                nc.vector.tensor_copy(QSH[32:64, :], QHH[0:32, :])
                nc.vector.drain()
                nc.vector.tensor_tensor(QHH[:, :], QHH[:, :], COSQ[0:64, :], op=ALU.mult)
                nc.vector.drain()
                nc.vector.tensor_tensor(QSH[:, :], QSH[:, :], SINQ[0:64, :], op=ALU.mult)
                nc.vector.drain()
                nc.vector.tensor_tensor(QHH[:, :], QHH[:, :], QSH[:, :], op=ALU.add)
                nc.vector.drain()
                for tb in range(8):
                    ksl = KH3[:, tb * 512:(tb + 1) * 512]
                    csl = COSK[0:64, tb * 512:(tb + 1) * 512]
                    ssl = SINK[0:64, tb * 512:(tb + 1) * 512]
                    nc.vector.tensor_copy(RSK[0:32, :], ksl[32:64, :])
                    nc.vector.drain()
                    nc.vector.tensor_copy(RSK[32:64, :], ksl[0:32, :])
                    nc.vector.drain()
                    nc.vector.tensor_tensor(ksl[0:64, :], ksl[0:64, :], csl, op=ALU.mult)
                    nc.vector.drain()
                    nc.vector.tensor_tensor(RSK[0:64, :], RSK[0:64, :], ssl, op=ALU.mult)
                    nc.vector.drain()
                    nc.vector.tensor_tensor(ksl[0:64, :], ksl[0:64, :], RSK[0:64, :],
                                            op=ALU.add)
                    nc.vector.drain()
                g.inc(nc.vector.drain(), ev)
                kn = g.sem(f"kn{h}")
                nc.vector.memset(SC[0:1, 10:11], 0.0)
                nc.vector.memset(SC[0:1, 13:14], 0.0)
                g.inc(nc.vector.drain(), kn)
                nc.scalar.wait_ge(ev.h, ev.n)
                for src_, ncols_, slot in ((KH3, 8, 10), (QHH, 2, 13)):
                    for tb in range(ncols_):
                        buf = SQ[0:64, (tb % 2) * 512:(tb % 2) * 512 + 512]
                        if tb >= 2 or slot == 13:
                            nc.scalar.wait_ge(kn.h, kn.n - 1)
                        g.inc(nc.scalar.activation(buf, src_[0:64, tb * 512:(tb + 1) * 512],
                                                   AF.Square), sq)
                        nc.tensor.wait_ge(sq.h, sq.n)
                        nc.tensor.wait_ge(kn.h, kn.n)
                        g.inc(MM(pst[0][0:1, :], ones[0:64, :], buf,
                                               start=True, stop=True), mm)
                        nc.vector.wait_ge(mm.h, mm.n)
                        nc.vector.tensor_reduce(SC[0:1, 11:12], pst[0][0:1, :], axis=AXL.X,
                                                op=ALU.max)
                        nc.vector.drain()
                        nc.vector.tensor_tensor(SC[0:1, slot:slot + 1], SC[0:1, slot:slot + 1],
                                                SC[0:1, 11:12], op=ALU.max)
                        g.inc(nc.vector.drain(), kn)
                nc.vector.tensor_tensor(SC[0:1, 12:13], SC[0:1, 10:11], SC[0:1, 13:14],
                                        op=ALU.mult)
                g.inc(nc.vector.drain(), ev)
                nc.scalar.wait_ge(ev.h, ev.n)
                g.inc(nc.scalar.activation(SC[0:1, 12:13], SC[0:1, 12:13], AF.Sqrt), sq)
                nc.vector.wait_ge(sq.h, sq.n)
                nc.vector.tensor_scalar_mul(SC[0:1, 12:13], SC[0:1, 12:13], -1.0)
                nc.vector.drain()
                scalar_bcast(SC[0:1, 12:13], CREP[:, 0:1], 128)
                g.bar()

                for qb in range(2):
                    for kt in range(32):
                        psb = pb[kt % 2]
                        if kt >= 2:
                            nc.tensor.wait_ge(sq.h, sq.n - 1)
                        MM(psb[:, :], KH3[:, kt * 128:(kt + 1) * 128],
                                         QHH[:, qb * 512:(qb + 1) * 512],
                                         start=True, stop=True)
                        g.inc(g.lastT, mm)
                        ptile = PT[:, (kt % 2) * 512:(kt % 2) * 512 + 512]
                        nc.scalar.wait_ge(mm.h, mm.n)
                        if kt >= 2:
                            nc.scalar.wait_ge(mo.h, mo.n - 2)
                        g.inc(nc.scalar.activation(ptile, psb[:, :], AF.Exp,
                                                   bias=CREP[:, 0:1]), sq)
                        nc.tensor.wait_ge(sq.h, sq.n)
                        g.inc(MM(pb[3][0:65, :],
                                               VH3[:, kt * 130:kt * 130 + 65],
                                               ptile, start=(kt == 0), stop=(kt == 31),
                                               skip_group_check=True), mo)
                        g.inc(MM(pb[4][0:65, :],
                                               VH3[:, kt * 130 + 65:kt * 130 + 130],
                                               ptile, start=(kt == 0), stop=(kt == 31),
                                               skip_group_check=True), mo)
                    nc.vector.wait_ge(mo.h, mo.n)
                    nc.vector.reciprocal(RECIP[0:1, :], pb[3][64:65, :])
                    g.inc(nc.vector.drain(), ev)
                    nc.tensor.wait_ge(ev.h, ev.n)
                    g.inc(MM(pb[5][0:64, :], ONESR[0:1, 0:64],
                                           RECIP[0:1, :], start=True, stop=True), mm)
                    nc.vector.wait_ge(mm.h, mm.n)
                    nc.vector.tensor_copy(SQ[64:128, 0:512], pb[5][0:64, :])
                    nc.vector.drain()
                    obuf = EV[:, 0:512]
                    nc.vector.tensor_tensor(obuf[0:64, :], pb[3][0:64, :], SQ[64:128, 0:512],
                                            op=ALU.mult)
                    nc.vector.drain()
                    nc.vector.tensor_tensor(obuf[64:128, :], pb[4][0:64, :], SQ[64:128, 0:512],
                                            op=ALU.mult)
                    g.inc(nc.vector.drain(), ev)
                    nc.tensor.wait_ge(ev.h, ev.n)
                    nc.sync.wait_ge(ev.h, ev.n)
                    g.inc(nc.sync.dma_start(out=bstash[h, :, qb * 512:(qb + 1) * 512],
                                            in_=obuf), st, 16)
                    nc.scalar.wait_ge(st.h, st.n)
                g.bar()
        es.close()
        g.bar()

        with nc.sbuf_tensor("OST_b", [128, KC * TOK], F32) as OST:
            ld3 = g.sem("wold")
            g.inc(nc.sync.dma_start(
                out=OST[:].rearrange("p (c t) -> p c t", c=KC),
                in_=AP(bstash.tensor, 0, [[TOK, 128], [128 * TOK, 16], [1, TOK]])), ld3, 16)
            nc.sync.wait_ge(ld3.h, ld3.n)
            g.bar()

            def evac_res3(ob, tb, ps, mm_, ev_, j):
                nc.vector.wait_ge(mm_.h, mm_.n)
                sl = slice(ob * TOK + tb * 512, ob * TOK + tb * 512 + 512)
                nc.vector.tensor_tensor(X[:, sl], X[:, sl], ps, op=ALU.add)
                g.inc(nc.vector.drain(), ev_)

            proj(i["l0b_wo"], 16,
                 lambda c, tb: OST[:, c * TOK + tb * 512: c * TOK + tb * 512 + 512],
                 2, evac_res3)


    def s5_l1_attn_a():
        from contextlib import ExitStack
        ccs = g.sem("a1cc")
        with nc.sbuf_tensor("XN_5", [128, KC * TOK], F32) as XN5, \
             nc.sbuf_tensor("LST_5", [1, 2 * TOK], F32) as LST:
            ln_stats(lambda c, tb: X[:, c * TOK + tb * 512: c * TOK + tb * 512 + 512],
                     TOK, LST)
            ln_apply(lambda c, tb: X[:, c * TOK + tb * 512: c * TOK + tb * 512 + 512],
                     lambda c, tb: XN5[:, c * TOK + tb * 512: c * TOK + tb * 512 + 512],
                     TOK, LST)
            st = g.sem("a1st")
            g.inc(nc.sync.dma_start(
                out=AP(cc_xn_in.tensor, 0, [[TOK, 128], [128 * TOK, KC], [1, TOK]]),
                in_=XN5[:].rearrange("p (c t) -> p c t", c=KC)), st, 16)
            nc.gpsimd.wait_ge(st.h, st.n)
            g.inc(nc.gpsimd.collective_compute(
                "AllGather", ALU.bypass, replica_groups=RG,
                ins=[cc_xn_in[:, :]], outs=[cc_xn_out[:, :]]), ccs)
        g.bar()

        with nc.sbuf_tensor("WQ5", [128, 4 * KC * 128], F32) as WQ5, \
             nc.sbuf_tensor("WK5", [128, 4 * KC * 128], F32) as WK5, \
             nc.sbuf_tensor("WV5", [128, 4 * KC * 128], F32) as WV5, \
             nc.sbuf_tensor("WIL5", [128, KC * 8], F32) as WIL5, \
             nc.sbuf_tensor("XG5", [128, KC * 512], F32) as XG5:
            ld = g.sem("a1ld")
            mm = g.sem("a1mm")
            ev = g.sem("a1ev")
            st = g.sem("a1st2")
            xl = g.sem("a1xl")
            for wt, nm in ((WQ5, "a1_wq"), (WK5, "a1_wk"), (WV5, "a1_wv"),
                           (WIL5, "a1_wilr")):
                g.inc(nc.sync.dma_start(out=wt[:], in_=i[nm][:, :]), ld, 16)
            nc.sync.wait_ge(ccs.h, ccs.n)
            nc.tensor.wait_ge(ld.h, ld.n)
            for gt in range(8):
                Q_, tb = divmod(gt, 2)
                buf = XG5[:, 0:KC * 512]
                if gt >= 1:
                    nc.sync.wait_ge(mm.h, mm.off + gt * 13)
                g.inc(nc.sync.dma_start(
                    out=buf.rearrange("p (c t) -> p c t", c=KC),
                    in_=AP(cc_xn_out.tensor, Q_ * D * TOK + tb * 512,
                           [[TOK, 128], [128 * TOK, KC], [1, 512]])), xl, 16)
                nc.tensor.wait_ge(xl.h, xl.off + 16 * (gt + 1))
                for wt, dst in ((WQ5, xcfm), (WK5, xbfm), (WV5, xafm)):
                    for ob in range(4):
                        psb = pb[ob % 2]
                        nc.tensor.wait_ge(ev.h, max(ev.off, ev.n - 1))
                        for c in range(KC):
                            MM(
                                psb[:, :],
                                wt[:, (ob * KC + c) * 128:(ob * KC + c) * 128 + 128],
                                buf[:, c * 512:(c + 1) * 512],
                                start=(c == 0), stop=(c == KC - 1))
                        g.inc(g.lastT, mm)
                        nc.vector.wait_ge(mm.h, mm.n)
                        ebuf = EV[:, (ob % 2) * 512:(ob % 2) * 512 + 512]
                        nc.vector.tensor_copy(ebuf, psb[:, :])
                        g.inc(nc.vector.drain(), ev)
                        nc.sync.wait_ge(ev.h, ev.n)
                        g.inc(nc.sync.dma_start(
                            out=dst[ob, :, Q_ * TOK + tb * 512: Q_ * TOK + tb * 512 + 512],
                            in_=ebuf), st, 16)
                nc.tensor.wait_ge(ev.h, max(ev.off, ev.n - 1))
                for c in range(KC):
                    MM(pb[2][0:8, :], WIL5[:, c * 8:(c + 1) * 8],
                                     buf[:, c * 512:(c + 1) * 512],
                                     start=(c == 0), stop=(c == KC - 1))
                g.inc(g.lastT, mm)
                nc.vector.wait_ge(mm.h, mm.n)
                nc.vector.tensor_copy(EV[0:8, 0:512], pb[2][0:8, :])
                g.inc(nc.vector.drain(), ev)
                nc.sync.wait_ge(ev.h, ev.n)
                g.inc(nc.sync.dma_start(
                    out=ilrst[:, Q_ * TOK + tb * 512: Q_ * TOK + tb * 512 + 512],
                    in_=EV[0:8, 0:512]), st, 16)
            nc.sync.wait_ge(st.h, st.n)
        g.bar()

        with nc.sbuf_tensor("CN5", [128, 21 * 128], F32) as CN, \
             nc.sbuf_tensor("CL5", [128, 40], F32) as CL, \
             nc.sbuf_tensor("SB5", [128, 19 * 64], F32) as SB, \
             nc.sbuf_tensor("XGB5", [128, 2 * 4 * 1024], F32) as XGB, \
             nc.sbuf_tensor("XGC5", [128, 2 * 4 * 1024], F32) as XGC, \
             nc.sbuf_tensor("XGA5", [128, 2 * 4 * 1024], F32) as XGA, \
             nc.sbuf_tensor("XCW5", [128, 4 * 1024], F32) as XCW:
            ld = g.sem("a1ld2")
            mm = g.sem("a1mm2")
            ev = g.sem("a1ev2")
            sq = g.sem("a1sq2")
            st = g.sem("a1st3")
            cld = g.sem("a1cld")
            gp = g.sem("a1gp")
            W1ST = SB[:, 0:256]
            B1ST = SB[0:8, 256:320]
            LNW = SB[:, 5 * 64:6 * 64]
            LNB = SB[:, 6 * 64:7 * 64]
            XBT = SB[:, 7 * 64:8 * 64]
            XCT = SB[:, 8 * 64:9 * 64]
            TGT = SB[:, 9 * 64:10 * 64]
            ZC = SB[:, 10 * 64:11 * 64]
            XH = SB[:, 11 * 64:12 * 64]
            GG = SB[:, 12 * 64:13 * 64]
            GRD = SB[:, 13 * 64:14 * 64]
            GRDC = SB[:, 14 * 64:15 * 64]
            OUTT = SB[:, 15 * 64:16 * 64]
            LNW2 = SB[:, 16 * 64:17 * 64]
            T2 = SB[:, 17 * 64:18 * 64]
            XCB = SB[:, 18 * 64:19 * 64]
            DGB = CN[:, 12 * 128:16 * 128]
            TRILM = CN[:, 8 * 128:9 * 128]
            I2 = CN[:, 9 * 128:9 * 128 + 64]
            IOTC = CN[:, 10 * 128:11 * 128]
            TSEL = CN[:, 11 * 128:11 * 128 + 16]
            HSELT = CN[:, 11 * 128 + 16:11 * 128 + 24]
            SEL15 = CN[:, 11 * 128 + 24:11 * 128 + 32]
            HSEL2 = CN[0:8, 11 * 128 + 32:11 * 128 + 32 + 96][0:8, 0:0 + 0] if False else None
            for nm, col in (("idxcol", 0), ("pcol", 1), ("phcol", 2), ("pm16col", 3)):
                g.inc(nc.sync.dma_start(out=CL[:, col:col + 1], in_=i[nm][:, :]), ld, 16)
            g.inc(nc.sync.dma_start(out=CL[0:8, 4:5], in_=i["a1_bilr8"][:, :]), ld, 16)
            g.inc(nc.sync.dma_start(out=W1ST, in_=i["a1_w1init"][:, :]), ld, 16)
            g.inc(nc.sync.dma_start(out=B1ST, in_=i["a1_b1init"][:, :]), ld, 16)
            g.inc(nc.sync.dma_start(out=SB[0:8, 5 * 64:6 * 64], in_=i["a1_lnw"][:, :]), ld, 16)
            g.inc(nc.sync.dma_start(out=SB[0:8, 6 * 64:7 * 64], in_=i["a1_lnb"][:, :]), ld, 16)
            nc.vector.memset(CL[:, 5:6], 1e-6)
            nc.vector.wait_ge(ld.h, ld.n)
            for c in range(4):
                m3 = CN[:, c * 128:(c + 1) * 128]
                nc.vector.memset(m3, 0.0)
                nc.vector.drain()
                nc.vector.memset(m3[0:64, 16 * (2 * c):16 * (2 * c) + 16], 1.0)
                nc.vector.drain()
                nc.vector.memset(m3[64:128, 16 * (2 * c + 1):16 * (2 * c + 1) + 16], 1.0)
                nc.vector.drain()
            nc.gpsimd.iota(IOTC, pattern=[[1, 128]], base=0, channel_multiplier=0,
                           allow_small_or_imprecise_dtypes=True)
            g.inc(nc.gpsimd.drain(), gp)
            nc.vector.wait_ge(gp.h, gp.n)
            nc.vector.tensor_scalar(CL[:, 21:22], CL[:, 2:3], 16.0, 16.0,
                                    op0=ALU.mult, op1=ALU.add)
            nc.vector.drain()
            nc.vector.tensor_scalar(TRILM, IOTC, CL[:, 1:2], None, op0=ALU.is_ge)
            nc.vector.drain()
            nc.vector.tensor_scalar(CN[:, 9 * 128:10 * 128], IOTC, CL[:, 21:22], None,
                                    op0=ALU.is_lt)
            nc.vector.drain()
            nc.vector.tensor_tensor(TRILM, TRILM, CN[:, 9 * 128:10 * 128], op=ALU.mult)
            nc.vector.drain()
            HI64 = CN[:, 9 * 128:10 * 128]
            nc.vector.tensor_scalar(HI64, IOTC, 63.5, None, op0=ALU.is_ge)
            nc.vector.drain()
            for c in range(4):
                mo = CN[:, (4 + c) * 128:(5 + c) * 128]
                nc.vector.tensor_scalar(mo, HI64, float(2 * c), None, op0=ALU.add)
                nc.vector.drain()
                nc.vector.tensor_scalar(mo, mo, CL[:, 2:3], None, op0=ALU.is_equal)
                nc.vector.drain()
            nc.vector.tensor_copy(I2[0:64, :], IDENT[0:64, 0:64])
            nc.vector.drain()
            nc.vector.tensor_copy(I2[64:128, :], IDENT[0:64, 0:64])
            nc.vector.drain()
            nc.vector.tensor_scalar(TSEL, IOTC[:, 0:16], CL[:, 3:4], None, op0=ALU.is_equal)
            nc.vector.drain()
            nc.vector.tensor_scalar(HSELT, IOTC[:, 0:8], CL[:, 2:3], None, op0=ALU.is_equal)
            nc.vector.drain()
            nc.vector.tensor_scalar(SEL15, IOTC[:, 0:8], 16.0, 15.0, op0=ALU.mult,
                                    op1=ALU.add)
            nc.vector.drain()
            nc.vector.tensor_scalar(SEL15, SEL15, CL[:, 1:2], None, op0=ALU.is_equal)
            nc.vector.drain()
            BM16 = CN[:, 16 * 128:17 * 128]
            PHJ = CN[:, 17 * 128:17 * 128 + 8]
            RAW8 = CN[0:8, 18 * 128:19 * 128]
            SIG8 = CN[0:8, 17 * 128 + 16:17 * 128 + 16 + 128]
            TOKT = CN[:, 18 * 128 + 64:18 * 128 + 72]
            COTMP = CN[:, 19 * 128:20 * 128]
            nc.gpsimd.iota(BM16, pattern=[[0, 8], [1, 16]], base=0, channel_multiplier=0,
                           allow_small_or_imprecise_dtypes=True)
            g.inc(nc.gpsimd.drain(), gp)
            nc.vector.wait_ge(gp.h, gp.n)
            nc.vector.tensor_scalar(BM16, BM16, CL[:, 3:4], None, op0=ALU.is_equal)
            nc.vector.drain()
            nc.vector.tensor_scalar(PHJ, IOTC[:, 0:8], CL[:, 2:3], None, op0=ALU.is_equal)
            nc.vector.drain()
            HSEL2 = CN[0:8, 9 * 128 + 64:9 * 128 + 64 + 128][0:8, :] if False else None
            HSEL2 = CN[0:8, 20 * 128:21 * 128]
            nc.gpsimd.iota(HSEL2, pattern=[[1, 128]], base=0, channel_multiplier=-16,
                           allow_small_or_imprecise_dtypes=True)
            g.inc(nc.gpsimd.drain(), gp)
            nc.vector.wait_ge(gp.h, gp.n)
            nc.vector.tensor_scalar(CN[0:8, 12 * 128:13 * 128], HSEL2, -0.5, None,
                                    op0=ALU.is_ge)
            nc.vector.drain()
            nc.vector.tensor_scalar(HSEL2, HSEL2, 15.5, None, op0=ALU.is_le)
            nc.vector.drain()
            nc.vector.tensor_tensor(HSEL2, HSEL2, CN[0:8, 12 * 128:13 * 128], op=ALU.mult)
            g.inc(nc.vector.drain(), ev)
            nc.tensor.wait_ge(ev.h, ev.n)
            g.inc(MM(pb[0][:, 0:64], HSEL2, SB[0:8, 5 * 64:6 * 64],
                                   start=True, stop=True), mm)
            g.inc(MM(pb[1][:, 0:64], HSEL2, SB[0:8, 6 * 64:7 * 64],
                                   start=True, stop=True), mm)
            nc.vector.wait_ge(mm.h, mm.n)
            nc.vector.tensor_copy(LNW, pb[0][:, 0:64])
            nc.vector.drain()
            nc.vector.tensor_copy(LNB, pb[1][:, 0:64])
            nc.vector.drain()
            nc.vector.tensor_tensor(LNW2, LNW, LNW, op=ALU.mult)
            nc.vector.drain()
            g.bar()

            pf = g.sem("a1pf")

            def _pref(grp_):
                gb_ = grp_ % 2
                for dstb, srct in ((XGB, xbfm), (XGC, xcfm), (XGA, xafm)):
                    g.inc(nc.sync.dma_start(
                        out=dstb[:, gb_ * 4096:(gb_ + 1) * 4096].rearrange(
                            "p (c t) -> p c t", c=4),
                        in_=AP(srct.tensor, grp_ * 1024,
                               [[S, 128], [128 * S, 4], [1, 1024]])), pf, 16)

            _pref(0)
            for grp in range(4):
                gbuf = grp % 2
                nc.sync.wait_ge(pf.h, pf.off + 48 * (grp + 1))
                if grp < 3:
                    _pref(grp + 1)
                g.bar()
                for nn in range(64):
                    n = grp * 64 + nn
                    cof = nn % 8
                    if cof == 0:
                        nc.sync.wait_ge(sq.h, sq.n)
                        g.inc(nc.sync.dma_start(out=RAW8,
                                                in_=ilrst[:, n * 16:n * 16 + 128]), cld, 16)
                        nc.scalar.wait_ge(cld.h, cld.n)
                        nc.scalar.wait_ge(mm.h, mm.n)
                        g.inc(nc.scalar.activation(SIG8, RAW8, AF.Sigmoid,
                                                   bias=CL[0:8, 4:5]), sq)
                        nc.tensor.wait_ge(sq.h, sq.n)
                        TR(pb[3][:, 2:10], SIG8, IDENT[0:8, 0:8])
                        g.inc(g.lastT, mm)
                        nc.vector.wait_ge(mm.h, mm.n)
                        nc.vector.tensor_copy(TOKT, pb[3][:, 2:10])
                        nc.vector.drain()
                        nc.vector.tensor_tensor(
                            COTMP,
                            AP(TOKT.tensor, TOKT.offset,
                               [list(TOKT.ap[0]), [1, 8], [0, 16]]),
                            BM16, op=ALU.mult)
                        g.inc(nc.vector.drain(), ev)
                        nc.tensor.wait_ge(ev.h, ev.n)
                        g.inc(MM(pb[3][:, 16:24], COTMP, PHJ[:, 0:8],
                                 start=True, stop=True, skip_group_check=True), mm)
                        nc.vector.wait_ge(mm.h, mm.n)
                        nc.vector.tensor_scalar(CL[:, 12:20], pb[3][:, 16:24],
                                                CL[:, 0:1], None, op0=ALU.mult)
                        g.inc(nc.vector.drain(), ev)
                        nc.tensor.wait_ge(ev.h, ev.n)
                        g.inc(MM(pb[3][0:8, 32:40], SEL15, CL[:, 12:20],
                                 start=True, stop=True, skip_group_check=True), mm)
                        nc.vector.wait_ge(mm.h, mm.n)
                        nc.vector.tensor_copy(CL[0:8, 32:40], pb[3][0:8, 32:40])
                        g.inc(nc.vector.drain(), ev)
                        nc.tensor.wait_ge(ev.h, ev.n)
                        g.inc(MM(pb[3][:, 40:48], HSEL2, CL[0:8, 32:40],
                                 start=True, stop=True, skip_group_check=True), mm)
                        nc.vector.wait_ge(mm.h, mm.n)
                        nc.vector.tensor_copy(CL[:, 24:32], pb[3][:, 40:48])
                        g.inc(nc.vector.drain(), ev)
                        nc.tensor.wait_ge(ev.h, ev.n)
                    co_col = CL[:, 12 + cof:13 + cof]
                    cl_col = CL[:, 24 + cof:25 + cof]

                    def xsl(bufT, c):
                        base = gbuf * 4096 + c * 1024 + nn * 16
                        return bufT[:, base:base + 16]

                    def xbc(bufT, c):
                        s_ = xsl(bufT, c)
                        return AP(s_.tensor, s_.offset, [list(s_.ap[0]), [0, 8], [1, 16]])

                    for c in range(4):
                        nc.vector.tensor_tensor(DGB[:, c * 128:(c + 1) * 128],
                                                xbc(XGB, c), CN[:, c * 128:(c + 1) * 128],
                                                op=ALU.mult)
                    g.inc(nc.vector.drain(), ev)
                    nc.tensor.wait_ge(ev.h, ev.n)
                    for c in range(4):
                        dg = DGB[:, c * 128:(c + 1) * 128]
                        MM(pb[0][:, 0:64], dg, W1ST[:, c * 64:(c + 1) * 64],
                                         start=(c == 0), stop=False, skip_group_check=True)
                        MM(pb[4][:, 0:16], dg, xsl(XGC, c),
                                         start=(c == 0), stop=(c == 3),
                                         skip_group_check=True)
                        MM(pb[1][:, 0:64], dg, I2,
                                         start=(c == 0), stop=(c == 3),
                                         skip_group_check=True)
                    MM(pb[0][:, 0:64], HSEL2, B1ST, start=False, stop=True,
                                     skip_group_check=True)
                    g.inc(g.lastT, mm)
                    nc.vector.wait_ge(mm.h, mm.n)
                    for c in range(4):
                        nc.vector.tensor_tensor(DGB[:, c * 128:(c + 1) * 128],
                                                xbc(XGC, c), CN[:, c * 128:(c + 1) * 128],
                                                op=ALU.mult)
                    g.inc(nc.vector.drain(), ev)
                    nc.tensor.wait_ge(ev.h, ev.n)
                    for c in range(4):
                        dg = DGB[:, c * 128:(c + 1) * 128]
                        MM(pb[1][:, 64:128], dg, I2,
                                         start=(c == 0), stop=(c == 3),
                                         skip_group_check=True)
                        MM(pb[0][:, 128:192], dg,
                                         W1ST[:, c * 64:(c + 1) * 64],
                                         start=(c == 0), stop=False,
                                         skip_group_check=True)
                    MM(pb[0][:, 128:192], HSEL2, B1ST, start=False,
                                     stop=True, skip_group_check=True)
                    g.inc(g.lastT, mm)
                    nc.vector.wait_ge(mm.h, mm.n)
                    for c in range(4):
                        nc.vector.tensor_tensor(DGB[:, c * 128:(c + 1) * 128],
                                                xbc(XGA, c), CN[:, c * 128:(c + 1) * 128],
                                                op=ALU.mult)
                    g.inc(nc.vector.drain(), ev)
                    nc.tensor.wait_ge(ev.h, ev.n)
                    for c in range(4):
                        MM(pb[1][:, 128:192], DGB[:, c * 128:(c + 1) * 128],
                                         I2, start=(c == 0), stop=(c == 3),
                                         skip_group_check=True)
                    g.inc(g.lastT, mm)
                    nc.vector.wait_ge(mm.h, mm.n)
                    nc.vector.tensor_copy(XBT, pb[1][:, 0:64])
                    nc.vector.tensor_reduce(CL[:, 6:7], pb[0][:, 0:64], axis=AXL.X,
                                            op=ALU.add)
                    nc.vector.drain()
                    nc.vector.tensor_copy(XCT, pb[1][:, 64:128])
                    nc.vector.tensor_tensor(TGT, pb[1][:, 128:192], XBT, op=ALU.subtract)
                    nc.vector.tensor_scalar_mul(CL[:, 6:7], CL[:, 6:7], 1.0 / 64.0)
                    nc.vector.drain()
                    nc.vector.tensor_scalar(ZC, pb[0][:, 0:64], CL[:, 6:7], None,
                                            op0=ALU.subtract)
                    nc.vector.tensor_tensor(T2, LNB, TGT, op=ALU.subtract)
                    nc.vector.tensor_tensor(XCB, LNB, XCT, op=ALU.add)
                    nc.vector.drain()
                    nc.vector.tensor_tensor(T2, T2, LNW, op=ALU.mult)
                    g.inc(nc.vector.drain(), ev)
                    nc.scalar.wait_ge(ev.h, ev.n)
                    g.inc(nc.scalar.activation(XH, ZC, AF.Square, accum_out=CL[:, 11:12]),
                          sq)
                    nc.scalar.drain()
                    g.inc(nc.scalar.activation(CL[:, 7:8], CL[:, 11:12], AF.Sqrt,
                                               bias=CL[:, 5:6], scale=1.0 / 64.0), sq)
                    nc.vector.wait_ge(sq.h, sq.n)
                    nc.vector.reciprocal(CL[:, 7:8], CL[:, 7:8])
                    nc.vector.drain()
                    nc.vector.tensor_scalar(XH, ZC, CL[:, 7:8], None, op0=ALU.mult)
                    nc.vector.drain()
                    nc.vector.tensor_tensor(GG, XH, LNW2, op=ALU.mult)
                    nc.vector.drain()
                    nc.vector.tensor_tensor(GG, GG, T2, op=ALU.add)
                    nc.vector.drain()
                    nc.vector.tensor_scalar(GRD, GG, 1.0, 0.0, op0=ALU.mult,
                                            op1=ALU.add, accum_out=CL[:, 8:9])
                    nc.vector.tensor_tensor(GRDC, GG, XH, op=ALU.mult)
                    nc.vector.drain()
                    nc.vector.tensor_scalar(GRDC, GRDC, 1.0, 0.0, op0=ALU.mult,
                                            op1=ALU.add, accum_out=CL[:, 9:10])
                    nc.vector.drain()
                    nc.vector.tensor_tensor(CL[:, 11:12], CL[:, 8:9], CL[:, 7:8],
                                            op=ALU.mult)
                    nc.vector.drain()
                    nc.vector.tensor_scalar_mul(CL[:, 8:9], CL[:, 11:12], 1.0 / 64.0)
                    nc.vector.drain()
                    nc.vector.tensor_tensor(CL[:, 11:12], CL[:, 9:10], CL[:, 7:8],
                                            op=ALU.mult)
                    nc.vector.drain()
                    nc.vector.tensor_scalar_mul(CL[:, 9:10], CL[:, 11:12], 1.0 / 64.0)
                    nc.vector.drain()
                    nc.vector.tensor_scalar(GRD, GG, CL[:, 7:8], None, op0=ALU.mult)
                    nc.vector.drain()
                    nc.vector.tensor_scalar(GRDC, XH, CL[:, 9:10], None, op0=ALU.mult)
                    nc.vector.drain()
                    nc.vector.tensor_tensor(GRD, GRD, GRDC, op=ALU.subtract)
                    nc.vector.drain()
                    nc.vector.tensor_scalar(GRD, GRD, CL[:, 8:9], None, op0=ALU.subtract)
                    nc.vector.drain()
                    nc.vector.tensor_tensor(
                        DGB[:, 0:128],
                        AP(pb[4].ap().tensor, pb[4].ap().offset,
                           [list(pb[4].ap().ap[0]), [0, 8], [1, 16]]),
                        TRILM, op=ALU.mult)
                    g.inc(nc.vector.drain(), ev)
                    nc.tensor.wait_ge(ev.h, ev.n)
                    MM(pb[0][:, 192:256], DGB[:, 0:128], GRD,
                                     start=True, stop=False, skip_group_check=True)
                    MM(pb[0][:, 192:256], TRILM, GRD,
                                     start=False, stop=True, skip_group_check=True)
                    g.inc(g.lastT, mm)
                    nc.vector.wait_ge(mm.h, mm.n)
                    nc.vector.tensor_scalar(XH, pb[0][:, 192:256], co_col, None,
                                            op0=ALU.mult)
                    nc.vector.drain()
                    nc.vector.tensor_tensor(ZC, pb[0][:, 128:192], XH, op=ALU.subtract)
                    nc.vector.drain()
                    nc.vector.tensor_reduce(CL[:, 6:7], ZC, axis=AXL.X, op=ALU.add)
                    nc.vector.drain()
                    nc.vector.tensor_scalar_mul(CL[:, 6:7], CL[:, 6:7], 1.0 / 64.0)
                    nc.vector.drain()
                    nc.vector.tensor_scalar(ZC, ZC, CL[:, 6:7], None, op0=ALU.subtract)
                    g.inc(nc.vector.drain(), ev)
                    nc.scalar.wait_ge(ev.h, ev.n)
                    g.inc(nc.scalar.activation(XH, ZC, AF.Square, accum_out=CL[:, 11:12]),
                          sq)
                    nc.scalar.drain()
                    g.inc(nc.scalar.activation(CL[:, 7:8], CL[:, 11:12], AF.Sqrt,
                                               bias=CL[:, 5:6], scale=1.0 / 64.0), sq)
                    nc.vector.wait_ge(sq.h, sq.n)
                    nc.vector.reciprocal(CL[:, 7:8], CL[:, 7:8])
                    nc.vector.drain()
                    nc.vector.tensor_scalar(XH, ZC, CL[:, 7:8], None, op0=ALU.mult)
                    nc.vector.drain()
                    nc.vector.tensor_tensor(OUTT, XH, LNW, op=ALU.mult)
                    nc.vector.drain()
                    nc.vector.tensor_tensor(OUTT, OUTT, XCB, op=ALU.add)
                    nc.vector.drain()
                    for c in range(4):
                        nc.vector.tensor_tensor(
                            DGB[:, c * 128:(c + 1) * 128],
                            AP(OUTT.tensor, OUTT.offset,
                               [list(OUTT.ap[0]), [0, 2], [1, 64]]),
                            CN[:, (4 + c) * 128:(5 + c) * 128], op=ALU.mult)
                    g.inc(nc.vector.drain(), ev)
                    nc.tensor.wait_ge(ev.h, ev.n)
                    for c in range(4):
                        MM(pb[1][:, 192 + 16 * c:208 + 16 * c],
                                         DGB[:, c * 128:(c + 1) * 128], TSEL,
                                         start=True, stop=True, skip_group_check=True)
                    g.inc(g.lastT, mm)
                    nc.vector.wait_ge(mm.h, mm.n)
                    for c in range(4):
                        nc.vector.tensor_copy(
                            XCW[:, c * 1024 + nn * 16: c * 1024 + nn * 16 + 16],
                            pb[1][:, 192 + 16 * c:208 + 16 * c])
                    nc.vector.tensor_scalar(GRDC, GRD, cl_col, None, op0=ALU.mult)
                    for c in range(4):
                        nc.vector.tensor_tensor(
                            DGB[:, c * 128:(c + 1) * 128],
                            AP(XBT.tensor, XBT.offset,
                               [list(XBT.ap[0]), [0, 2], [1, 64]]),
                            CN[:, (4 + c) * 128:(5 + c) * 128], op=ALU.mult)
                    g.inc(nc.vector.drain(), ev)
                    nc.tensor.wait_ge(ev.h, ev.n)
                    for c in range(4):
                        MM(pb[2][:, c * 64:(c + 1) * 64],
                                         DGB[:, c * 128:(c + 1) * 128], GRDC,
                                         start=True, stop=True, skip_group_check=True)
                    MM(pb[3][0:8, 72:136], HSELT, GRDC,
                                     start=True, stop=True, skip_group_check=True)
                    g.inc(g.lastT, mm)
                    nc.vector.wait_ge(mm.h, mm.n)
                    for c in range(4):
                        nc.vector.tensor_tensor(W1ST[:, c * 64:(c + 1) * 64],
                                                W1ST[:, c * 64:(c + 1) * 64],
                                                pb[2][:, c * 64:(c + 1) * 64],
                                                op=ALU.subtract)
                    nc.vector.tensor_tensor(B1ST, B1ST, pb[3][0:8, 72:136], op=ALU.subtract)
                    g.inc(nc.vector.drain(), ev)
                    nc.tensor.wait_ge(ev.h, ev.n)
                    if DEBUG and n == 8:
                        g.bar()
                        dsb = g.sem("a1sbd")
                        g.inc(nc.sync.dma_start(out=sbdump[:, 0:16 * 64], in_=SB[:]), dsb, 16)
                        g.inc(nc.sync.dma_start(out=sbdump[:, 16 * 64:], in_=CL[:]), dsb, 16)
                        g.inc(nc.sync.dma_start(out=cndump[:, :],
                                                in_=CN[:, 16 * 128:20 * 128]), dsb, 16)
                        nc.sync.wait_ge(dsb.h, dsb.n)
                        g.bar()
                nc.sync.wait_ge(ev.h, ev.n)
                g.inc(nc.sync.dma_start(
                    out=AP(cc_s_in.tensor, grp * 1024,
                           [[S, 128], [128 * S, 4], [1, 1024]]),
                    in_=XCW[:].rearrange("p (c t) -> p c t", c=4)), st, 16)
                nc.sync.wait_ge(st.h, st.n)
            g.bar()

        if DEBUG:
            dsem = g.sem("a1dbg")
            g.inc(nc.sync.dma_start(out=scandump[:, :], in_=cc_s_in[:, :]), dsem, 16)
            nc.sync.wait_ge(dsem.h, dsem.n)
            g.bar()
        ccs2 = g.sem("a1cc2")
        g.inc(nc.gpsimd.collective_compute(
            "AllGather", ALU.bypass, replica_groups=RG,
            ins=[cc_s_in[:, :]], outs=[cc_s_out[:, :]]), ccs2)
        with nc.sbuf_tensor("OST_5", [128, KC * TOK], F32) as OST, \
             nc.sbuf_tensor("QSL_5", [128, 4], F32) as QSL, \
             nc.sbuf_tensor("OC4_5", [128, 4 * 1024], F32) as OC4:
            ld4 = g.sem("a1ld4")
            ev4 = g.sem("a1ev4")
            g.inc(nc.sync.dma_start(out=QSL[:, :], in_=i["qselcol"][:, :]), ld4, 16)
            nc.sync.wait_ge(ccs2.h, ccs2.n)
            for c in range(KC):
                if c >= 1:
                    nc.sync.wait_ge(ev4.h, ev4.n)
                for k in range(4):
                    g.inc(nc.sync.dma_start(
                        out=OC4[:, k * 1024:(k + 1) * 1024],
                        in_=cc_s_out[c * 128:(c + 1) * 128, k * 1024:(k + 1) * 1024]),
                        ld4, 16)
                nc.vector.wait_ge(ld4.h, ld4.n)
                nc.vector.tensor_scalar(OST[:, c * TOK:(c + 1) * TOK],
                                        OC4[:, 0:1024], QSL[:, 0:1], None, op0=ALU.mult)
                nc.vector.drain()
                for k in range(1, 4):
                    nc.vector.scalar_tensor_tensor(OST[:, c * TOK:(c + 1) * TOK],
                                                   OC4[:, k * 1024:(k + 1) * 1024],
                                                   QSL[:, k:k + 1],
                                                   OST[:, c * TOK:(c + 1) * TOK],
                                                   op0=ALU.mult, op1=ALU.add)
                    nc.vector.drain()
                g.inc(nc.vector.drain(), ev4)
            g.bar()

            def evac_res5(ob, tb, ps, mm_, ev_, j):
                nc.vector.wait_ge(mm_.h, mm_.n)
                sl = slice(ob * TOK + tb * 512, ob * TOK + tb * 512 + 512)
                nc.vector.tensor_tensor(X[:, sl], X[:, sl], ps, op=ALU.add)
                g.inc(nc.vector.drain(), ev_)

            proj(i["a1_wo"], 16,
                 lambda c, tb: OST[:, c * TOK + tb * 512: c * TOK + tb * 512 + 512],
                 2, evac_res5)


    def s_l1_ffn(first_call):
        FF = 8192
        FOB = FF // 128
        if first_call:
            for (wsrc, wdst, nel, nblk) in ((i["f1_w1t"], w1q, 2048 * FF, 64),
                                            (i["f1_w2t"], w2q, FF * 2048, 64)):
                ld = g.sem("qld")
                ev = g.sem("qev")
                st = g.sem("qst")
                mmq = g.sem("qmm")
                blkcols = wsrc.shape[1] // nblk
                with nc.sbuf_tensor(f"QW{g.uid()}", [128, 2 * blkcols], F32) as QW, \
                     nc.sbuf_tensor(f"QA{g.uid()}", [128, 4], F32) as QA, \
                     nc.sbuf_tensor(f"QT{g.uid()}", [128, blkcols], F32) as QT, \
                     nc.sbuf_tensor(f"QB{g.uid()}", [128, 2 * blkcols], BF16) as QB:
                    nc.vector.memset(QA[:, 0:1], 0.0)
                    g.inc(nc.vector.drain(), ev)
                    for bk in range(nblk):
                        buf = QW[:, (bk % 2) * blkcols:(bk % 2) * blkcols + blkcols]
                        if bk >= 2:
                            nc.sync.wait_ge(ev.h, ev.off + bk)
                        g.inc(nc.sync.dma_start(
                            out=buf, in_=wsrc[:, bk * blkcols:(bk + 1) * blkcols]), ld, 16)
                        nc.vector.wait_ge(ld.h, ld.off + 16 * (bk + 1))
                        nc.vector.tensor_reduce(QA[:, 1:2], buf, axis=AXL.X, op=ALU.add,
                                                apply_absolute_value=True)
                        nc.vector.drain()
                        nc.vector.tensor_tensor(QA[:, 0:1], QA[:, 0:1], QA[:, 1:2],
                                                op=ALU.add)
                        g.inc(nc.vector.drain(), ev)
                    nc.tensor.wait_ge(ev.h, ev.n)
                    g.inc(MM(pst[0][0:1, 0:1], ones, QA[:, 0:1],
                                           start=True, stop=True), mmq)
                    nc.vector.wait_ge(mmq.h, mmq.n)
                    nc.vector.tensor_scalar_mul(SC[0:1, 16:17], pst[0][0:1, 0:1],
                                                0.7 / float(nel))
                    nc.vector.drain()
                    nc.vector.tensor_scalar_mul(SC[0:1, 17:18], SC[0:1, 16:17], -1.0)
                    nc.vector.drain()
                    scalar_bcast(SC[0:1, 16:17], QA[:, 2:3], 128)
                    scalar_bcast(SC[0:1, 17:18], QA[:, 3:4], 128)
                    g.bar()
                    for bk in range(nblk):
                        buf = QW[:, (bk % 2) * blkcols:(bk % 2) * blkcols + blkcols]
                        obuf = QB[:, (bk % 2) * blkcols:(bk % 2) * blkcols + blkcols]
                        if bk >= 2:
                            nc.sync.wait_ge(ev.h, ev.n - 1)
                        g.inc(nc.sync.dma_start(
                            out=buf, in_=wsrc[:, bk * blkcols:(bk + 1) * blkcols]), ld, 16)
                        nc.vector.wait_ge(ld.h, ld.off + 16 * (nblk + bk + 1))
                        if bk >= 2:
                            nc.vector.wait_ge(st.h, st.n - 16)
                        nc.vector.tensor_scalar(QT[:, :], buf, QA[:, 2:3], None,
                                                op0=ALU.is_ge)
                        nc.vector.drain()
                        nc.vector.tensor_scalar(buf, buf, QA[:, 3:4], None, op0=ALU.is_le)
                        nc.vector.drain()
                        nc.vector.tensor_tensor(obuf, QT[:, :], buf, op=ALU.subtract)
                        g.inc(nc.vector.drain(), ev)
                        nc.sync.wait_ge(ev.h, ev.n)
                        g.inc(nc.sync.dma_start(
                            out=wdst[:, bk * blkcols:(bk + 1) * blkcols], in_=obuf), st, 16)
                    nc.sync.wait_ge(st.h, st.n)
                g.bar()

        with nc.sbuf_tensor(f"LSTf{g.uid()}", [1, 2 * TOK], F32) as LST, \
             nc.sbuf_tensor(f"B12{g.uid()}", [128, 160], F32) as BB, \
             nc.sbuf_tensor(f"XBH{g.uid()}", [128, KC * 512], BF16) as XBH, \
             nc.sbuf_tensor(f"XBL{g.uid()}", [128, KC * 512], BF16) as XBL, \
             nc.sbuf_tensor(f"HQH{g.uid()}", [128, 16 * 512], BF16) as HQH, \
             nc.sbuf_tensor(f"HQL{g.uid()}", [128, 16 * 512], BF16) as HQL, \
             nc.sbuf_tensor(f"YAC{g.uid()}", [128, KC * 512], F32) as YAC, \
             nc.sbuf_tensor(f"GEL{g.uid()}", [128, 3 * 512], F32) as GEL, \
             nc.sbuf_tensor(f"WBF{g.uid()}", [128, 2 * KC * 128], BF16) as WB:
            ldc = g.sem("fldc")
            g.inc(nc.sync.dma_start(out=BB[:, 0:64], in_=i["f1_b1c"][:, :]), ldc, 16)
            g.inc(nc.sync.dma_start(out=BB[:, 64:80], in_=i["f1_b2c"][:, :]), ldc, 16)
            ln_stats(lambda c, tb: X[:, c * TOK + tb * 512: c * TOK + tb * 512 + 512],
                     TOK, LST)
            nc.vector.wait_ge(ldc.h, ldc.n)
            nc.vector.tensor_scalar_mul(BB[:, 80:144], BB[:, 0:64], 0.7071067811865476)
            nc.vector.drain()
            g.bar()
            mm = g.sem("fm6")
            ev = g.sem("fe6")
            sq = g.sem("fs6")
            wl = g.sem("fw6")
            for tb in range(2):
                lam = g.sem("flam")
                mcol = LST[0:1, tb * 512:tb * 512 + 512]
                rcol = LST[0:1, TOK + tb * 512:TOK + tb * 512 + 512]
                nc.tensor.wait_ge(ev.h, ev.n)
                MM(pb[4][:, :], ONESR[0:1, :], mcol, start=True, stop=True)
                g.inc(MM(pb[5][:, :], ONESR[0:1, :], rcol,
                                       start=True, stop=True), lam)
                nc.vector.wait_ge(lam.h, lam.n)
                for c in range(KC):
                    ssl = slice(c * TOK + tb * 512, c * TOK + tb * 512 + 512)
                    dsl = slice(c * 512, c * 512 + 512)
                    xnt = GEL[:, 0:512]
                    nc.vector.tensor_tensor(xnt, X[:, ssl], pb[4][:, :], op=ALU.subtract)
                    nc.vector.drain()
                    nc.vector.tensor_tensor(xnt, xnt, pb[5][:, :], op=ALU.mult)
                    nc.vector.drain()
                    nc.vector.tensor_copy(XBH[:, dsl], xnt)
                    nc.vector.drain()
                    nc.vector.tensor_tensor(XBL[:, dsl], xnt, XBH[:, dsl],
                                            op=ALU.subtract)
                    nc.vector.drain()
                nc.vector.memset(YAC[:, :], 0.0)
                g.inc(nc.vector.drain(), lam)
                nc.tensor.wait_ge(lam.h, lam.n)
                g.bar()
                for fq in range(4):
                    j = 0
                    for fb in range(16):
                        gfb = fq * 16 + fb
                        buf = WB[:, (fb % 2) * KC * 128:(fb % 2) * KC * 128 + KC * 128]
                        if fb >= 2:
                            nc.sync.wait_ge(mm.h, mm.n - 1)
                        g.inc(nc.sync.dma_start(
                            out=buf, in_=w1q[:, gfb * KC * 128:(gfb + 1) * KC * 128]),
                            wl, 16)
                        nc.tensor.wait_ge(wl.h, wl.n)
                        psb = pb[j % 2]
                        if j >= 2:
                            nc.tensor.wait_ge(sq.h, sq.n - 2)
                        for c in range(KC):
                            wslc = buf[:, c * 128:(c + 1) * 128]
                            MM(psb[:, :], wslc,
                                             XBH[:, c * 512:(c + 1) * 512],
                                             start=(c == 0), stop=False,
                                             skip_group_check=True)
                            MM(psb[:, :], wslc,
                                             XBL[:, c * 512:(c + 1) * 512],
                                             start=False, stop=(c == KC - 1),
                                             skip_group_check=True)
                        g.inc(g.lastT, mm)
                        nc.scalar.wait_ge(mm.h, mm.n)
                        g.inc(nc.scalar.activation(GEL[:, 0:512], psb[:, :], AF.Erf,
                                                   bias=BB[:, 80 + gfb:81 + gfb],
                                                   scale=0.7071067811865476), sq)
                        g.inc(nc.scalar.activation(GEL[:, 512:1024], psb[:, :], AF.Identity,
                                                   bias=BB[:, gfb:gfb + 1]), sq)
                        nc.vector.wait_ge(sq.h, sq.n)
                        nc.vector.tensor_tensor(GEL[:, 1024:1536],
                                                GEL[:, 0:512], GEL[:, 512:1024],
                                                op=ALU.mult)
                        nc.vector.drain()
                        nc.vector.tensor_tensor(GEL[:, 1024:1536],
                                                GEL[:, 1024:1536],
                                                GEL[:, 512:1024], op=ALU.add)
                        nc.vector.drain()
                        nc.vector.tensor_copy(HQH[:, fb * 512:(fb + 1) * 512],
                                              GEL[:, 1024:1536])
                        nc.vector.drain()
                        nc.vector.tensor_tensor(HQL[:, fb * 512:(fb + 1) * 512],
                                                GEL[:, 1024:1536],
                                                HQH[:, fb * 512:(fb + 1) * 512],
                                                op=ALU.subtract)
                        g.inc(nc.vector.drain(), ev)
                        j += 1
                    g.bar()
                    j = 0
                    for ob in range(16):
                        buf = WB[:, (ob % 2) * KC * 128:(ob % 2) * KC * 128 + KC * 128]
                        if ob >= 2:
                            nc.sync.wait_ge(mm.h, mm.n - 1)
                        g.inc(nc.sync.dma_start(
                            out=buf,
                            in_=w2q[:, (ob * 64 + fq * 16) * 128:
                                    (ob * 64 + fq * 16 + 16) * 128]), wl, 16)
                        nc.tensor.wait_ge(wl.h, wl.n)
                        psb = pb[2 + j % 2]
                        if j >= 2:
                            nc.tensor.wait_ge(ev.h, ev.n - 1)
                        for c in range(16):
                            wslc = buf[:, c * 128:(c + 1) * 128]
                            MM(psb[:, :], wslc,
                                             HQH[:, c * 512:(c + 1) * 512],
                                             start=(c == 0), stop=False,
                                             skip_group_check=True)
                            MM(psb[:, :], wslc,
                                             HQL[:, c * 512:(c + 1) * 512],
                                             start=False, stop=(c == 15),
                                             skip_group_check=True)
                        g.inc(g.lastT, mm)
                        nc.vector.wait_ge(mm.h, mm.n)
                        nc.vector.tensor_tensor(YAC[:, ob * 512:(ob + 1) * 512],
                                                YAC[:, ob * 512:(ob + 1) * 512],
                                                psb[:, :], op=ALU.add)
                        g.inc(nc.vector.drain(), ev)
                        j += 1
                    g.bar()
                for c in range(KC):
                    sl = slice(c * TOK + tb * 512, c * TOK + tb * 512 + 512)
                    nc.vector.scalar_tensor_tensor(GEL[:, 0:512], YAC[:, c * 512:(c + 1) * 512],
                                                   0.5, X[:, sl], op0=ALU.mult, op1=ALU.add)
                    nc.vector.drain()
                    nc.vector.tensor_scalar(X[:, sl], GEL[:, 0:512], BB[:, 64 + c:65 + c],
                                            None, op0=ALU.add)
                    nc.vector.drain()
                g.bar()

    def s7_l1_attn_b():
        ccs = g.sem("b7cc")
        with nc.sbuf_tensor("XN_7", [128, KC * TOK], F32) as XN7, \
             nc.sbuf_tensor("LST_7", [1, 2 * TOK], F32) as LST:
            ln_stats(lambda c, tb: X[:, c * TOK + tb * 512: c * TOK + tb * 512 + 512],
                     TOK, LST)
            ln_apply(lambda c, tb: X[:, c * TOK + tb * 512: c * TOK + tb * 512 + 512],
                     lambda c, tb: XN7[:, c * TOK + tb * 512: c * TOK + tb * 512 + 512],
                     TOK, LST)
            st = g.sem("b7st")
            g.inc(nc.sync.dma_start(
                out=AP(cc_xn_in.tensor, 0, [[TOK, 128], [128 * TOK, KC], [1, TOK]]),
                in_=XN7[:].rearrange("p (c t) -> p c t", c=KC)), st, 16)
            nc.gpsimd.wait_ge(st.h, st.n)
            g.inc(nc.gpsimd.collective_compute(
                "AllGather", ALU.bypass, replica_groups=RG,
                ins=[cc_xn_in[:, :]], outs=[cc_xn_out[:, :]]), ccs)
        g.bar()

        with nc.sbuf_tensor("QH_7", [128, S], F32) as QH, \
             nc.sbuf_tensor("KH_7", [128, S], F32) as KH, \
             nc.sbuf_tensor("VH_7", [128, 32 * 130], F32) as VH, \
             nc.sbuf_tensor("WQKV7", [128, 3 * KC * 128], F32) as WQKV, \
             nc.sbuf_tensor("XG7", [128, KC * 512], F32) as XG:
            PT = EV
            for kt in range(32):
                nc.vector.memset(VH[:, kt * 130 + 64:kt * 130 + 65], 1.0)
                nc.vector.memset(VH[:, kt * 130 + 129:kt * 130 + 130], 1.0)
            nc.vector.drain()
            hl = g.sem("b7hl")
            mm = g.sem("b7mm")
            mo = g.sem("b7mo")
            sq = g.sem("b7sq")
            ev = g.sem("b7ev")
            st = g.sem("b7st2")
            xl = g.sem("b7xl")
            nc.sync.wait_ge(ccs.h, ccs.n)
            g.bar()
            for h in range(4):
                for wi, nm in ((0, "b1_wq"), (1, "b1_wk"), (2, "b1_wv")):
                    g.inc(nc.sync.dma_start(
                        out=WQKV[:, wi * KC * 128:(wi + 1) * KC * 128],
                        in_=i[nm][:, h * KC * 128:(h + 1) * KC * 128]), hl, 16)
                nc.tensor.wait_ge(hl.h, hl.n)
                for gt in range(8):
                    if True:
                        g.inc(nc.sync.dma_start(
                            out=XG[:].rearrange("p (c t) -> p c t", c=KC),
                            in_=AP(cc_xn_out.tensor, (gt // 2) * D * TOK + (gt % 2) * 512,
                                   [[TOK, 128], [128 * TOK, KC], [1, 512]])), xl, 16)
                        nc.tensor.wait_ge(xl.h, xl.n)
                    for wi, dstb in ((0, QH), (1, KH)):
                        psb = pb[wi]
                        for c in range(KC):
                            MM(
                                psb[:, :],
                                WQKV[:, wi * KC * 128 + c * 128:wi * KC * 128 + c * 128 + 128],
                                XG[:, c * 512:(c + 1) * 512],
                                start=(c == 0), stop=(c == KC - 1))
                        g.inc(g.lastT, mm)
                        nc.vector.wait_ge(mm.h, mm.n)
                        if wi == 0:
                            nc.vector.tensor_scalar_mul(dstb[:, gt * 512:(gt + 1) * 512],
                                                        psb[:, :], 0.08838834764831845)
                        else:
                            nc.vector.tensor_copy(dstb[:, gt * 512:(gt + 1) * 512], psb[:, :])
                        g.inc(nc.vector.drain(), ev)
                    for tt in range(4):
                        gtt = gt * 4 + tt
                        psb = pb[2 + tt % 2]
                        for c in range(KC):
                            MM(
                                psb[:, 0:128],
                                XG[:, c * 512 + tt * 128: c * 512 + tt * 128 + 128],
                                WQKV[:, 2 * KC * 128 + c * 128:2 * KC * 128 + c * 128 + 128],
                                start=(c == 0), stop=(c == KC - 1))
                        g.inc(g.lastT, mm)
                        nc.vector.wait_ge(mm.h, mm.n)
                        nc.vector.tensor_copy(VH[:, gtt * 130:gtt * 130 + 64], psb[:, 0:64])
                        nc.vector.drain()
                        nc.vector.tensor_copy(VH[:, gtt * 130 + 65:gtt * 130 + 129],
                                              psb[:, 64:128])
                        g.inc(nc.vector.drain(), ev)
                    nc.tensor.wait_ge(ev.h, ev.n)
                g.bar()
                kn = g.sem(f"b7kn{h}")
                nc.vector.memset(SC[0:1, 10:11], 0.0)
                nc.vector.memset(SC[0:1, 13:14], 0.0)
                g.inc(nc.vector.drain(), kn)
                for src_, slot in ((KH, 10), (QH, 13)):
                    for tbb in range(8):
                        buf = SQ[:, (tbb % 2) * 512:(tbb % 2) * 512 + 512]
                        if tbb >= 2 or slot == 13:
                            nc.scalar.wait_ge(kn.h, kn.n - 1)
                        g.inc(nc.scalar.activation(buf, src_[:, tbb * 512:(tbb + 1) * 512],
                                                   AF.Square), sq)
                        nc.tensor.wait_ge(sq.h, sq.n)
                        nc.tensor.wait_ge(kn.h, kn.n)
                        g.inc(MM(pst[0][0:1, :], ones, buf,
                                               start=True, stop=True), mm)
                        nc.vector.wait_ge(mm.h, mm.n)
                        nc.vector.tensor_reduce(SC[0:1, 11:12], pst[0][0:1, :], axis=AXL.X,
                                                op=ALU.max)
                        nc.vector.drain()
                        nc.vector.tensor_tensor(SC[0:1, slot:slot + 1],
                                                SC[0:1, slot:slot + 1], SC[0:1, 11:12],
                                                op=ALU.max)
                        g.inc(nc.vector.drain(), kn)
                nc.vector.tensor_tensor(SC[0:1, 12:13], SC[0:1, 10:11], SC[0:1, 13:14],
                                        op=ALU.mult)
                g.inc(nc.vector.drain(), ev)
                nc.scalar.wait_ge(ev.h, ev.n)
                g.inc(nc.scalar.activation(SC[0:1, 12:13], SC[0:1, 12:13], AF.Sqrt), sq)
                nc.vector.wait_ge(sq.h, sq.n)
                nc.vector.tensor_scalar_mul(SC[0:1, 12:13], SC[0:1, 12:13], -1.0)
                nc.vector.drain()
                scalar_bcast(SC[0:1, 12:13], CREP[:, 0:1], 128)
                g.bar()
                for qb in range(8):
                    for kt in range(32):
                        psb = pb[kt % 2]
                        if kt >= 2:
                            nc.tensor.wait_ge(sq.h, sq.n - 1)
                        MM(psb[:, :], KH[:, kt * 128:(kt + 1) * 128],
                                         QH[:, qb * 512:(qb + 1) * 512],
                                         start=True, stop=True)
                        g.inc(g.lastT, mm)
                        ptile = PT[:, (kt % 2) * 512:(kt % 2) * 512 + 512]
                        nc.scalar.wait_ge(mm.h, mm.n)
                        if kt >= 2:
                            nc.scalar.wait_ge(mo.h, mo.n - 2)
                        g.inc(nc.scalar.activation(ptile, psb[:, :], AF.Exp,
                                                   bias=CREP[:, 0:1]), sq)
                        nc.tensor.wait_ge(sq.h, sq.n)
                        g.inc(MM(pb[3][0:65, :],
                                               VH[:, kt * 130:kt * 130 + 65],
                                               ptile, start=(kt == 0), stop=(kt == 31),
                                               skip_group_check=True), mo)
                        g.inc(MM(pb[4][0:65, :],
                                               VH[:, kt * 130 + 65:kt * 130 + 130],
                                               ptile, start=(kt == 0), stop=(kt == 31),
                                               skip_group_check=True), mo)
                    nc.vector.wait_ge(mo.h, mo.n)
                    nc.vector.reciprocal(RECIP[0:1, :], pb[3][64:65, :])
                    g.inc(nc.vector.drain(), ev)
                    nc.tensor.wait_ge(ev.h, ev.n)
                    g.inc(MM(pb[5][0:64, :], ONESR[0:1, 0:64],
                                           RECIP[0:1, :], start=True, stop=True), mm)
                    nc.vector.wait_ge(mm.h, mm.n)
                    nc.vector.tensor_copy(SQ[64:128, 0:512], pb[5][0:64, :])
                    nc.vector.drain()
                    obuf = EV[:, 0:512]
                    nc.vector.tensor_tensor(obuf[0:64, :], pb[3][0:64, :],
                                            SQ[64:128, 0:512], op=ALU.mult)
                    nc.vector.drain()
                    nc.vector.tensor_tensor(obuf[64:128, :], pb[4][0:64, :],
                                            SQ[64:128, 0:512], op=ALU.mult)
                    g.inc(nc.vector.drain(), ev)
                    nc.tensor.wait_ge(ev.h, ev.n)
                    nc.sync.wait_ge(ev.h, ev.n)
                    g.inc(nc.sync.dma_start(
                        out=cc_s_in[h * 128:(h + 1) * 128, qb * 512:(qb + 1) * 512],
                        in_=obuf), st, 16)
                    nc.scalar.wait_ge(st.h, st.n)
                g.bar()

        ccs2 = g.sem("b7cc2")
        g.inc(nc.gpsimd.collective_compute(
            "AllGather", ALU.bypass, replica_groups=RG,
            ins=[cc_s_in[:, :]], outs=[cc_s_out[:, :]]), ccs2)
        with nc.sbuf_tensor("OST_7", [128, KC * TOK], F32) as OST, \
             nc.sbuf_tensor("QSL_7", [128, 4], F32) as QSL, \
             nc.sbuf_tensor("OC4_7", [128, 4 * 1024], F32) as OC4:
            ld4 = g.sem("b7ld4")
            ev4 = g.sem("b7ev4")
            g.inc(nc.sync.dma_start(out=QSL[:, :], in_=i["qselcol"][:, :]), ld4, 16)
            nc.sync.wait_ge(ccs2.h, ccs2.n)
            for c in range(KC):
                if c >= 1:
                    nc.sync.wait_ge(ev4.h, ev4.n)
                for k in range(4):
                    g.inc(nc.sync.dma_start(
                        out=OC4[:, k * 1024:(k + 1) * 1024],
                        in_=cc_s_out[c * 128:(c + 1) * 128, k * 1024:(k + 1) * 1024]),
                        ld4, 16)
                nc.vector.wait_ge(ld4.h, ld4.n)
                nc.vector.tensor_scalar(OST[:, c * TOK:(c + 1) * TOK],
                                        OC4[:, 0:1024], QSL[:, 0:1], None, op0=ALU.mult)
                nc.vector.drain()
                for k in range(1, 4):
                    nc.vector.scalar_tensor_tensor(OST[:, c * TOK:(c + 1) * TOK],
                                                   OC4[:, k * 1024:(k + 1) * 1024],
                                                   QSL[:, k:k + 1],
                                                   OST[:, c * TOK:(c + 1) * TOK],
                                                   op0=ALU.mult, op1=ALU.add)
                    nc.vector.drain()
                g.inc(nc.vector.drain(), ev4)
            g.bar()

            def evac_res7(ob, tb, ps, mm_, ev_, j):
                nc.vector.wait_ge(mm_.h, mm_.n)
                sl = slice(ob * TOK + tb * 512, ob * TOK + tb * 512 + 512)
                nc.vector.tensor_tensor(X[:, sl], X[:, sl], ps, op=ALU.add)
                g.inc(nc.vector.drain(), ev_)

            proj(i["b1_wo"], 16,
                 lambda c, tb: OST[:, c * TOK + tb * 512: c * TOK + tb * 512 + 512],
                 2, evac_res7)

    if n_sub >= 1:
        s1_l0_attn_a()
    if n_sub >= 2:
        s_l0_ffn(True)
    if n_sub >= 3:
        s3_l0_attn_b()
    if n_sub >= 4:
        s_l0_ffn(False)
    if n_sub >= 5:
        s5_l1_attn_a()
    if n_sub >= 6:
        s_l1_ffn(True)
    if n_sub >= 7:
        s7_l1_attn_b()
    if n_sub >= 8:
        s_l1_ffn(False)
    if n_sub < 1:
        ld0 = g.sem("ld0")
        g.inc(nc.sync.dma_start(out=X[:], in_=xin[:, :]), ld0, 16)
        nc.sync.wait_ge(ld0.h, ld0.n)
        g.bar()

    fin = g.sem("fin")
    g.inc(nc.sync.dma_start(out=y[:, :], in_=X[:]), fin, 16)
    nc.sync.wait_ge(fin.h, fin.n)
    return nc



def kernel(**inputs):
    ins = {k: np.ascontiguousarray(np.asarray(v, np.float32)) for k, v in inputs.items()}
    x = ins["L0_attn_a_x"]

    dogp = np.zeros((32, 8), np.float32)
    dogp[:, 0:7] = ins["L0_attn_a_dog_params"]
    shared = {
        "l0a_wq": wprep(ins["L0_attn_a_Wq"]),
        "l0a_wk": wprep(ins["L0_attn_a_Wk"]),
        "l0a_wv": wprep(ins["L0_attn_a_Wv"]),
        "l0a_wo": wprep(ins["L0_attn_a_Wo"]),
        "l0a_dog": dogp,
        "l0a_sinks": ins["L0_attn_a_sinks"].reshape(1, 32),
    }
    if N_SUB >= 2:
        shared["l0f_wT"] = wprep(np.ascontiguousarray(ins["L0_ffn_weight"].T))
        shared["l0f_cols"] = np.concatenate(
            [colprep(ins["L0_ffn_gamma"]), colprep(ins["L0_ffn_beta"]),
             colprep(ins["L0_ffn_ln_weight"]), colprep(ins["L0_ffn_ln_bias"])], axis=1)
    if N_SUB >= 3:
        shared["l0b_wqd"] = wprep(ins["L0_attn_b_wq_d"])
        shared["l0b_wkvd"] = wprep(ins["L0_attn_b_wkv_d"], obs=256)
        shared["l0b_wqu"] = wprep(ins["L0_attn_b_wq_u"])
        shared["l0b_wku"] = wprep(ins["L0_attn_b_wk_u"])
        shared["l0b_wvu"] = wprep(ins["L0_attn_b_wv_u"])
        shared["l0b_wo"] = wprep(ins["L0_attn_b_wo"])
    if N_SUB >= 5:
        p_ = np.arange(128)
        shared["idxcol"] = (1.0 / ((p_ % 16) + 1) / 64.0).reshape(128, 1).astype(np.float32)
        shared["pcol"] = p_.reshape(128, 1).astype(np.float32)
        shared["phcol"] = (p_ // 16).reshape(128, 1).astype(np.float32)
        shared["pm16col"] = (p_ % 16).reshape(128, 1).astype(np.float32)
        shared["a1_wo"] = wprep(ins["L1_attn_a_Wo"])
    if N_SUB >= 6:
        shared["f1_w1t"] = wprep(np.ascontiguousarray(ins["L1_ffn_w1"].T))
        shared["f1_w2t"] = wprep(np.ascontiguousarray(ins["L1_ffn_w2"].T))
        shared["f1_b1c"] = colprep(ins["L1_ffn_b1"])
        shared["f1_b2c"] = colprep(ins["L1_ffn_b2"])
    if N_SUB >= 7:
        shared["b1_wo"] = wprep(ins["L1_attn_b_wo"])
    if N_SUB < 1:
        shared = {}

    in_maps = []
    for c in range(NCORES):
        b, q = divmod(c, 4)
        t0 = q * TOK
        m = dict(shared)
        m["xin"] = fmprep(x[b, t0:t0 + TOK])
        if q == 0:
            m["xhalo"] = np.zeros((128, KC * HALO), np.float32)
            m["hk"] = np.zeros((1, 1), np.float32)
        else:
            m["xhalo"] = fmprep(x[b, t0 - HALO:t0])
            m["hk"] = np.ones((1, 1), np.float32)
        if N_SUB >= 3:
            m["tpos"] = np.full((1, 1), float(t0), np.float32)
        if N_SUB >= 5:
            hs = slice(q * 512, q * 512 + 512)
            m["a1_wq"] = wprep(np.ascontiguousarray(ins["L1_attn_a_Wq"][:, hs]))
            m["a1_wk"] = wprep(np.ascontiguousarray(ins["L1_attn_a_Wk"][:, hs]))
            m["a1_wv"] = wprep(np.ascontiguousarray(ins["L1_attn_a_Wv"][:, hs]))
            m["a1_wilr"] = wprep(np.ascontiguousarray(
                ins["L1_attn_a_W_ilr"][:, 8 * q:8 * q + 8]), obs=8)
            m["a1_bilr8"] = np.ascontiguousarray(
                ins["L1_attn_a_b_ilr"][8 * q:8 * q + 8].reshape(8, 1))
            m["a1_lnw"] = np.ascontiguousarray(ins["L1_attn_a_ln_weight"][8 * q:8 * q + 8])
            m["a1_lnb"] = np.ascontiguousarray(ins["L1_attn_a_ln_bias"][8 * q:8 * q + 8])
            m["a1_w1init"] = np.ascontiguousarray(
                ins["L1_attn_a_W1_init"][8 * q:8 * q + 8].reshape(4, 2, 64, 64)
                .transpose(1, 2, 0, 3).reshape(128, 256))
            m["a1_b1init"] = np.ascontiguousarray(
                ins["L1_attn_a_b1_init"][8 * q:8 * q + 8].reshape(8, 64))
            qsel = np.zeros((128, 4), np.float32)
            qsel[:, q] = 1.0
            m["qselcol"] = qsel
        if N_SUB >= 7:
            hs2 = slice(q * 512, q * 512 + 512)
            m["b1_wq"] = wprep(np.ascontiguousarray(ins["L1_attn_b_wq"][:, hs2]))
            m["b1_wk"] = wprep(np.ascontiguousarray(ins["L1_attn_b_wk"][:, hs2]))
            m["b1_wv"] = wprep(np.ascontiguousarray(ins["L1_attn_b_wv"][:, hs2]))
        in_maps.append(m)

    nc = build_program(N_SUB)
    res = run_bass_kernel_spmd(nc, in_maps, core_ids=list(range(NCORES)))
    if DEBUG:
        globals()["dbg"] = res.results
    out = np.empty((B, S, D), np.float32)
    for c in range(NCORES):
        b, q = divmod(c, 4)
        yc = res.results[c]["y"]
        out[b, q * TOK:(q + 1) * TOK] = yc.reshape(128, KC, TOK).transpose(1, 0, 2).reshape(D, TOK).T
    return out
